# version 20
# baseline (speedup 1.0000x reference)
import numpy as np
import concourse.bass as bass
import concourse.mybir as mybir
from concourse.bass_utils import run_bass_kernel_spmd

F32 = mybir.dt.float32
BF16 = mybir.dt.bfloat16
AF = mybir.ActivationFunctionType
ALU = mybir.AluOpType
AX = mybir.AxisListType

D = 1024
S = 2048
NSEQ = 2
T = NSEQ * S
H = 16
DH = 4096
NTB = S // 128
C_SHIFT = 10.0

ENGS = ("sync", "gpsimd", "tensor", "scalar", "vector")

C_NEGC, C_S8, C_EPS, C_INV1024, C_NEG1, C_NEGHALF = range(6)
CONST_VALS = [-C_SHIFT, 0.125, 1e-5, 1.0 / 1024.0, -1.0, -0.5]

(ST_SUM, ST_SQ0, ST_SQ1, ST_M, ST_M2, ST_V, ST_R, ST_NEGM, ST_MR, ST_AMR,
 ST_T0, ST_T1, ST_RZ, ST_T2, ST_T3) = range(15)


class Ctr:
    def __init__(self, h):
        self.h = h
        self.n = 0

    def bump(self, k=1):
        self.n += k
        return self.n


PLAN_LIMIT = None
DEBUG_DRAM = False
DEBUG_DUMP = None


class Plan:

    IDX = 0

    def __init__(self, nc, dmas=None):
        self.nc = nc
        self.dmas = dmas or {}
        self.ops = {e: [] for e in ENGS}

    def op(self, eng, fn, waits=(), inc=None):
        self.ops[eng].append((list(waits), fn, inc))

    def emit(self):
        Plan.IDX += 1
        if PLAN_LIMIT is not None and Plan.IDX > PLAN_LIMIT:
            return
        for eng, ctrs in self.dmas.items():
            if self.ops[eng]:
                self.op(eng, lambda e: e.nop(),
                        waits=[(c.h, c.n) for c in ctrs if c.n > 0])

        def mk(ops):
            def run(e):
                for waits, fn, inc in ops:
                    for sem, val in waits:
                        if val > 0:
                            e.wait_ge(sem, val)
                    inst = fn(e)
                    if inc is not None:
                        inst.then_inc(inc[0], inc[1])
            return run

        with self.nc.Block() as block:
            for eng in ENGS:
                if self.ops[eng]:
                    getattr(block, eng)(mk(self.ops[eng]))


class Mem:

    def __init__(self, nc):
        self.nc = nc
        self.guards = {}
        self.order = {}
        self.seq = 0

    def _reg(self, name, g):
        h = g.__enter__()
        self.guards[name] = g
        self.order[name] = self.seq
        self.seq += 1
        return h

    def sbuf(self, name, shape, dt, side=None):
        assert name not in self.guards
        return self._reg(name, self.nc.sbuf_tensor(f"{name}_u{self.seq}", shape, dt, side=side))

    def init_psum_pool(self):
        g = []
        for nm, shape, dt in [("psA", [128, 1024], F32), ("psB", [128, 1024], F32),
                              ("psC", [128, 65], F32), ("psD", [128, 65], F32),
                              ("psT0", [128, 512], BF16), ("psT1", [128, 512], BF16)]:
            cm = self.nc.psum_tensor(nm, shape, dt)
            g.append(cm.__enter__())
        pA, pB, pC, pD, pT0, pT1 = g
        self.pool = {
            "1024F": [pA[:], pB[:]],
            "512F": [pA[:, 0:512], pA[:, 512:1024], pB[:, 0:512], pB[:, 512:1024]],
            "65F": [pC[:], pD[:]],
            "128B": [pT0[:], pT1[:]],
            "512B": [pT0[:], pT1[:]],
        }
        self.pool_free = {k: list(range(len(v))) for k, v in self.pool.items()}
        self.pool_named = {}

    def psum(self, name, shape, dt=F32):
        key = {(1024, 0): "1024F", (512, 0): "512F", (65, 0): "65F",
               (128, 1): "128B", (512, 1): "512B"}[(shape[1], 1 if dt == BF16 else 0)]
        idx = self.pool_free[key].pop(0)
        self.pool_named[name] = (key, idx)
        return self.pool[key][idx]

    def free(self, *names):
        ps = [n for n in names if n in self.pool_named]
        sb = [n for n in names if n not in self.pool_named]
        for n in ps:
            key, idx = self.pool_named.pop(n)
            self.pool_free[key].append(idx)
            self.pool_free[key].sort()
        for n in sorted(sb, key=lambda n: -self.order[n]):
            self.guards.pop(n).__exit__(None, None, None)
            del self.order[n]


BASS_KW = {}


def build_program():
    Plan.IDX = 0
    nc = bass.Bass("TRN2", target_bir_lowering=False, **BASS_KW)

    din = {}
    for name, shape in [
        ("x", [T, D]), ("wqT", [D, D]), ("wkT", [D, D]), ("wvT", [D, D]),
        ("woT", [D, D]), ("w1T", [D, DH]), ("w2T", [D, DH]), ("h1T", [DH, DH]),
        ("h2T", [DH, DH]), ("w3T", [DH, D]), ("s1r", [128, 32]), ("s2r", [128, 32]),
        ("hs1r", [128, 32]), ("hs2r", [128, 32]), ("s3row", [1, D]),
        ("cosT", [128, S]), ("sinT", [128, S]), ("pswap", [128, 128]),
        ("identc", [128, 128]), ("maskc", [128, 128]),
    ]:
        din[name] = nc.dram_tensor(name, shape, F32, kind="ExternalInput").ap()

    out_d = nc.dram_tensor("out", [T, D], F32, kind="ExternalOutput").ap()
    scratch_kind = "ExternalOutput" if DEBUG_DRAM else "Internal"
    x1d = nc.dram_tensor("x1d", [T, D], F32, kind=scratch_kind).ap()
    xln1d = nc.dram_tensor("xln1d", [8, 128, T], BF16, kind=scratch_kind).ap()
    hd = nc.dram_tensor("hd", [8, 32, 128, 512], BF16, kind=scratch_kind).ap()
    g1d = nc.dram_tensor("g1d", [8, 32, 128, 512], BF16, kind=scratch_kind).ap()
    g2d = nc.dram_tensor("g2d", [32, 32, 128, 128], BF16, kind=scratch_kind).ap()

    dbg = (nc.dram_tensor("dbg", [128, 16640], BF16, kind="ExternalOutput").ap()
           if DEBUG_DUMP else None)

    xr = din["x"].rearrange("(tb p) d -> tb p d", p=128)
    x1r = x1d.rearrange("(tb p) d -> tb p d", p=128)
    outr = out_d.rearrange("(tb p) d -> tb p d", p=128)

    mem = Mem(nc)
    mem.init_psum_pool()
    sem_guards = []
    sems = {}
    for nm in ("dG", "dG2", "dG3", "dL0", "dL1", "dS0", "dS1", "mm", "mm2", "act", "vec", "vec2"):
        g = nc.semaphore(nm)
        sems[nm] = Ctr(g.__enter__())
        sem_guards.append(g)
    dG = sems["dG"]
    dG2 = sems["dG2"]
    dG3 = sems["dG3"]
    dL = [sems["dL0"], sems["dL1"]]
    dSt = [sems["dS0"], sems["dS1"]]
    smm, smm2, sact, svec, svec2 = (sems[k] for k in ("mm", "mm2", "act", "vec", "vec2"))
    DMAS = {"sync": [dL[0], dL[1]], "gpsimd": [dG, dG2, dG3, dSt[0], dSt[1]]}

    consts = mem.sbuf("consts", [128, 8], F32)
    identb = mem.sbuf("identb", [128, 128], BF16)
    pswapb = mem.sbuf("pswapb", [128, 128], BF16)
    maskb = mem.sbuf("maskb", [128, 128], BF16)
    cosf = mem.sbuf("cosf", [128, S], F32, side="right")
    sinf = mem.sbuf("sinf", [128, S], F32, side="right")
    cosb = mem.sbuf("cosb", [128, S], BF16, side="right")
    s1s = mem.sbuf("s1s", [128, 32], F32)
    s2s = mem.sbuf("s2s", [128, 32], F32)
    hs1s = mem.sbuf("hs1s", [128, 32], F32)
    hs2s = mem.sbuf("hs2s", [128, 32], F32)
    scrap = mem.sbuf("scrap", [128, 2048], BF16)
    stt = mem.sbuf("stt", [128, 48], F32)
    bst = mem.sbuf("bst", [128, 8 * 16], F32)

    def st(i, par=0):
        return stt[:, par * 16 + i:par * 16 + i + 1]

    (B_SUM, B_SQ, B_M, B_M2, B_V, B_R, B_MR, B_AMR) = range(8)

    def bs(i):
        return bst[:, i * 16:(i + 1) * 16]

    def bc(i, tb):
        return bst[:, i * 16 + tb:i * 16 + tb + 1]

    def ln_batch(p, pre_wait, with_amr=False):
        def step(eng, fn, extra=()):
            p.op(eng, fn, waits=[(svec2.h, svec2.n)] + list(extra), inc=(svec2.h, 1))
            svec2.bump()

        step("vector", lambda e: e.tensor_scalar(bs(B_M), bs(B_SUM), cc(C_INV1024),
                                                 None, ALU.mult), extra=list(pre_wait))
        step("vector", lambda e: e.tensor_tensor(bs(B_M2), bs(B_M), bs(B_M), ALU.mult))
        step("vector", lambda e: e.tensor_scalar(bs(B_V), bs(B_SQ), cc(C_INV1024),
                                                 None, ALU.mult))
        step("vector", lambda e: e.tensor_tensor(bs(B_V), bs(B_V), bs(B_M2), ALU.subtract))
        step("scalar", lambda e: e.activation(bs(B_V), bs(B_V), AF.Ln, bias=cc(C_EPS)))
        step("scalar", lambda e: e.activation(bs(B_R), bs(B_V), AF.Exp, scale=cc(C_NEGHALF)))
        if with_amr:
            step("vector", lambda e: e.tensor_tensor(bs(B_MR), bs(B_M), bs(B_R), ALU.mult))
            step("vector", lambda e: e.tensor_scalar(bs(B_AMR), bs(B_MR), cc(C_NEG1),
                                                     None, ALU.mult))
        return svec2.n

    def cc(i):
        return consts[:, i:i + 1]

    p = Plan(nc, DMAS)
    for i, v in enumerate(CONST_VALS):
        p.op("vector", lambda e, i=i, v=v: e.memset(consts[:, i:i + 1], v))
    for dst, src in [(identb, din["identc"]), (pswapb, din["pswap"]),
                     (maskb, din["maskc"])]:
        p.op("gpsimd", lambda e, d=dst, s=src: e.dma_start(out=d[:], in_=s),
             inc=(dG.h, 16))
        dG.bump(16)
    for dst, src in [(cosf, din["cosT"]), (sinf, din["sinT"]), (s1s, din["s1r"]),
                     (s2s, din["s2r"]), (hs1s, din["hs1r"]), (hs2s, din["hs2r"])]:
        p.op("sync", lambda e, d=dst, s=src: e.dma_start(out=d[:], in_=s),
             inc=(dL[0].h, 16))
        dL[0].bump(16)
    p.op("vector", lambda e: e.tensor_copy(cosb[:], cosf[:]),
         waits=[(dL[0].h, dL[0].n)])
    p.emit()

    def ln_tail(p, sq_wait, two_sq=False, par=0):
        def step(eng, fn, extra=()):
            p.op(eng, fn, waits=[(svec2.h, svec2.n)] + list(extra), inc=(svec2.h, 1))
            svec2.bump()

        if two_sq:
            step("vector", lambda e, par=par: e.tensor_tensor(
                st(ST_T0, par), st(ST_SQ0, par), st(ST_SQ1, par), ALU.add), extra=[sq_wait])
            step("vector", lambda e, par=par: e.tensor_scalar(
                st(ST_T0, par), st(ST_T0, par), cc(C_INV1024), None, ALU.mult))
        else:
            step("vector", lambda e, par=par: e.tensor_scalar(
                st(ST_T0, par), st(ST_SQ0, par), cc(C_INV1024), None, ALU.mult),
                extra=[sq_wait])
        step("vector", lambda e, par=par: e.tensor_scalar(
            st(ST_M, par), st(ST_SUM, par), cc(C_INV1024), None, ALU.mult))
        step("vector", lambda e, par=par: e.tensor_tensor(
            st(ST_M2, par), st(ST_M, par), st(ST_M, par), ALU.mult))
        step("vector", lambda e, par=par: e.tensor_tensor(
            st(ST_V, par), st(ST_T0, par), st(ST_M2, par), ALU.subtract))
        step("scalar", lambda e, par=par: e.activation(
            st(ST_T1, par), st(ST_V, par), AF.Ln, bias=cc(C_EPS)))
        step("scalar", lambda e, par=par: e.activation(
            st(ST_R, par), st(ST_T1, par), AF.Exp, scale=cc(C_NEGHALF)))
        return svec2.n

    def emit_dump(buf, cols):
        p = Plan(nc, DMAS)
        for c0 in range(0, cols, 4096):
            cw = min(4096, cols - c0)
            p.op("gpsimd", lambda e, c0=c0, cw=cw: e.dma_start(
                out=dbg[:, c0:c0 + cw], in_=buf[:, c0:c0 + cw]), inc=(dSt[0].h, 16))
            dSt[0].bump(16)
        p.emit()

    for seq in range(NSEQ):
        tb0 = seq * NTB

        xln0 = mem.sbuf(f"xln0_{seq}", [128, 8 * S], BF16)
        xt2 = mem.sbuf("a1_xt", [128, 2 * D], F32)
        xtm2 = mem.sbuf("a1_tm", [128, 2 * D], BF16)
        psT = [mem.psum("a1_ps0", [128, 512], BF16), mem.psum("a1_ps1", [128, 512], BF16)]
        p = Plan(nc, DMAS)
        rel_xt = [[], []]
        for tb in range(NTB):
            sl = tb % 2
            xt = xt2[:, sl * D:sl * D + D]
            p.op("sync", lambda e, t=tb, d=xt: e.dma_start(out=d, in_=xr[tb0 + t]),
                 waits=rel_xt[sl], inc=(dL[sl].h, 16))
            dsn = dL[sl].bump(16)
            p.op("vector", lambda e, d=xt, t=tb: e.reduce_sum(bc(B_SUM, t), d, axis=AX.X),
                 waits=[(dL[sl].h, dsn)], inc=(svec.h, 1))
            vr = svec.bump()
            p.op("scalar", lambda e, d=xt, t=tb: e.activation(
                scrap[:, (t % 2) * D:(t % 2) * D + D], d, AF.Square,
                accum_out=bc(B_SQ, t)),
                waits=[(dL[sl].h, dsn)], inc=(sact.h, 1))
            an = sact.bump()
            rel_xt[sl] = [(svec.h, vr), (sact.h, an)]
        rn = ln_batch(p, [(svec.h, svec.n), (sact.h, sact.n)])
        rel_xt_a = [list(rel_xt[0]), list(rel_xt[1])]
        rel_xt = [[], []]
        rel_psT = [[], []]
        rel_xtm = [[], []]
        for tb in range(NTB):
            sl = tb % 2
            xt = xt2[:, sl * D:sl * D + D]
            xtm = xtm2[:, sl * D:sl * D + D]
            p.op("sync", lambda e, t=tb, d=xt: e.dma_start(out=d, in_=xr[tb0 + t]),
                 waits=rel_xt[sl] + (rel_xt_a[sl] if tb < 2 else []), inc=(dL[sl].h, 16))
            dsn = dL[sl].bump(16)
            p.op("vector", lambda e, d=xt, m=xtm, t=tb: e.tensor_scalar(
                m, d, bc(B_M, t), bc(B_R, t), ALU.subtract, ALU.mult),
                waits=[(dL[sl].h, dsn), (svec2.h, rn)] + rel_xtm[sl], inc=(svec.h, 1))
            xn = svec.bump()
            rel_xt[sl] = [(svec.h, xn)]
            last_mn = 0
            for g in range(2):
                for c4 in range(4):
                    c = g * 4 + c4
                    w = ([(svec.h, xn)] if c4 == 0 else []) + (rel_psT[g] if c4 == 0 else [])
                    p.op("tensor", lambda e, c=c, c4=c4, m=xtm, g=g: e.transpose(
                        psT[g][:, c4 * 128:c4 * 128 + 128],
                        m[:, c * 128:c * 128 + 128], identb[:]),
                        waits=w, inc=((smm.h, 1) if c4 == 3 else None))
                last_mn = smm.bump()
                dv = xln0[:].rearrange("p (c t) -> p c t", c=8)
                p.op("vector", lambda e, g=g, t=tb, dv=dv: e.tensor_copy(
                    dv[:, g * 4:(g + 1) * 4, t * 128:t * 128 + 128], psT[g][:]),
                    waits=[(smm.h, last_mn)], inc=(svec2.h, 1))
                rel_psT[g] = [(svec2.h, svec2.bump())]
            rel_xtm[sl] = [(smm.h, last_mn)]
        p.emit()
        if seq == 0 and DEBUG_DUMP == "xln0":
            emit_dump(xln0, 8 * S)
        mem.free("a1_xt", "a1_tm", "a1_ps0", "a1_ps1")

        Qb = mem.sbuf(f"Qb_{seq}", [128, 8 * S], BF16, side="right")
        Kb = mem.sbuf(f"Kb_{seq}", [128, 8 * S], BF16, side="right")
        Va = mem.sbuf(f"Va_{seq}", [128, 16 * 1040], BF16, side="right")
        ws = mem.sbuf("a2_ws", [128, 8 * 1024], BF16)
        tm1b = mem.sbuf("a2_t1", [128, 3 * 512], BF16)
        tm2b = mem.sbuf("a2_t2", [128, 3 * 512], BF16)
        pa = [mem.psum("a2_pa0", [128, 512], F32), mem.psum("a2_pa1", [128, 512], F32)]
        pb = [mem.psum("a2_pb0", [128, 512], F32), mem.psum("a2_pb1", [128, 512], F32)]

        for wname, buf in (("wqT", Qb), ("wkT", Kb)):
            p = Plan(nc, DMAS)
            ww = rel_ws if wname == "wkT" else []
            for kc in range(8):
                p.op("gpsimd", lambda e, k=kc, nm=wname: e.dma_start(
                    out=ws[:, k * 1024:(k + 1) * 1024],
                    in_=din[nm][k * 128:(k + 1) * 128, :]),
                    waits=(ww if kc == 0 else []), inc=(dG.h, 16))
                dG.bump(16)
            dGn = dG.n
            rel_pa = [[], []]
            rel_pb = [[], []]
            ti = 0
            pend = []

            def emit_rot(item):
                tj, dstj, ttj, anj = item
                tm1 = tm1b[:, (tj % 3) * 512:(tj % 3) * 512 + 512]
                tm2 = tm2b[:, (tj % 3) * 512:(tj % 3) * 512 + 512]
                pbs = pb[tj % 2]
                p.op("tensor", lambda e, d=dstj, pbs=pbs: e.matmul(pbs[:], pswapb[:], d,
                                                                   start=True, stop=True),
                     waits=[(sact.h, anj)] + rel_pb[tj % 2], inc=(smm2.h, 1))
                m2 = smm2.bump()
                p.op("vector", lambda e, t=ttj, pbs=pbs, tm2=tm2: e.tensor_tensor(
                    tm2, pbs[:], sinf[:, t * 512:t * 512 + 512], ALU.mult),
                    waits=[(smm2.h, m2)], inc=(svec.h, 1))
                vt2 = svec.bump()
                rel_pb[tj % 2] = [(svec.h, vt2)]
                p.op("vector", lambda e, d=dstj, tm1=tm1, tm2=tm2: e.tensor_tensor(
                    d, tm1, tm2, ALU.add),
                    waits=[(svec.h, vt2)], inc=(svec.h, 1))
                svec.bump()

            for oc in range(8):
                for tt in range(4):
                    dst = buf[:, oc * S + tt * 512:oc * S + tt * 512 + 512]
                    ps = pa[ti % 2]
                    w0 = [(dG.h, dGn)] + rel_pa[ti % 2]
                    for kc in range(8):
                        p.op("tensor", lambda e, k=kc, o=oc, t=tt, ps=ps: e.matmul(
                            ps[:], ws[:, k * 1024 + o * 128:k * 1024 + o * 128 + 128],
                            xln0[:, k * S + t * 512:k * S + t * 512 + 512],
                            start=(k == 0), stop=(k == 7)),
                            waits=(w0 if kc == 0 else []),
                            inc=((smm.h, 1) if kc == 7 else None))
                    mn = smm.bump()
                    p.op("scalar", lambda e, d=dst, ps=ps: e.activation(d, ps[:], AF.Identity),
                         waits=[(smm.h, mn)], inc=(sact.h, 1))
                    an = sact.bump()
                    tm1 = tm1b[:, (ti % 3) * 512:(ti % 3) * 512 + 512]
                    p.op("vector", lambda e, t=tt, ps=ps, tm1=tm1: e.tensor_tensor(
                        tm1, ps[:], cosf[:, t * 512:t * 512 + 512], ALU.mult),
                        waits=[(sact.h, an)], inc=(svec.h, 1))
                    v1 = svec.bump()
                    rel_pa[ti % 2] = [(sact.h, an), (svec.h, v1)]
                    pend.append((ti, dst, tt, an))
                    if len(pend) > 1:
                        emit_rot(pend.pop(0))
                    ti += 1
            while pend:
                emit_rot(pend.pop(0))
            rel_ws = [(smm.h, smm.n)]
            p.emit()
            if seq == 0 and DEBUG_DUMP == ("Qb" if wname == "wqT" else "Kb"):
                emit_dump(buf, 8 * S)

        p = Plan(nc, DMAS)
        for kc in range(8):
            p.op("gpsimd", lambda e, k=kc: e.dma_start(
                out=ws[:, k * 1024:(k + 1) * 1024],
                in_=din["wvT"][k * 128:(k + 1) * 128, :]),
                waits=(rel_ws if kc == 0 else []), inc=(dG.h, 16))
            dG.bump(16)
        dGn = dG.n
        p.op("vector", lambda e: e.memset(Va[:, 64::65], 1.0))
        vav = Va.ap().rearrange("p (tb h c) -> p tb h c", tb=16, h=16, c=65)
        rel_pa = [[], []]
        ti = 0
        for tb in range(NTB):
            for ot in range(2):
                ps = pa[ti % 2]
                w0 = [(dG.h, dGn)] + rel_pa[ti % 2]
                for kc in range(8):
                    p.op("tensor", lambda e, k=kc, t=tb, o=ot, ps=ps: e.matmul(
                        ps[:], xln0[:, k * S + t * 128:k * S + t * 128 + 128],
                        ws[:, k * 1024 + o * 512:k * 1024 + o * 512 + 512],
                        start=(k == 0), stop=(k == 7)),
                        waits=(w0 if kc == 0 else []),
                        inc=((smm.h, 1) if kc == 7 else None))
                mn = smm.bump()
                p.op("vector", lambda e, t=tb, o=ot, ps=ps: e.tensor_copy(
                    vav[:, t, o * 8:(o + 1) * 8, 0:64],
                    ps.rearrange("p (h c) -> p h c", h=8)),
                    waits=[(smm.h, mn)], inc=(svec.h, 1))
                rel_pa[ti % 2] = [(svec.h, svec.bump())]
                ti += 1
        rel_ws = [(smm.h, smm.n)]
        p.emit()
        if seq == 0 and DEBUG_DUMP == "Va":
            emit_dump(Va, 16 * 1040)
        mem.free(f"xln0_{seq}", "a2_ws", "a2_t1", "a2_t2", "a2_pa0", "a2_pa1", "a2_pb0", "a2_pb1")

        hctm = mem.sbuf(f"hctm_{seq}", [128, 16 * 1024], BF16)
        eb = mem.sbuf("a3_e", [128, 3 * S], BF16)
        scs = [mem.psum("a3_sc0", [128, 1024], F32), mem.psum("a3_sc1", [128, 1024], F32)]
        pvs = [mem.psum("a3_pv0", [128, 65], F32), mem.psum("a3_pv1", [128, 65], F32)]
        p = Plan(nc, DMAS)
        iters = [(h, qb) for h in range(H) for qb in range(NTB)]
        NIT = len(iters)
        rel_sc = [[], []]
        rel_pv = [[], []]
        rel_e = [[], [], []]
        mg = {}
        ag = {}
        vm = {}
        g0_of = {}
        gctr = [0]

        def grp_of(qb):
            return [list(range(g, min(g + 8, qb + 1))) for g in range(0, qb + 1, 8)]

        def plan_scores(it):
            h, qb = iters[it]
            hp, hr = h // 2, (h % 2) * 64
            q0 = hp * S + qb * 128
            g0_of[it] = gctr[0]
            mg[it] = []
            for kbg in grp_of(qb):
                ms = scs[gctr[0] % 2]
                w0 = list(rel_sc[gctr[0] % 2])
                for j, kb in enumerate(kbg):
                    p.op("tensor", lambda e, kb=kb, j=j, ms=ms, hr=hr, hp=hp, q0=q0: e.matmul(
                        ms[:, j * 128:j * 128 + 128],
                        Kb[hr:hr + 64, hp * S + kb * 128:hp * S + kb * 128 + 128],
                        Qb[hr:hr + 64, q0:q0 + 128],
                        start=True, stop=True),
                        waits=(w0 if j == 0 else []),
                        inc=((smm.h, 1) if j == len(kbg) - 1 else None))
                mg[it].append(smm.bump())
                gctr[0] += 1

        def plan_exps(it):
            h, qb = iters[it]
            e0 = (it % 3) * S
            groups = grp_of(qb)
            for gi, kbg in enumerate(groups):
                ln = len(kbg)
                ms = scs[(g0_of[it] + gi) % 2]
                base = e0 + kbg[0] * 128
                wts = [(smm.h, mg[it][gi])] + (list(rel_e[it % 3]) if gi == 0 else [])
                p.op("scalar", lambda e, b=base, ln=ln, ms=ms: e.activation(
                    eb[:, b:b + ln * 128], ms[:, 0:ln * 128],
                    AF.Exp, bias=cc(C_NEGC), scale=cc(C_S8)),
                    waits=wts, inc=(sact.h, 1))
                rel_sc[(g0_of[it] + gi) % 2] = [(sact.h, sact.bump())]
            ag[it] = sact.n

        def plan_mask(it):
            h, qb = iters[it]
            e0 = (it % 3) * S
            p.op("vector", lambda e, qb=qb, e0=e0: e.tensor_tensor(
                eb[:, e0 + qb * 128:e0 + qb * 128 + 128],
                eb[:, e0 + qb * 128:e0 + qb * 128 + 128], maskb[:], ALU.mult),
                waits=[(sact.h, ag[it])], inc=(svec.h, 1))
            vm[it] = svec.bump()

        def plan_pv(it):
            h, qb = iters[it]
            e0 = (it % 3) * S
            pv = pvs[it % 2]
            w0 = [(sact.h, ag[it])] + rel_pv[it % 2]
            for kb in range(qb + 1):
                wts = (w0 if kb == 0 else []) + ([(svec.h, vm[it])] if kb == qb else [])
                p.op("tensor", lambda e, kb=kb, pv=pv, h=h, qb=qb, e0=e0: e.matmul(
                    pv, eb[:, e0 + kb * 128:e0 + kb * 128 + 128], vav[:, kb, h, :],
                    start=(kb == 0), stop=(kb == qb)),
                    waits=wts, inc=((smm2.h, 1) if kb == qb else None))
            rel_e[it % 3] = [(smm2.h, smm2.bump())]

        def plan_evac(it):
            h, qb = iters[it]
            pv = pvs[it % 2]
            p.op("vector", lambda e, pv=pv: e.reciprocal(st(ST_RZ), pv[:, 64:65]),
                 waits=[(smm2.h, smm2.n)], inc=(svec2.h, 1))
            rzn = svec2.bump()
            p.op("vector", lambda e, pv=pv, qb=qb, h=h: e.tensor_scalar(
                hctm[:, qb * 1024 + h * 64:qb * 1024 + h * 64 + 64],
                pv[:, 0:64], st(ST_RZ), None, ALU.mult),
                waits=[(svec2.h, rzn)], inc=(svec2.h, 1))
            rel_pv[it % 2] = [(svec2.h, svec2.bump())]

        plan_scores(0)
        for it in range(NIT):
            plan_exps(it)
            plan_mask(it)
            if it + 1 < NIT:
                plan_scores(it + 1)
            plan_pv(it)
            plan_evac(it)
        p.emit()
        if seq == 0 and DEBUG_DUMP == "hctm":
            emit_dump(hctm, 16 * 1024)
        mem.free(f"Qb_{seq}", f"Kb_{seq}", f"Va_{seq}", "a3_e",
                 "a3_sc0", "a3_sc1", "a3_pv0", "a3_pv1")

        hcf = mem.sbuf(f"hcf_{seq}", [128, 8 * S], BF16, side="right")
        psT = [mem.psum("a4_ps0", [128, 512], BF16), mem.psum("a4_ps1", [128, 512], BF16)]
        p = Plan(nc, DMAS)
        rel_psT = [[], []]
        for tb in range(NTB):
            for g in range(2):
                for c4 in range(4):
                    c = g * 4 + c4
                    p.op("tensor", lambda e, tb=tb, c=c, c4=c4, g=g: e.transpose(
                        psT[g][:, c4 * 128:c4 * 128 + 128],
                        hctm[:, tb * 1024 + c * 128:tb * 1024 + c * 128 + 128],
                        identb[:]),
                        waits=(rel_psT[g] if c4 == 0 else []),
                        inc=((smm.h, 1) if c4 == 3 else None))
                mn = smm.bump()
                dv = hcf[:].rearrange("p (c t) -> p c t", c=8)
                p.op("vector", lambda e, g=g, tb=tb, dv=dv: e.tensor_copy(
                    dv[:, g * 4:(g + 1) * 4, tb * 128:tb * 128 + 128], psT[g][:]),
                    waits=[(smm.h, mn)], inc=(svec.h, 1))
                rel_psT[g] = [(svec.h, svec.bump())]
        p.emit()
        if seq == 0 and DEBUG_DUMP == "hcf":
            emit_dump(hcf, 8 * S)
        mem.free(f"hctm_{seq}", "a4_ps0", "a4_ps1")

        ws = mem.sbuf("a5_ws", [128, 8 * 1024], BF16)
        aof = mem.sbuf("a5_ao", [128, 16 * 1024], F32)
        xt2 = mem.sbuf("a5_xt", [128, 2 * D], F32)
        x1t2 = mem.sbuf("a5_x1", [128, 2 * D], F32)
        xadj2 = mem.sbuf("a5_xa", [128, 2 * D], F32)
        xtm2 = mem.sbuf("a5_tm", [128, 2 * D], BF16)
        xstg2 = mem.sbuf("a5_st", [128, 2 * D], BF16)
        pOs = [mem.psum("a5_p0", [128, 512], F32), mem.psum("a5_p1", [128, 512], F32),
               mem.psum("a5_p2", [128, 512], F32), mem.psum("a5_p3", [128, 512], F32)]
        psT = [mem.psum("a5_t0", [128, 512], BF16), mem.psum("a5_t1", [128, 512], BF16)]
        p = Plan(nc, DMAS)
        for kc in range(8):
            p.op("gpsimd", lambda e, k=kc: e.dma_start(
                out=ws[:, k * 1024:(k + 1) * 1024],
                in_=din["woT"][k * 128:(k + 1) * 128, :]),
                waits=(rel_ws if kc == 0 else []), inc=(dG.h, 16))
            dG.bump(16)
        dGn = dG.n
        rel_pO = [[], [], [], []]
        pi = 0
        for tb in range(NTB):
            aos = aof[:, tb * 1024:(tb + 1) * 1024]
            for ot in range(2):
                ps = pOs[pi % 4]
                w0 = [(dG.h, dGn)] + rel_pO[pi % 4]
                for dc in range(8):
                    p.op("tensor", lambda e, d=dc, t=tb, o=ot, ps=ps: e.matmul(
                        ps[:], hcf[:, d * S + t * 128:d * S + t * 128 + 128],
                        ws[:, d * 1024 + o * 512:d * 1024 + o * 512 + 512],
                        start=(d == 0), stop=(d == 7)),
                        waits=(w0 if dc == 0 else []),
                        inc=((smm.h, 1) if dc == 7 else None))
                mn = smm.bump()
                p.op("scalar", lambda e, o=ot, d=aos, ps=ps: e.activation(
                    d[:, o * 512:o * 512 + 512], ps[:], AF.Identity),
                    waits=[(smm.h, mn)], inc=(sact.h, 1))
                rel_pO[pi % 4] = [(sact.h, sact.bump())]
                pi += 1
            evn = sact.n
            p.op("vector", lambda e, d=aos, t=tb: e.reduce_sum(bc(B_SUM, t), d, axis=AX.X),
                 waits=[(sact.h, evn)], inc=(svec.h, 1))
            svec.bump()
            p.op("scalar", lambda e, d=aos, t=tb: e.activation(
                scrap[:, (t % 2) * D:(t % 2) * D + D], d, AF.Square,
                accum_out=bc(B_SQ, t)),
                waits=[(sact.h, evn)], inc=(sact.h, 1))
            sact.bump()
        ln_batch(p, [(svec.h, svec.n), (sact.h, sact.n)], with_amr=True)
        rn = svec2.n
        rel_xt = [[], []]
        rel_x1t = [[], []]
        for tb in range(NTB):
            sl = tb % 2
            xt = xt2[:, sl * D:sl * D + D]
            x1t = x1t2[:, sl * D:sl * D + D]
            xadj = xadj2[:, sl * D:sl * D + D]
            aos = aof[:, tb * 1024:(tb + 1) * 1024]
            p.op("sync", lambda e, t=tb, d=xt: e.dma_start(out=d, in_=xr[tb0 + t]),
                 waits=rel_xt[sl], inc=(dL[sl].h, 16))
            dsx = dL[sl].bump(16)
            p.op("vector", lambda e, d=xt, a=xadj, t=tb: e.tensor_scalar(
                a, d, bc(B_AMR, t), None, ALU.add),
                waits=[(dL[sl].h, dsx), (svec2.h, rn)], inc=(svec.h, 1))
            xan = svec.bump()
            rel_xt[sl] = [(svec.h, xan)]
            p.op("vector", lambda e, d=x1t, a=xadj, s=aos, t=tb: e.scalar_tensor_tensor(
                d, s, bc(B_R, t), a, ALU.mult, ALU.add),
                waits=[(svec.h, xan)] + rel_x1t[sl], inc=(svec.h, 1))
            vstt = svec.bump()
            p.op("vector", lambda e, d=x1t, t=tb: e.reduce_sum(bc(B_SUM, t), d, axis=AX.X),
                 waits=[(svec.h, vstt)], inc=(svec.h, 1))
            vr1 = svec.bump()
            p.op("scalar", lambda e, d=x1t, t=tb: e.activation(
                scrap[:, (t % 2) * D:(t % 2) * D + D], d, AF.Square,
                accum_out=bc(B_SQ, t)),
                waits=[(svec.h, vstt)], inc=(sact.h, 1))
            an1 = sact.bump()
            p.op("gpsimd", lambda e, t=tb, d=x1t: e.dma_start(out=x1r[tb0 + t], in_=d),
                 waits=[(svec.h, vstt)], inc=(dSt[sl].h, 16))
            dSt[sl].bump(16)
            rel_x1t[sl] = [(dSt[sl].h, dSt[sl].n), (vr1 and svec.h, vr1), (sact.h, an1)]
        ln_batch(p, [(svec.h, svec.n), (sact.h, sact.n)])
        rn1 = svec2.n
        stc = [(dSt[0].h, dSt[0].n), (dSt[1].h, dSt[1].n)]
        rel_x1t2 = [[], []]
        rel_psT = [[], []]
        rel_stg = [[], []]
        rel_xtm = [[], []]
        for tb in range(NTB):
            sl = tb % 2
            x1t = x1t2[:, sl * D:sl * D + D]
            xtm = xtm2[:, sl * D:sl * D + D]
            xstg = xstg2[:, sl * D:sl * D + D]
            p.op("sync", lambda e, t=tb, d=x1t: e.dma_start(out=d, in_=x1r[tb0 + t]),
                 waits=rel_x1t2[sl] + (rel_x1t[sl] + stc if tb < 2 else []),
                 inc=(dL[sl].h, 16))
            dsx = dL[sl].bump(16)
            p.op("vector", lambda e, d=x1t, m=xtm, t=tb: e.tensor_scalar(
                m, d, bc(B_M, t), bc(B_R, t), ALU.subtract, ALU.mult),
                waits=[(dL[sl].h, dsx), (svec2.h, rn1)] + rel_xtm[sl], inc=(svec.h, 1))
            xn = svec.bump()
            rel_x1t2[sl] = [(svec.h, xn)]
            last_mn = 0
            for g in range(2):
                for c4 in range(4):
                    c = g * 4 + c4
                    w = ([(svec.h, xn)] if c4 == 0 else []) + (rel_psT[g] if c4 == 0 else [])
                    p.op("tensor", lambda e, c=c, c4=c4, m=xtm, g=g: e.transpose(
                        psT[g][:, c4 * 128:c4 * 128 + 128],
                        m[:, c * 128:c * 128 + 128], identb[:]),
                        waits=w, inc=((smm.h, 1) if c4 == 3 else None))
                last_mn = smm.bump()
                p.op("vector", lambda e, g=g, d=xstg: e.tensor_copy(
                    d[:, g * 512:g * 512 + 512], psT[g][:]),
                    waits=[(smm.h, last_mn)] + (rel_stg[sl] if g == 0 else []),
                    inc=(svec2.h, 1))
                rel_psT[g] = [(svec2.h, svec2.bump())]
            rel_xtm[sl] = [(smm.h, last_mn)]
            col0 = seq * S + tb * 128
            p.op("gpsimd", lambda e, d=xstg, c0=col0: e.dma_start(
                out=xln1d[:, :, c0:c0 + 128].transpose([1, 0, 2]),
                in_=d.rearrange("p (c f) -> p c f", c=8)),
                waits=[(svec2.h, svec2.n)], inc=(dSt[sl].h, 16))
            dSt[sl].bump(16)
            rel_stg[sl] = [(dSt[sl].h, dSt[sl].n)]
        p.emit()
        rel_ws = [(smm.h, smm.n)]
        mem.free(f"hcf_{seq}", "a5_ws", "a5_ao", "a5_xt", "a5_x1", "a5_xa", "a5_tm",
                 "a5_st", "a5_p0", "a5_p1", "a5_p2", "a5_p3", "a5_t0", "a5_t1")

    mem.free("cosf", "sinf", "cosb")

    w1s = mem.sbuf("w1s", [128, 8 * DH], BF16)
    w2s = mem.sbuf("w2s", [128, 8 * DH], BF16)
    xt1 = mem.sbuf("f1_x", [128, 2 * 8 * 512], BF16)
    t1f = mem.sbuf("f1_t1", [128, 2 * 512], F32)
    hstg = mem.sbuf("f1_h", [128, 2 * 512], BF16)
    pa = [mem.psum("f1_pa0", [128, 512], F32), mem.psum("f1_pa1", [128, 512], F32)]
    pb = [mem.psum("f1_pb0", [128, 512], F32), mem.psum("f1_pb1", [128, 512], F32)]
    p = Plan(nc, DMAS)
    grp_sem = {0: dG, 1: dG2, 2: dG3, 3: dSt[0]}
    wgate = {}
    for g in range(4):
        nm = "w1T" if g < 2 else "w2T"
        dstb = w1s if g < 2 else w2s
        sem = grp_sem[g]
        for j in range(4):
            kc = (g % 2) * 4 + j
            p.op("gpsimd", lambda e, k=kc, dstb=dstb, nm=nm: e.dma_start(
                out=dstb[:, k * DH:(k + 1) * DH], in_=din[nm][k * 128:(k + 1) * 128, :]),
                waits=(rel_ws if g == 0 and j == 0 else []), inc=(sem.h, 16))
            sem.bump(16)
        wgate[("w1" if g < 2 else "w2", (g % 2) * 4)] = (sem.h, sem.n)
    rel_x = [[], []]
    rel_pa = [[], []]
    rel_pb = [[], []]
    rel_t1 = [[], []]
    rel_h = [[], []]
    ti = 0
    for tt in range(8):
        sl = tt % 2
        xv = xt1[:, sl * 4096:sl * 4096 + 4096]
        p.op("sync", lambda e, t=tt, d=xv: e.dma_start(
            out=d.rearrange("p (k f) -> p k f", k=8),
            in_=xln1d[:, :, t * 512:t * 512 + 512].transpose([1, 0, 2])),
            waits=rel_x[sl], inc=(dL[sl].h, 16))
        dsn = dL[sl].bump(16)
        for oc in range(32):
            psl = ti % 2
            w0 = [(dL[sl].h, dsn)] + rel_pa[psl]
            for kc in range(8):
                wts = (w0 if kc == 0 else []) + (
                    [wgate[("w1", kc)]] if ti == 0 and kc in (0, 4) else [])
                p.op("tensor", lambda e, k=kc, o=oc, d=xv, ps=pa[psl]: e.matmul(
                    ps[:], w1s[:, k * DH + o * 128:k * DH + o * 128 + 128],
                    d[:, k * 512:k * 512 + 512], start=(k == 0), stop=(k == 7)),
                    waits=wts, inc=((smm.h, 1) if kc == 7 else None))
            mn1 = smm.bump()
            w0 = rel_pb[psl]
            for kc in range(8):
                wts = (w0 if kc == 0 else []) + (
                    [wgate[("w2", kc)]] if ti == 0 and kc in (0, 4) else [])
                p.op("tensor", lambda e, k=kc, o=oc, d=xv, ps=pb[psl]: e.matmul(
                    ps[:], w2s[:, k * DH + o * 128:k * DH + o * 128 + 128],
                    d[:, k * 512:k * 512 + 512], start=(k == 0), stop=(k == 7)),
                    waits=wts, inc=((smm.h, 1) if kc == 7 else None))
            mn2 = smm.bump()
            p.op("scalar", lambda e, o=oc, ps=pa[psl], d=t1f[:, psl * 512:psl * 512 + 512]:
                 e.activation(d, ps[:], AF.Silu, scale=s1s[:, o:o + 1]),
                 waits=[(smm.h, mn1)] + rel_t1[psl], inc=(sact.h, 1))
            an = sact.bump()
            rel_pa[psl] = [(sact.h, an)]
            p.op("vector", lambda e, o=oc, ps=pb[psl], t1=t1f[:, psl * 512:psl * 512 + 512],
                 d=hstg[:, psl * 512:psl * 512 + 512]:
                 e.scalar_tensor_tensor(d, ps[:], s2s[:, o:o + 1], t1, ALU.mult, ALU.mult),
                 waits=[(smm.h, mn2), (sact.h, an)] + rel_h[psl], inc=(svec.h, 1))
            vn = svec.bump()
            rel_pb[psl] = [(svec.h, vn)]
            rel_t1[psl] = [(svec.h, vn)]
            p.op("gpsimd", lambda e, t=tt, o=oc, d=hstg[:, psl * 512:psl * 512 + 512]:
                 e.dma_start(out=hd[t, o], in_=d), waits=[(svec.h, vn)], inc=(dSt[psl].h, 16))
            dSt[psl].bump(16)
            rel_h[psl] = [(dSt[psl].h, dSt[psl].n)]
            ti += 1
        rel_x[sl] = [(smm.h, smm.n)]
    p.emit()
    mem.free("w1s", "w2s", "f1_x", "f1_t1", "f1_h", "f1_pa0", "f1_pa1", "f1_pb0", "f1_pb1")

    for phase, (wnm, src_d, dst_d, hss, dst_kind) in enumerate(
            [("h1T", hd, g1d, hs1s, "tt"), ("h2T", g1d, g2d, hs2s, "tb")]):
        wg = mem.sbuf("f2_wg", [128, 48 * 1024], BF16)
        ht = mem.sbuf("f2_ht", [128, 2 * 32 * 512], BF16)
        gstg = mem.sbuf("f2_g", [128, 16 * 512], BF16)
        ps2 = [mem.psum("f2_p0", [128, 512], F32), mem.psum("f2_p1", [128, 512], F32)]
        p = Plan(nc, DMAS)
        half_sem = {}
        rel_half = {}
        grp_end = {}

        def load_half(half):
            slot = half % 3
            sem = [dG, dG2, dG3][slot]
            g, off = half // 2, (half % 2) * 16
            w = rel_half.get(slot, [])
            for j in range(16):
                kc = off + j
                p.op("gpsimd", lambda e, k=kc, g=g, j=j, slot=slot, nm=wnm: e.dma_start(
                    out=wg[:, (slot * 16 + j) * 1024:(slot * 16 + j) * 1024 + 1024],
                    in_=din[nm][k * 128:(k + 1) * 128, g * 1024:(g + 1) * 1024]),
                    waits=(w if j == 0 else []), inc=(sem.h, 16))
                sem.bump(16)
            half_sem[half] = (sem.h, sem.n)

        def wslice(g, kc):
            half = 2 * g + (1 if kc >= 16 else 0)
            slot = half % 3
            j = kc % 16
            c0 = (slot * 16 + j) * 1024
            return c0

        load_half(0)
        load_half(1)
        rel_ht = [[], []]
        rel_ps = [[], []]
        rel_g = [[], []]
        ht_tile = [-1, -1]
        ht_dsn = [0, 0]
        ti = 0
        for ocg in range(4):
            if ocg >= 1:
                rel_half[(2 * ocg + 1) % 3] = [(smm.h, grp_end[ocg - 1])]
                load_half(2 * ocg + 1)
            if ocg + 1 < 4:
                rel_half[(2 * ocg + 2) % 3] = ([(smm.h, grp_end[ocg - 1])]
                                               if ocg >= 1 else [])
                load_half(2 * ocg + 2)
            tts = list(range(8)) if ocg % 2 == 0 else list(range(7, -1, -1))
            for tt in tts:
                sl = tt % 2
                hv = ht[:, sl * 16384:sl * 16384 + 16384]
                if ht_tile[sl] != tt:
                    for q in range(4):
                        p.op("sync", lambda e, t=tt, q=q, d=hv, sd=src_d: e.dma_start(
                            out=d.rearrange("p (k f) -> p k f", k=32)[:, q * 8:(q + 1) * 8, :],
                            in_=sd[t, q * 8:(q + 1) * 8].transpose([1, 0, 2])),
                            waits=(rel_ht[sl] if q == 0 else []), inc=(dL[sl].h, 16))
                        dL[sl].bump(16)
                    ht_tile[sl] = tt
                    ht_dsn[sl] = dL[sl].n
                dsn = ht_dsn[sl]
                an = 0
                for oc in range(8):
                    psl = ti % 2
                    w0 = [half_sem[2 * ocg], (dL[sl].h, dsn)] + rel_ps[psl]
                    for kc in range(32):
                        wts = w0 if kc == 0 else ([half_sem[2 * ocg + 1]] if kc == 16 else [])
                        p.op("tensor", lambda e, k=kc, o=oc, d=hv, ps=ps2[psl],
                             c0=wslice(ocg, kc): e.matmul(
                                 ps[:], wg[:, c0 + o * 128:c0 + o * 128 + 128],
                                 d[:, k * 512:k * 512 + 512],
                                 start=(k == 0), stop=(k == 31)),
                             waits=wts, inc=((smm.h, 1) if kc == 31 else None))
                    mn = smm.bump()
                    gsl = gstg[:, (sl * 8 + oc) * 512:(sl * 8 + oc) * 512 + 512]
                    p.op("scalar", lambda e, o=oc, g=ocg, ps=ps2[psl], d=gsl, hss=hss:
                         e.activation(d, ps[:], AF.Gelu, scale=hss[:, g * 8 + o:g * 8 + o + 1]),
                         waits=[(smm.h, mn)] + (rel_g[sl] if oc == 0 else []),
                         inc=(sact.h, 1))
                    an = sact.bump()
                    rel_ps[psl] = [(sact.h, an)]
                    ti += 1
                for oc in range(8):
                    gsl = gstg[:, (sl * 8 + oc) * 512:(sl * 8 + oc) * 512 + 512]
                    if dst_kind == "tt":
                        p.op("gpsimd", lambda e, t=tt, o=ocg * 8 + oc, d=gsl, dd=dst_d:
                             e.dma_start(out=dd[t, o], in_=d),
                             waits=([(sact.h, an)] if oc == 0 else []), inc=(dSt[sl].h, 16))
                    else:
                        p.op("gpsimd", lambda e, t=tt, o=ocg * 8 + oc, d=gsl, dd=dst_d:
                             e.dma_start(out=dd[t * 4:(t + 1) * 4, o].transpose([1, 0, 2]),
                                         in_=d.rearrange("p (b f) -> p b f", b=4)),
                             waits=([(sact.h, an)] if oc == 0 else []), inc=(dSt[sl].h, 16))
                    dSt[sl].bump(16)
                rel_g[sl] = [(dSt[sl].h, dSt[sl].n)]
                rel_ht[sl] = [(smm.h, smm.n)]
            grp_end[ocg] = smm.n
        p.emit()
        mem.free("f2_wg", "f2_ht", "f2_g", "f2_p0", "f2_p1")

    w3s = mem.sbuf("w3s", [128, 32 * 1024], BF16)
    w3raw = mem.sbuf("f4_wr", [128, 2 * 1024], BF16)
    s3f = mem.sbuf("f4_s3f", [1, 1024], F32)
    s3b = mem.sbuf("f4_s3b", [1, 1024], BF16)
    ones1 = mem.sbuf("f4_ones", [1, 128], BF16)
    s3bc = mem.sbuf("f4_s3bc", [128, 1024], BF16)
    g2t = mem.sbuf("f4_g2", [128, 2 * 4096], BF16)
    x1t2 = mem.sbuf("f4_x1", [128, 2 * D], F32)
    xadj2 = mem.sbuf("f4_xa", [128, 2 * D], F32)
    outt2 = mem.sbuf("f4_o", [128, 2 * D], F32)
    pO = [mem.psum("f4_p0", [128, 512], F32), mem.psum("f4_p1", [128, 512], F32)]
    p = Plan(nc, DMAS)
    p.op("sync", lambda e: e.dma_start(out=s3f[:], in_=din["s3row"]), inc=(dL[0].h, 16))
    dsn = dL[0].bump(16)
    p.op("vector", lambda e: e.tensor_copy(s3b[:], s3f[:]), waits=[(dL[0].h, dsn)])
    p.op("vector", lambda e: e.memset(ones1[:], 1.0), inc=(svec.h, 1))
    vn0 = svec.bump()
    for o in range(2):
        p.op("tensor", lambda e, o=o: e.matmul(pO[o][:], ones1[:],
                                               s3b[0:1, o * 512:o * 512 + 512],
                                               start=True, stop=True),
             waits=[(svec.h, vn0)], inc=(smm.h, 1))
        mn = smm.bump()
        p.op("vector", lambda e, o=o: e.tensor_copy(s3bc[:, o * 512:o * 512 + 512], pO[o][:]),
             waits=[(smm.h, mn)], inc=(svec.h, 1))
        svec.bump()
    vbc = svec.n
    rel_wr = [[], []]
    wsem = [dG, dG2, dG3, dG]
    vw3b = []
    for b in range(4):
        sem = wsem[b]
        for j in range(8):
            c = b * 8 + j
            sl = c % 2
            p.op("gpsimd", lambda e, c=c, d=w3raw[:, (c % 2) * 1024:(c % 2) * 1024 + 1024]:
                 e.dma_start(out=d, in_=din["w3T"][c * 128:(c + 1) * 128, :]),
                 waits=rel_wr[sl], inc=(sem.h, 16))
            dgn = sem.bump(16)
            p.op("vector", lambda e, c=c, s=w3raw[:, (c % 2) * 1024:(c % 2) * 1024 + 1024]:
                 e.tensor_tensor(w3s[:, c * 1024:(c + 1) * 1024], s, s3bc[:], ALU.mult),
                 waits=[(sem.h, dgn), (svec.h, vbc)], inc=(svec.h, 1))
            rel_wr[sl] = [(svec.h, svec.bump())]
        vw3b.append(svec.n)
    vw3 = svec.n
    ff = mem.sbuf("f4_ff", [128, 16 * 1024], F32)
    rel_g2 = [[], []]
    rel_x1 = [[], []]
    rel_pO = [[], [], [], []]
    rel_out = [[], []]
    pOs = [pO[0], pO[1], mem.psum("f4_p2", [128, 512], F32),
           mem.psum("f4_p3", [128, 512], F32)]
    pi = 0
    rel_ff = [[], []]
    for half in range(4):
        hb = half * 8
        fo = (half % 2) * 8
        for tbl in range(8):
            tb = hb + tbl
            sl = tb % 2
            gv = g2t[:, sl * 4096:sl * 4096 + 4096]
            ffs = ff[:, (fo + tbl) * 1024:(fo + tbl) * 1024 + 1024]
            for q in range(4):
                p.op("sync", lambda e, t=tb, q=q, d=gv: e.dma_start(
                    out=d.rearrange("p (k f) -> p k f", k=32)[:, q * 8:(q + 1) * 8, :],
                    in_=g2d[t, q * 8:(q + 1) * 8].transpose([1, 0, 2])),
                    waits=(rel_g2[sl] if q == 0 else []), inc=(dL[sl].h, 16))
                dL[sl].bump(16)
            dsn = dL[sl].n
            for ot in range(2):
                ps = pOs[pi % 4]
                w0 = [(dL[sl].h, dsn), (svec.h, vw3b[0])] + rel_pO[pi % 4]
                for dc in range(32):
                    wts = (w0 if dc == 0 else
                           ([(svec.h, vw3b[dc // 8])] if dc % 8 == 0 else []))
                    p.op("tensor", lambda e, d=dc, o=ot, g=gv, ps=ps: e.matmul(
                        ps[:], g[:, d * 128:d * 128 + 128],
                        w3s[:, d * 1024 + o * 512:d * 1024 + o * 512 + 512],
                        start=(d == 0), stop=(d == 31)),
                        waits=wts, inc=((smm.h, 1) if dc == 31 else None))
                mn = smm.bump()
                p.op("scalar", lambda e, o=ot, d=ffs, ps=ps: e.activation(
                    d[:, o * 512:o * 512 + 512], ps[:], AF.Identity),
                    waits=[(smm.h, mn)] + (rel_ff[half % 2] if tbl == 0 and ot == 0 else []),
                    inc=(sact.h, 1))
                rel_pO[pi % 4] = [(sact.h, sact.bump())]
                pi += 1
            rel_g2[sl] = [(smm.h, smm.n)]
            evn = sact.n
            p.op("vector", lambda e, d=ffs, t=fo + tbl: e.reduce_sum(
                bc(B_SUM, t), d, axis=AX.X),
                waits=[(sact.h, evn)], inc=(svec.h, 1))
            svec.bump()
            p.op("scalar", lambda e, d=ffs, t=fo + tbl: e.activation(
                scrap[:, (t % 2) * D:(t % 2) * D + D], d, AF.Square,
                accum_out=bc(B_SQ, t)),
                waits=[(sact.h, evn)], inc=(sact.h, 1))
            sact.bump()
        ln_batch(p, [(svec.h, svec.n), (sact.h, sact.n)], with_amr=True)
        rn = svec2.n
        for tbl in range(8):
            tb = hb + tbl
            sl = tb % 2
            x1t = x1t2[:, sl * D:sl * D + D]
            outt = outt2[:, sl * D:sl * D + D]
            xadj = xadj2[:, sl * D:sl * D + D]
            ffs = ff[:, (fo + tbl) * 1024:(fo + tbl) * 1024 + 1024]
            p.op("gpsimd", lambda e, t=tb, d=x1t: e.dma_start(out=d, in_=x1r[t]),
                 waits=rel_x1[sl], inc=(dG2.h, 16))
            dsx = dG2.bump(16)
            p.op("vector", lambda e, d=x1t, a=xadj, t=fo + tbl: e.tensor_scalar(
                a, d, bc(B_AMR, t), None, ALU.add),
                waits=[(dG2.h, dsx), (svec2.h, rn)], inc=(svec.h, 1))
            xan = svec.bump()
            rel_x1[sl] = [(svec.h, xan)]
            p.op("vector", lambda e, d=outt, a=xadj, s=ffs, t=fo + tbl:
                 e.scalar_tensor_tensor(d, s, bc(B_R, t), a, ALU.mult, ALU.add),
                 waits=[(svec.h, xan)] + rel_out[sl], inc=(svec.h, 1))
            vstt = svec.bump()
            p.op("gpsimd", lambda e, t=tb, d=outt: e.dma_start(out=outr[t], in_=d),
                 waits=[(svec.h, vstt)], inc=(dSt[sl].h, 16))
            dSt[sl].bump(16)
            rel_out[sl] = [(dSt[sl].h, dSt[sl].n), (svec.h, vstt)]
        rel_ff[half % 2] = [(svec.h, svec.n)]
    p.op("gpsimd", lambda e: e.nop(), waits=[(dSt[0].h, dSt[0].n), (dSt[1].h, dSt[1].n)])
    p.emit()
    mem.free("w3s", "f4_wr", "f4_s3f", "f4_s3b", "f4_ones", "f4_s3bc", "f4_g2",
             "f4_ff", "f4_x1", "f4_xa", "f4_o", "f4_p0", "f4_p1", "f4_p2", "f4_p3")

    for g in sem_guards:
        g.__exit__(None, None, None)
    mem.free(*list(mem.guards))
    return nc


_NC_CACHE = None


def kernel(**inputs):
    global _NC_CACHE
    f32c = lambda a: np.ascontiguousarray(np.asarray(a, dtype=np.float32))
    x = f32c(inputs["L0_attn_x"])
    wqT = f32c(np.asarray(inputs["L0_attn_wq"]).T)
    wkT = f32c(np.asarray(inputs["L0_attn_wk"]).T)
    wvT = f32c(np.asarray(inputs["L0_attn_wv"]).T)
    woT = f32c(np.asarray(inputs["L0_attn_wo"]).T)
    w1T = f32c(np.asarray(inputs["L0_ffn_w1_w"]).T)
    w2T = f32c(np.asarray(inputs["L0_ffn_w2_w"]).T)
    h1T = f32c(np.asarray(inputs["L0_ffn_h1_w"]).T)
    h2T = f32c(np.asarray(inputs["L0_ffn_h2_w"]).T)
    w3T = f32c(np.asarray(inputs["L0_ffn_w3_w"]).T)
    s1r = f32c(np.asarray(inputs["L0_ffn_w1_s"]).reshape(32, 128).T)
    s2r = f32c(np.asarray(inputs["L0_ffn_w2_s"]).reshape(32, 128).T)
    hs1r = f32c(np.asarray(inputs["L0_ffn_h1_s"]).reshape(32, 128).T)
    hs2r = f32c(np.asarray(inputs["L0_ffn_h2_s"]).reshape(32, 128).T)
    s3row = f32c(np.asarray(inputs["L0_ffn_w3_s"]).reshape(1, 1024))

    j = (np.arange(128) % 64) // 2
    inv = np.power(10000.0, -2.0 * j.astype(np.float64) / 64.0)
    ang = inv[:, None] * np.arange(S, dtype=np.float64)[None, :]
    cosT = np.cos(ang).astype(np.float32)
    sinT = np.sin(ang).astype(np.float32)
    pswap = np.zeros((128, 128), np.float32)
    idx = np.arange(0, 128, 2)
    pswap[idx + 1, idx] = -1.0
    pswap[idx, idx + 1] = 1.0
    identc = np.eye(128, dtype=np.float32)
    maskc = (np.arange(128)[:, None] <= np.arange(128)[None, :]).astype(np.float32)

    if _NC_CACHE is None:
        _NC_CACHE = build_program()
    nc = _NC_CACHE

    shared = dict(wqT=wqT, wkT=wkT, wvT=wvT, woT=woT, w1T=w1T, w2T=w2T,
                  h1T=h1T, h2T=h2T, w3T=w3T, s1r=s1r, s2r=s2r, hs1r=hs1r,
                  hs2r=hs2r, s3row=s3row, cosT=cosT, sinT=sinT, pswap=pswap,
                  identc=identc, maskc=maskc)
    in_maps = []
    for c in range(8):
        m = dict(shared)
        m["x"] = np.ascontiguousarray(x[2 * c:2 * c + 2].reshape(T, D))
        in_maps.append(m)
    res = run_bass_kernel_spmd(nc, in_maps, core_ids=list(range(8)))
    out = np.concatenate(
        [np.asarray(res.results[c]["out"], np.float32).reshape(2, S, D) for c in range(8)],
        axis=0)
    return out


# version 21
# speedup vs baseline: 1.0398x; 1.0398x over previous
import numpy as np
import concourse.bass as bass
import concourse.mybir as mybir
from concourse.bass_utils import run_bass_kernel_spmd

F32 = mybir.dt.float32
BF16 = mybir.dt.bfloat16
AF = mybir.ActivationFunctionType
ALU = mybir.AluOpType
AX = mybir.AxisListType

D = 1024
S = 2048
NSEQ = 2
T = NSEQ * S
H = 16
DH = 4096
NTB = S // 128
C_SHIFT = 10.0

ENGS = ("sync", "gpsimd", "tensor", "scalar", "vector")

C_NEGC, C_S8, C_EPS, C_INV1024, C_NEG1, C_NEGHALF = range(6)
CONST_VALS = [-C_SHIFT, 0.125, 1e-5, 1.0 / 1024.0, -1.0, -0.5]

(ST_SUM, ST_SQ0, ST_SQ1, ST_M, ST_M2, ST_V, ST_R, ST_NEGM, ST_MR, ST_AMR,
 ST_T0, ST_T1, ST_RZ, ST_T2, ST_T3) = range(15)


class Ctr:
    def __init__(self, h):
        self.h = h
        self.n = 0

    def bump(self, k=1):
        self.n += k
        return self.n


PLAN_LIMIT = None
DEBUG_DRAM = False
DEBUG_DUMP = None


class Plan:

    IDX = 0

    def __init__(self, nc, dmas=None):
        self.nc = nc
        self.dmas = dmas or {}
        self.ops = {e: [] for e in ENGS}

    def op(self, eng, fn, waits=(), inc=None):
        self.ops[eng].append((list(waits), fn, inc))

    def emit(self):
        Plan.IDX += 1
        if PLAN_LIMIT is not None and Plan.IDX > PLAN_LIMIT:
            return
        for eng, ctrs in self.dmas.items():
            if self.ops[eng]:
                self.op(eng, lambda e: e.nop(),
                        waits=[(c.h, c.n) for c in ctrs if c.n > 0])

        def mk(ops):
            def run(e):
                for waits, fn, inc in ops:
                    for sem, val in waits:
                        if val > 0:
                            e.wait_ge(sem, val)
                    inst = fn(e)
                    if inc is not None:
                        inst.then_inc(inc[0], inc[1])
            return run

        with self.nc.Block() as block:
            for eng in ENGS:
                if self.ops[eng]:
                    getattr(block, eng)(mk(self.ops[eng]))


class Mem:

    def __init__(self, nc):
        self.nc = nc
        self.guards = {}
        self.order = {}
        self.seq = 0

    def _reg(self, name, g):
        h = g.__enter__()
        self.guards[name] = g
        self.order[name] = self.seq
        self.seq += 1
        return h

    def sbuf(self, name, shape, dt, side=None):
        assert name not in self.guards
        return self._reg(name, self.nc.sbuf_tensor(f"{name}_u{self.seq}", shape, dt, side=side))

    def init_psum_pool(self):
        g = []
        for nm, shape, dt in [("psA", [128, 1024], F32), ("psB", [128, 1024], F32),
                              ("psC", [128, 65], F32), ("psD", [128, 65], F32),
                              ("psT0", [128, 512], BF16), ("psT1", [128, 512], BF16)]:
            cm = self.nc.psum_tensor(nm, shape, dt)
            g.append(cm.__enter__())
        pA, pB, pC, pD, pT0, pT1 = g
        self.pool = {
            "1024F": [pA[:], pB[:]],
            "512F": [pA[:, 0:512], pA[:, 512:1024], pB[:, 0:512], pB[:, 512:1024]],
            "65F": [pC[:], pD[:]],
            "128B": [pT0[:], pT1[:]],
            "512B": [pT0[:], pT1[:]],
        }
        self.pool_free = {k: list(range(len(v))) for k, v in self.pool.items()}
        self.pool_named = {}

    def psum(self, name, shape, dt=F32):
        key = {(1024, 0): "1024F", (512, 0): "512F", (65, 0): "65F",
               (128, 1): "128B", (512, 1): "512B"}[(shape[1], 1 if dt == BF16 else 0)]
        idx = self.pool_free[key].pop(0)
        self.pool_named[name] = (key, idx)
        return self.pool[key][idx]

    def free(self, *names):
        ps = [n for n in names if n in self.pool_named]
        sb = [n for n in names if n not in self.pool_named]
        for n in ps:
            key, idx = self.pool_named.pop(n)
            self.pool_free[key].append(idx)
            self.pool_free[key].sort()
        for n in sorted(sb, key=lambda n: -self.order[n]):
            self.guards.pop(n).__exit__(None, None, None)
            del self.order[n]


BASS_KW = {}


def build_program():
    Plan.IDX = 0
    nc = bass.Bass("TRN2", target_bir_lowering=False, **BASS_KW)

    din = {}
    for name, shape in [
        ("x", [T, D]), ("wqT", [D, D]), ("wkT", [D, D]), ("wvT", [D, D]),
        ("woT", [D, D]), ("w1T", [D, DH]), ("w2T", [D, DH]), ("h1T", [DH, DH]),
        ("h2T", [DH, DH]), ("w3T", [DH, D]), ("s1r", [128, 32]), ("s2r", [128, 32]),
        ("hs1r", [128, 32]), ("hs2r", [128, 32]), ("s3row", [1, D]),
        ("cosT", [128, S]), ("sinT", [128, S]), ("pswap", [128, 128]),
        ("identc", [128, 128]), ("maskc", [128, 128]),
    ]:
        din[name] = nc.dram_tensor(name, shape, F32, kind="ExternalInput").ap()

    out_d = nc.dram_tensor("out", [T, D], F32, kind="ExternalOutput").ap()
    scratch_kind = "ExternalOutput" if DEBUG_DRAM else "Internal"
    x1d = nc.dram_tensor("x1d", [T, D], F32, kind=scratch_kind).ap()
    xln1d = nc.dram_tensor("xln1d", [8, 128, T], BF16, kind=scratch_kind).ap()
    hd = nc.dram_tensor("hd", [8, 32, 128, 512], BF16, kind=scratch_kind).ap()
    g1d = nc.dram_tensor("g1d", [8, 32, 128, 512], BF16, kind=scratch_kind).ap()
    g2d = nc.dram_tensor("g2d", [32, 32, 128, 128], BF16, kind=scratch_kind).ap()

    dbg = (nc.dram_tensor("dbg", [128, 16640], BF16, kind="ExternalOutput").ap()
           if DEBUG_DUMP else None)

    xr = din["x"].rearrange("(tb p) d -> tb p d", p=128)
    x1r = x1d.rearrange("(tb p) d -> tb p d", p=128)
    outr = out_d.rearrange("(tb p) d -> tb p d", p=128)

    mem = Mem(nc)
    mem.init_psum_pool()
    sem_guards = []
    sems = {}
    for nm in ("dG", "dG2", "dG3", "dL0", "dL1", "dS0", "dS1", "mm", "mm2", "act", "vec", "vec2"):
        g = nc.semaphore(nm)
        sems[nm] = Ctr(g.__enter__())
        sem_guards.append(g)
    dG = sems["dG"]
    dG2 = sems["dG2"]
    dG3 = sems["dG3"]
    dL = [sems["dL0"], sems["dL1"]]
    dSt = [sems["dS0"], sems["dS1"]]
    smm, smm2, sact, svec, svec2 = (sems[k] for k in ("mm", "mm2", "act", "vec", "vec2"))
    DMAS = {"sync": [dL[0], dL[1]], "gpsimd": [dG, dG2, dG3, dSt[0], dSt[1]]}

    consts = mem.sbuf("consts", [128, 8], F32)
    identb = mem.sbuf("identb", [128, 128], BF16)
    pswapb = mem.sbuf("pswapb", [128, 128], BF16)
    maskb = mem.sbuf("maskb", [128, 128], BF16)
    cosf = mem.sbuf("cosf", [128, S], F32, side="right")
    sinf = mem.sbuf("sinf", [128, S], F32, side="right")
    cosb = mem.sbuf("cosb", [128, S], BF16, side="right")
    s1s = mem.sbuf("s1s", [128, 32], F32)
    s2s = mem.sbuf("s2s", [128, 32], F32)
    hs1s = mem.sbuf("hs1s", [128, 32], F32)
    hs2s = mem.sbuf("hs2s", [128, 32], F32)
    scrap = mem.sbuf("scrap", [128, 2048], BF16)
    stt = mem.sbuf("stt", [128, 48], F32)
    bst = mem.sbuf("bst", [128, 8 * 16], F32)

    def st(i, par=0):
        return stt[:, par * 16 + i:par * 16 + i + 1]

    (B_SUM, B_SQ, B_M, B_M2, B_V, B_R, B_MR, B_AMR) = range(8)

    def bs(i):
        return bst[:, i * 16:(i + 1) * 16]

    def bc(i, tb):
        return bst[:, i * 16 + tb:i * 16 + tb + 1]

    def ln_batch(p, pre_wait, with_amr=False):
        def step(eng, fn, extra=()):
            p.op(eng, fn, waits=[(svec2.h, svec2.n)] + list(extra), inc=(svec2.h, 1))
            svec2.bump()

        step("vector", lambda e: e.tensor_scalar(bs(B_M), bs(B_SUM), cc(C_INV1024),
                                                 None, ALU.mult), extra=list(pre_wait))
        step("vector", lambda e: e.tensor_tensor(bs(B_M2), bs(B_M), bs(B_M), ALU.mult))
        step("vector", lambda e: e.tensor_scalar(bs(B_V), bs(B_SQ), cc(C_INV1024),
                                                 None, ALU.mult))
        step("vector", lambda e: e.tensor_tensor(bs(B_V), bs(B_V), bs(B_M2), ALU.subtract))
        step("scalar", lambda e: e.activation(bs(B_V), bs(B_V), AF.Ln, bias=cc(C_EPS)))
        step("scalar", lambda e: e.activation(bs(B_R), bs(B_V), AF.Exp, scale=cc(C_NEGHALF)))
        if with_amr:
            step("vector", lambda e: e.tensor_tensor(bs(B_MR), bs(B_M), bs(B_R), ALU.mult))
            step("vector", lambda e: e.tensor_scalar(bs(B_AMR), bs(B_MR), cc(C_NEG1),
                                                     None, ALU.mult))
        return svec2.n

    def cc(i):
        return consts[:, i:i + 1]

    p = Plan(nc, DMAS)
    for i, v in enumerate(CONST_VALS):
        p.op("vector", lambda e, i=i, v=v: e.memset(consts[:, i:i + 1], v))
    for dst, src in [(identb, din["identc"]), (pswapb, din["pswap"]),
                     (maskb, din["maskc"])]:
        p.op("gpsimd", lambda e, d=dst, s=src: e.dma_start(out=d[:], in_=s),
             inc=(dG.h, 16))
        dG.bump(16)
    for dst, src in [(cosf, din["cosT"]), (sinf, din["sinT"]), (s1s, din["s1r"]),
                     (s2s, din["s2r"]), (hs1s, din["hs1r"]), (hs2s, din["hs2r"])]:
        p.op("sync", lambda e, d=dst, s=src: e.dma_start(out=d[:], in_=s),
             inc=(dL[0].h, 16))
        dL[0].bump(16)
    p.op("vector", lambda e: e.tensor_copy(cosb[:], cosf[:]),
         waits=[(dL[0].h, dL[0].n)])
    p.emit()

    def ln_tail(p, sq_wait, two_sq=False, par=0):
        def step(eng, fn, extra=()):
            p.op(eng, fn, waits=[(svec2.h, svec2.n)] + list(extra), inc=(svec2.h, 1))
            svec2.bump()

        if two_sq:
            step("vector", lambda e, par=par: e.tensor_tensor(
                st(ST_T0, par), st(ST_SQ0, par), st(ST_SQ1, par), ALU.add), extra=[sq_wait])
            step("vector", lambda e, par=par: e.tensor_scalar(
                st(ST_T0, par), st(ST_T0, par), cc(C_INV1024), None, ALU.mult))
        else:
            step("vector", lambda e, par=par: e.tensor_scalar(
                st(ST_T0, par), st(ST_SQ0, par), cc(C_INV1024), None, ALU.mult),
                extra=[sq_wait])
        step("vector", lambda e, par=par: e.tensor_scalar(
            st(ST_M, par), st(ST_SUM, par), cc(C_INV1024), None, ALU.mult))
        step("vector", lambda e, par=par: e.tensor_tensor(
            st(ST_M2, par), st(ST_M, par), st(ST_M, par), ALU.mult))
        step("vector", lambda e, par=par: e.tensor_tensor(
            st(ST_V, par), st(ST_T0, par), st(ST_M2, par), ALU.subtract))
        step("scalar", lambda e, par=par: e.activation(
            st(ST_T1, par), st(ST_V, par), AF.Ln, bias=cc(C_EPS)))
        step("scalar", lambda e, par=par: e.activation(
            st(ST_R, par), st(ST_T1, par), AF.Exp, scale=cc(C_NEGHALF)))
        return svec2.n

    def emit_dump(buf, cols):
        p = Plan(nc, DMAS)
        for c0 in range(0, cols, 4096):
            cw = min(4096, cols - c0)
            p.op("gpsimd", lambda e, c0=c0, cw=cw: e.dma_start(
                out=dbg[:, c0:c0 + cw], in_=buf[:, c0:c0 + cw]), inc=(dSt[0].h, 16))
            dSt[0].bump(16)
        p.emit()

    rel_ws = []
    for seq in range(NSEQ):
        tb0 = seq * NTB

        xln0 = mem.sbuf(f"xln0_{seq}", [128, 8 * S], BF16)
        xt2 = mem.sbuf("a1_xt", [128, 2 * D], F32)
        xtm2 = mem.sbuf("a1_tm", [128, 2 * D], BF16)
        psT = [mem.psum("a1_ps0", [128, 512], BF16), mem.psum("a1_ps1", [128, 512], BF16)]
        p = Plan(nc, DMAS)
        rel_xt = [[], []]
        for tb in range(NTB):
            sl = tb % 2
            xt = xt2[:, sl * D:sl * D + D]
            p.op("sync", lambda e, t=tb, d=xt: e.dma_start(out=d, in_=xr[tb0 + t]),
                 waits=rel_xt[sl], inc=(dL[sl].h, 16))
            dsn = dL[sl].bump(16)
            p.op("vector", lambda e, d=xt, t=tb: e.reduce_sum(bc(B_SUM, t), d, axis=AX.X),
                 waits=[(dL[sl].h, dsn)], inc=(svec.h, 1))
            vr = svec.bump()
            p.op("scalar", lambda e, d=xt, t=tb: e.activation(
                scrap[:, (t % 2) * D:(t % 2) * D + D], d, AF.Square,
                accum_out=bc(B_SQ, t)),
                waits=[(dL[sl].h, dsn)], inc=(sact.h, 1))
            an = sact.bump()
            rel_xt[sl] = [(svec.h, vr), (sact.h, an)]
        rn = ln_batch(p, [(svec.h, svec.n), (sact.h, sact.n)])
        rel_xt_a = [list(rel_xt[0]), list(rel_xt[1])]
        rel_xt = [[], []]
        rel_psT = [[], []]
        rel_xtm = [[], []]
        for tb in range(NTB):
            sl = tb % 2
            xt = xt2[:, sl * D:sl * D + D]
            xtm = xtm2[:, sl * D:sl * D + D]
            p.op("sync", lambda e, t=tb, d=xt: e.dma_start(out=d, in_=xr[tb0 + t]),
                 waits=rel_xt[sl] + (rel_xt_a[sl] if tb < 2 else []), inc=(dL[sl].h, 16))
            dsn = dL[sl].bump(16)
            p.op("vector", lambda e, d=xt, m=xtm, t=tb: e.tensor_scalar(
                m, d, bc(B_M, t), bc(B_R, t), ALU.subtract, ALU.mult),
                waits=[(dL[sl].h, dsn), (svec2.h, rn)] + rel_xtm[sl], inc=(svec.h, 1))
            xn = svec.bump()
            rel_xt[sl] = [(svec.h, xn)]
            last_mn = 0
            for g in range(2):
                for c4 in range(4):
                    c = g * 4 + c4
                    w = ([(svec.h, xn)] if c4 == 0 else []) + (rel_psT[g] if c4 == 0 else [])
                    p.op("tensor", lambda e, c=c, c4=c4, m=xtm, g=g: e.transpose(
                        psT[g][:, c4 * 128:c4 * 128 + 128],
                        m[:, c * 128:c * 128 + 128], identb[:]),
                        waits=w, inc=((smm.h, 1) if c4 == 3 else None))
                last_mn = smm.bump()
                dv = xln0[:].rearrange("p (c t) -> p c t", c=8)
                p.op("vector", lambda e, g=g, t=tb, dv=dv: e.tensor_copy(
                    dv[:, g * 4:(g + 1) * 4, t * 128:t * 128 + 128], psT[g][:]),
                    waits=[(smm.h, last_mn)], inc=(svec2.h, 1))
                rel_psT[g] = [(svec2.h, svec2.bump())]
            rel_xtm[sl] = [(smm.h, last_mn)]
        p.emit()
        if seq == 0 and DEBUG_DUMP == "xln0":
            emit_dump(xln0, 8 * S)
        mem.free("a1_xt", "a1_tm", "a1_ps0", "a1_ps1")

        Qb = mem.sbuf(f"Qb_{seq}", [128, 8 * S], BF16, side="right")
        Kb = mem.sbuf(f"Kb_{seq}", [128, 8 * S], BF16, side="right")
        Va = mem.sbuf(f"Va_{seq}", [128, 16 * 1040], BF16, side="right")
        ws2 = mem.sbuf("a2_ws", [128, 2 * 8 * 1024], BF16)
        tm1b = mem.sbuf("a2_t1", [128, 3 * 512], BF16)
        tm2b = mem.sbuf("a2_t2", [128, 3 * 512], BF16)
        pa = [mem.psum("a2_pa0", [128, 512], F32), mem.psum("a2_pa1", [128, 512], F32)]
        pb = [mem.psum("a2_pb0", [128, 512], F32), mem.psum("a2_pb1", [128, 512], F32)]
        p = Plan(nc, DMAS)
        wsem = [dG, dG2, dG3]
        wn = {}
        rel_wstage = [[], []]

        def load_w(wi, nm):
            sem = wsem[wi]
            wst = ws2[:, (wi % 2) * 8192:(wi % 2) * 8192 + 8192]
            for kc in range(8):
                p.op("gpsimd", lambda e, k=kc, nm=nm, w=wst: e.dma_start(
                    out=w[:, k * 1024:(k + 1) * 1024],
                    in_=din[nm][k * 128:(k + 1) * 128, :]),
                    waits=((rel_ws + rel_wstage[wi % 2]) if kc == 0 else []),
                    inc=(sem.h, 16))
                sem.bump(16)
            wn[wi] = (sem.h, sem.n)

        load_w(0, "wqT")
        load_w(1, "wkT")
        for wi, buf in ((0, Qb), (1, Kb)):
            ws = ws2[:, (wi % 2) * 8192:(wi % 2) * 8192 + 8192]
            rel_pa = [[], []]
            rel_pb = [[], []]
            ti = 0
            pend = []

            def emit_rot(item, ws=ws):
                tj, dstj, ttj, anj = item
                tm1 = tm1b[:, (tj % 3) * 512:(tj % 3) * 512 + 512]
                tm2 = tm2b[:, (tj % 3) * 512:(tj % 3) * 512 + 512]
                pbs = pb[tj % 2]
                p.op("tensor", lambda e, d=dstj, pbs=pbs: e.matmul(pbs[:], pswapb[:], d,
                                                                   start=True, stop=True),
                     waits=[(sact.h, anj)] + rel_pb[tj % 2], inc=(smm2.h, 1))
                m2 = smm2.bump()
                p.op("vector", lambda e, t=ttj, pbs=pbs, tm2=tm2: e.tensor_tensor(
                    tm2, pbs[:], sinf[:, t * 512:t * 512 + 512], ALU.mult),
                    waits=[(smm2.h, m2)], inc=(svec.h, 1))
                vt2 = svec.bump()
                rel_pb[tj % 2] = [(svec.h, vt2)]
                p.op("vector", lambda e, d=dstj, tm1=tm1, tm2=tm2: e.tensor_tensor(
                    d, tm1, tm2, ALU.add),
                    waits=[(svec.h, vt2)], inc=(svec.h, 1))
                svec.bump()

            for oc in range(8):
                for tt in range(4):
                    dst = buf[:, oc * S + tt * 512:oc * S + tt * 512 + 512]
                    ps = pa[ti % 2]
                    w0 = [wn[wi]] + rel_pa[ti % 2]
                    for kc in range(8):
                        p.op("tensor", lambda e, k=kc, o=oc, t=tt, ps=ps, ws=ws: e.matmul(
                            ps[:], ws[:, k * 1024 + o * 128:k * 1024 + o * 128 + 128],
                            xln0[:, k * S + t * 512:k * S + t * 512 + 512],
                            start=(k == 0), stop=(k == 7)),
                            waits=(w0 if kc == 0 else []),
                            inc=((smm.h, 1) if kc == 7 else None))
                    mn = smm.bump()
                    p.op("scalar", lambda e, d=dst, ps=ps: e.activation(d, ps[:], AF.Identity),
                         waits=[(smm.h, mn)], inc=(sact.h, 1))
                    an = sact.bump()
                    tm1 = tm1b[:, (ti % 3) * 512:(ti % 3) * 512 + 512]
                    p.op("vector", lambda e, t=tt, ps=ps, tm1=tm1: e.tensor_tensor(
                        tm1, ps[:], cosf[:, t * 512:t * 512 + 512], ALU.mult),
                        waits=[(sact.h, an)], inc=(svec.h, 1))
                    v1 = svec.bump()
                    rel_pa[ti % 2] = [(sact.h, an), (svec.h, v1)]
                    pend.append((ti, dst, tt, an))
                    if len(pend) > 1:
                        emit_rot(pend.pop(0))
                    ti += 1
            while pend:
                emit_rot(pend.pop(0))
            rel_wstage[wi % 2] = [(smm.h, smm.n)]
            if wi == 0:
                load_w(2, "wvT")

        ws = ws2[:, 0:8192]
        p.op("vector", lambda e: e.memset(Va[:, 64::65], 1.0))
        vav = Va.ap().rearrange("p (tb h c) -> p tb h c", tb=16, h=16, c=65)
        rel_pa = [[], []]
        ti = 0
        for tb in range(NTB):
            for ot in range(2):
                ps = pa[ti % 2]
                w0 = [wn[2]] + rel_pa[ti % 2]
                for kc in range(8):
                    p.op("tensor", lambda e, k=kc, t=tb, o=ot, ps=ps, ws=ws: e.matmul(
                        ps[:], xln0[:, k * S + t * 128:t * 128 + k * S + 128],
                        ws[:, k * 1024 + o * 512:k * 1024 + o * 512 + 512],
                        start=(k == 0), stop=(k == 7)),
                        waits=(w0 if kc == 0 else []),
                        inc=((smm.h, 1) if kc == 7 else None))
                mn = smm.bump()
                p.op("vector", lambda e, t=tb, o=ot, ps=ps: e.tensor_copy(
                    vav[:, t, o * 8:(o + 1) * 8, 0:64],
                    ps.rearrange("p (h c) -> p h c", h=8)),
                    waits=[(smm.h, mn)], inc=(svec.h, 1))
                rel_pa[ti % 2] = [(svec.h, svec.bump())]
                ti += 1
        rel_ws = [(smm.h, smm.n)]
        p.emit()
        if seq == 0 and DEBUG_DUMP in ("Qb", "Kb", "Va"):
            emit_dump({"Qb": Qb, "Kb": Kb, "Va": Va}[DEBUG_DUMP],
                      8 * S if DEBUG_DUMP != "Va" else 16 * 1040)
        mem.free(f"xln0_{seq}", "a2_ws", "a2_t1", "a2_t2", "a2_pa0", "a2_pa1",
                 "a2_pb0", "a2_pb1")

        hctm = mem.sbuf(f"hctm_{seq}", [128, 16 * 1024], BF16)
        eb = mem.sbuf("a3_e", [128, 3 * S], BF16)
        scs = [mem.psum("a3_sc0", [128, 1024], F32), mem.psum("a3_sc1", [128, 1024], F32)]
        pvs = [mem.psum("a3_pv0", [128, 65], F32), mem.psum("a3_pv1", [128, 65], F32)]
        p = Plan(nc, DMAS)
        iters = [(h, qb) for h in range(H) for qb in range(NTB)]
        NIT = len(iters)
        rel_sc = [[], []]
        rel_pv = [[], []]
        rel_e = [[], [], []]
        mg = {}
        ag = {}
        vm = {}
        g0_of = {}
        gctr = [0]

        def grp_of(qb):
            return [list(range(g, min(g + 8, qb + 1))) for g in range(0, qb + 1, 8)]

        def plan_scores(it):
            h, qb = iters[it]
            hp, hr = h // 2, (h % 2) * 64
            q0 = hp * S + qb * 128
            g0_of[it] = gctr[0]
            mg[it] = []
            for kbg in grp_of(qb):
                ms = scs[gctr[0] % 2]
                w0 = list(rel_sc[gctr[0] % 2])
                for j, kb in enumerate(kbg):
                    p.op("tensor", lambda e, kb=kb, j=j, ms=ms, hr=hr, hp=hp, q0=q0: e.matmul(
                        ms[:, j * 128:j * 128 + 128],
                        Kb[hr:hr + 64, hp * S + kb * 128:hp * S + kb * 128 + 128],
                        Qb[hr:hr + 64, q0:q0 + 128],
                        start=True, stop=True),
                        waits=(w0 if j == 0 else []),
                        inc=((smm.h, 1) if j == len(kbg) - 1 else None))
                mg[it].append(smm.bump())
                gctr[0] += 1

        def plan_exps(it):
            h, qb = iters[it]
            e0 = (it % 3) * S
            groups = grp_of(qb)
            for gi, kbg in enumerate(groups):
                ln = len(kbg)
                ms = scs[(g0_of[it] + gi) % 2]
                base = e0 + kbg[0] * 128
                wts = [(smm.h, mg[it][gi])] + (list(rel_e[it % 3]) if gi == 0 else [])
                p.op("scalar", lambda e, b=base, ln=ln, ms=ms: e.activation(
                    eb[:, b:b + ln * 128], ms[:, 0:ln * 128],
                    AF.Exp, bias=cc(C_NEGC), scale=cc(C_S8)),
                    waits=wts, inc=(sact.h, 1))
                rel_sc[(g0_of[it] + gi) % 2] = [(sact.h, sact.bump())]
            ag[it] = sact.n

        def plan_mask(it):
            h, qb = iters[it]
            e0 = (it % 3) * S
            p.op("vector", lambda e, qb=qb, e0=e0: e.tensor_tensor(
                eb[:, e0 + qb * 128:e0 + qb * 128 + 128],
                eb[:, e0 + qb * 128:e0 + qb * 128 + 128], maskb[:], ALU.mult),
                waits=[(sact.h, ag[it])], inc=(svec.h, 1))
            vm[it] = svec.bump()

        def plan_pv(it):
            h, qb = iters[it]
            e0 = (it % 3) * S
            pv = pvs[it % 2]
            w0 = [(sact.h, ag[it])] + rel_pv[it % 2]
            for kb in range(qb + 1):
                wts = (w0 if kb == 0 else []) + ([(svec.h, vm[it])] if kb == qb else [])
                p.op("tensor", lambda e, kb=kb, pv=pv, h=h, qb=qb, e0=e0: e.matmul(
                    pv, eb[:, e0 + kb * 128:e0 + kb * 128 + 128], vav[:, kb, h, :],
                    start=(kb == 0), stop=(kb == qb)),
                    waits=wts, inc=((smm2.h, 1) if kb == qb else None))
            rel_e[it % 3] = [(smm2.h, smm2.bump())]

        def plan_evac(it):
            h, qb = iters[it]
            pv = pvs[it % 2]
            p.op("vector", lambda e, pv=pv: e.reciprocal(st(ST_RZ), pv[:, 64:65]),
                 waits=[(smm2.h, smm2.n)], inc=(svec2.h, 1))
            rzn = svec2.bump()
            p.op("vector", lambda e, pv=pv, qb=qb, h=h: e.tensor_scalar(
                hctm[:, qb * 1024 + h * 64:qb * 1024 + h * 64 + 64],
                pv[:, 0:64], st(ST_RZ), None, ALU.mult),
                waits=[(svec2.h, rzn)], inc=(svec2.h, 1))
            rel_pv[it % 2] = [(svec2.h, svec2.bump())]

        plan_scores(0)
        for it in range(NIT):
            plan_exps(it)
            plan_mask(it)
            if it + 1 < NIT:
                plan_scores(it + 1)
            plan_pv(it)
            plan_evac(it)
        p.emit()
        if seq == 0 and DEBUG_DUMP == "hctm":
            emit_dump(hctm, 16 * 1024)
        mem.free(f"Qb_{seq}", f"Kb_{seq}", f"Va_{seq}", "a3_e",
                 "a3_sc0", "a3_sc1", "a3_pv0", "a3_pv1")

        hcf = mem.sbuf(f"hcf_{seq}", [128, 8 * S], BF16, side="right")
        psT = [mem.psum("a4_ps0", [128, 512], BF16), mem.psum("a4_ps1", [128, 512], BF16)]
        p = Plan(nc, DMAS)
        rel_psT = [[], []]
        for tb in range(NTB):
            for g in range(2):
                for c4 in range(4):
                    c = g * 4 + c4
                    p.op("tensor", lambda e, tb=tb, c=c, c4=c4, g=g: e.transpose(
                        psT[g][:, c4 * 128:c4 * 128 + 128],
                        hctm[:, tb * 1024 + c * 128:tb * 1024 + c * 128 + 128],
                        identb[:]),
                        waits=(rel_psT[g] if c4 == 0 else []),
                        inc=((smm.h, 1) if c4 == 3 else None))
                mn = smm.bump()
                dv = hcf[:].rearrange("p (c t) -> p c t", c=8)
                p.op("vector", lambda e, g=g, tb=tb, dv=dv: e.tensor_copy(
                    dv[:, g * 4:(g + 1) * 4, tb * 128:tb * 128 + 128], psT[g][:]),
                    waits=[(smm.h, mn)], inc=(svec.h, 1))
                rel_psT[g] = [(svec.h, svec.bump())]
        p.emit()
        if seq == 0 and DEBUG_DUMP == "hcf":
            emit_dump(hcf, 8 * S)
        mem.free(f"hctm_{seq}", "a4_ps0", "a4_ps1")

        ws = mem.sbuf("a5_ws", [128, 8 * 1024], BF16)
        aof = mem.sbuf("a5_ao", [128, 16 * 1024], F32)
        xt2 = mem.sbuf("a5_xt", [128, 2 * D], F32)
        x1t2 = mem.sbuf("a5_x1", [128, 2 * D], F32)
        xadj2 = mem.sbuf("a5_xa", [128, 2 * D], F32)
        xtm2 = mem.sbuf("a5_tm", [128, 2 * D], BF16)
        xstg2 = mem.sbuf("a5_st", [128, 2 * D], BF16)
        pOs = [mem.psum("a5_p0", [128, 512], F32), mem.psum("a5_p1", [128, 512], F32),
               mem.psum("a5_p2", [128, 512], F32), mem.psum("a5_p3", [128, 512], F32)]
        psT = [mem.psum("a5_t0", [128, 512], BF16), mem.psum("a5_t1", [128, 512], BF16)]
        p = Plan(nc, DMAS)
        for kc in range(8):
            p.op("gpsimd", lambda e, k=kc: e.dma_start(
                out=ws[:, k * 1024:(k + 1) * 1024],
                in_=din["woT"][k * 128:(k + 1) * 128, :]),
                waits=(rel_ws if kc == 0 else []), inc=(dG.h, 16))
            dG.bump(16)
        dGn = dG.n
        rel_pO = [[], [], [], []]
        pi = 0
        for tb in range(NTB):
            aos = aof[:, tb * 1024:(tb + 1) * 1024]
            for ot in range(2):
                ps = pOs[pi % 4]
                w0 = [(dG.h, dGn)] + rel_pO[pi % 4]
                for dc in range(8):
                    p.op("tensor", lambda e, d=dc, t=tb, o=ot, ps=ps: e.matmul(
                        ps[:], hcf[:, d * S + t * 128:d * S + t * 128 + 128],
                        ws[:, d * 1024 + o * 512:d * 1024 + o * 512 + 512],
                        start=(d == 0), stop=(d == 7)),
                        waits=(w0 if dc == 0 else []),
                        inc=((smm.h, 1) if dc == 7 else None))
                mn = smm.bump()
                p.op("scalar", lambda e, o=ot, d=aos, ps=ps: e.activation(
                    d[:, o * 512:o * 512 + 512], ps[:], AF.Identity),
                    waits=[(smm.h, mn)], inc=(sact.h, 1))
                rel_pO[pi % 4] = [(sact.h, sact.bump())]
                pi += 1
            evn = sact.n
            p.op("vector", lambda e, d=aos, t=tb: e.reduce_sum(bc(B_SUM, t), d, axis=AX.X),
                 waits=[(sact.h, evn)], inc=(svec.h, 1))
            svec.bump()
            p.op("scalar", lambda e, d=aos, t=tb: e.activation(
                scrap[:, (t % 2) * D:(t % 2) * D + D], d, AF.Square,
                accum_out=bc(B_SQ, t)),
                waits=[(sact.h, evn)], inc=(sact.h, 1))
            sact.bump()
        ln_batch(p, [(svec.h, svec.n), (sact.h, sact.n)], with_amr=True)
        rn = svec2.n
        rel_xt = [[], []]
        rel_x1t = [[], []]
        for tb in range(NTB):
            sl = tb % 2
            xt = xt2[:, sl * D:sl * D + D]
            x1t = x1t2[:, sl * D:sl * D + D]
            xadj = xadj2[:, sl * D:sl * D + D]
            aos = aof[:, tb * 1024:(tb + 1) * 1024]
            p.op("sync", lambda e, t=tb, d=xt: e.dma_start(out=d, in_=xr[tb0 + t]),
                 waits=rel_xt[sl], inc=(dL[sl].h, 16))
            dsx = dL[sl].bump(16)
            p.op("vector", lambda e, d=xt, a=xadj, t=tb: e.tensor_scalar(
                a, d, bc(B_AMR, t), None, ALU.add),
                waits=[(dL[sl].h, dsx), (svec2.h, rn)], inc=(svec.h, 1))
            xan = svec.bump()
            rel_xt[sl] = [(svec.h, xan)]
            p.op("vector", lambda e, d=x1t, a=xadj, s=aos, t=tb: e.scalar_tensor_tensor(
                d, s, bc(B_R, t), a, ALU.mult, ALU.add),
                waits=[(svec.h, xan)] + rel_x1t[sl], inc=(svec.h, 1))
            vstt = svec.bump()
            p.op("vector", lambda e, d=x1t, t=tb: e.reduce_sum(bc(B_SUM, t), d, axis=AX.X),
                 waits=[(svec.h, vstt)], inc=(svec.h, 1))
            vr1 = svec.bump()
            p.op("scalar", lambda e, d=x1t, t=tb: e.activation(
                scrap[:, (t % 2) * D:(t % 2) * D + D], d, AF.Square,
                accum_out=bc(B_SQ, t)),
                waits=[(svec.h, vstt)], inc=(sact.h, 1))
            an1 = sact.bump()
            p.op("gpsimd", lambda e, t=tb, d=x1t: e.dma_start(out=x1r[tb0 + t], in_=d),
                 waits=[(svec.h, vstt)], inc=(dSt[sl].h, 16))
            dSt[sl].bump(16)
            rel_x1t[sl] = [(dSt[sl].h, dSt[sl].n), (vr1 and svec.h, vr1), (sact.h, an1)]
        ln_batch(p, [(svec.h, svec.n), (sact.h, sact.n)])
        rn1 = svec2.n
        stc = [(dSt[0].h, dSt[0].n), (dSt[1].h, dSt[1].n)]
        rel_x1t2 = [[], []]
        rel_psT = [[], []]
        rel_stg = [[], []]
        rel_xtm = [[], []]
        for tb in range(NTB):
            sl = tb % 2
            x1t = x1t2[:, sl * D:sl * D + D]
            xtm = xtm2[:, sl * D:sl * D + D]
            xstg = xstg2[:, sl * D:sl * D + D]
            p.op("sync", lambda e, t=tb, d=x1t: e.dma_start(out=d, in_=x1r[tb0 + t]),
                 waits=rel_x1t2[sl] + (rel_x1t[sl] + stc if tb < 2 else []),
                 inc=(dL[sl].h, 16))
            dsx = dL[sl].bump(16)
            p.op("vector", lambda e, d=x1t, m=xtm, t=tb: e.tensor_scalar(
                m, d, bc(B_M, t), bc(B_R, t), ALU.subtract, ALU.mult),
                waits=[(dL[sl].h, dsx), (svec2.h, rn1)] + rel_xtm[sl], inc=(svec.h, 1))
            xn = svec.bump()
            rel_x1t2[sl] = [(svec.h, xn)]
            last_mn = 0
            for g in range(2):
                for c4 in range(4):
                    c = g * 4 + c4
                    w = ([(svec.h, xn)] if c4 == 0 else []) + (rel_psT[g] if c4 == 0 else [])
                    p.op("tensor", lambda e, c=c, c4=c4, m=xtm, g=g: e.transpose(
                        psT[g][:, c4 * 128:c4 * 128 + 128],
                        m[:, c * 128:c * 128 + 128], identb[:]),
                        waits=w, inc=((smm.h, 1) if c4 == 3 else None))
                last_mn = smm.bump()
                p.op("vector", lambda e, g=g, d=xstg: e.tensor_copy(
                    d[:, g * 512:g * 512 + 512], psT[g][:]),
                    waits=[(smm.h, last_mn)] + (rel_stg[sl] if g == 0 else []),
                    inc=(svec2.h, 1))
                rel_psT[g] = [(svec2.h, svec2.bump())]
            rel_xtm[sl] = [(smm.h, last_mn)]
            col0 = seq * S + tb * 128
            p.op("gpsimd", lambda e, d=xstg, c0=col0: e.dma_start(
                out=xln1d[:, :, c0:c0 + 128].transpose([1, 0, 2]),
                in_=d.rearrange("p (c f) -> p c f", c=8)),
                waits=[(svec2.h, svec2.n)], inc=(dSt[sl].h, 16))
            dSt[sl].bump(16)
            rel_stg[sl] = [(dSt[sl].h, dSt[sl].n)]
        p.emit()
        rel_ws = [(smm.h, smm.n)]
        mem.free(f"hcf_{seq}", "a5_ws", "a5_ao", "a5_xt", "a5_x1", "a5_xa", "a5_tm",
                 "a5_st", "a5_p0", "a5_p1", "a5_p2", "a5_p3", "a5_t0", "a5_t1")

    mem.free("cosf", "sinf", "cosb")

    w1s = mem.sbuf("w1s", [128, 8 * DH], BF16)
    w2s = mem.sbuf("w2s", [128, 8 * DH], BF16)
    xt1 = mem.sbuf("f1_x", [128, 2 * 8 * 512], BF16)
    t1f = mem.sbuf("f1_t1", [128, 2 * 512], F32)
    hstg = mem.sbuf("f1_h", [128, 2 * 512], BF16)
    pa = [mem.psum("f1_pa0", [128, 512], F32), mem.psum("f1_pa1", [128, 512], F32)]
    pb = [mem.psum("f1_pb0", [128, 512], F32), mem.psum("f1_pb1", [128, 512], F32)]
    p = Plan(nc, DMAS)
    grp_sem = {0: dG, 1: dG2, 2: dG3, 3: dSt[0]}
    wgate = {}
    for g in range(4):
        nm = "w1T" if g < 2 else "w2T"
        dstb = w1s if g < 2 else w2s
        sem = grp_sem[g]
        for j in range(4):
            kc = (g % 2) * 4 + j
            p.op("gpsimd", lambda e, k=kc, dstb=dstb, nm=nm: e.dma_start(
                out=dstb[:, k * DH:(k + 1) * DH], in_=din[nm][k * 128:(k + 1) * 128, :]),
                waits=(rel_ws if g == 0 and j == 0 else []), inc=(sem.h, 16))
            sem.bump(16)
        wgate[("w1" if g < 2 else "w2", (g % 2) * 4)] = (sem.h, sem.n)
    rel_x = [[], []]
    rel_pa = [[], []]
    rel_pb = [[], []]
    rel_t1 = [[], []]
    rel_h = [[], []]
    ti = 0
    for tt in range(8):
        sl = tt % 2
        xv = xt1[:, sl * 4096:sl * 4096 + 4096]
        p.op("sync", lambda e, t=tt, d=xv: e.dma_start(
            out=d.rearrange("p (k f) -> p k f", k=8),
            in_=xln1d[:, :, t * 512:t * 512 + 512].transpose([1, 0, 2])),
            waits=rel_x[sl], inc=(dL[sl].h, 16))
        dsn = dL[sl].bump(16)
        for oc in range(32):
            psl = ti % 2
            w0 = [(dL[sl].h, dsn)] + rel_pa[psl]
            for kc in range(8):
                wts = (w0 if kc == 0 else []) + (
                    [wgate[("w1", kc)]] if ti == 0 and kc in (0, 4) else [])
                p.op("tensor", lambda e, k=kc, o=oc, d=xv, ps=pa[psl]: e.matmul(
                    ps[:], w1s[:, k * DH + o * 128:k * DH + o * 128 + 128],
                    d[:, k * 512:k * 512 + 512], start=(k == 0), stop=(k == 7)),
                    waits=wts, inc=((smm.h, 1) if kc == 7 else None))
            mn1 = smm.bump()
            w0 = rel_pb[psl]
            for kc in range(8):
                wts = (w0 if kc == 0 else []) + (
                    [wgate[("w2", kc)]] if ti == 0 and kc in (0, 4) else [])
                p.op("tensor", lambda e, k=kc, o=oc, d=xv, ps=pb[psl]: e.matmul(
                    ps[:], w2s[:, k * DH + o * 128:k * DH + o * 128 + 128],
                    d[:, k * 512:k * 512 + 512], start=(k == 0), stop=(k == 7)),
                    waits=wts, inc=((smm.h, 1) if kc == 7 else None))
            mn2 = smm.bump()
            p.op("scalar", lambda e, o=oc, ps=pa[psl], d=t1f[:, psl * 512:psl * 512 + 512]:
                 e.activation(d, ps[:], AF.Silu, scale=s1s[:, o:o + 1]),
                 waits=[(smm.h, mn1)] + rel_t1[psl], inc=(sact.h, 1))
            an = sact.bump()
            rel_pa[psl] = [(sact.h, an)]
            p.op("vector", lambda e, o=oc, ps=pb[psl], t1=t1f[:, psl * 512:psl * 512 + 512],
                 d=hstg[:, psl * 512:psl * 512 + 512]:
                 e.scalar_tensor_tensor(d, ps[:], s2s[:, o:o + 1], t1, ALU.mult, ALU.mult),
                 waits=[(smm.h, mn2), (sact.h, an)] + rel_h[psl], inc=(svec.h, 1))
            vn = svec.bump()
            rel_pb[psl] = [(svec.h, vn)]
            rel_t1[psl] = [(svec.h, vn)]
            p.op("gpsimd", lambda e, t=tt, o=oc, d=hstg[:, psl * 512:psl * 512 + 512]:
                 e.dma_start(out=hd[t, o], in_=d), waits=[(svec.h, vn)], inc=(dSt[psl].h, 16))
            dSt[psl].bump(16)
            rel_h[psl] = [(dSt[psl].h, dSt[psl].n)]
            ti += 1
        rel_x[sl] = [(smm.h, smm.n)]
    p.emit()
    mem.free("w1s", "w2s", "f1_x", "f1_t1", "f1_h", "f1_pa0", "f1_pa1", "f1_pb0", "f1_pb1")

    for phase, (wnm, src_d, dst_d, hss, dst_kind) in enumerate(
            [("h1T", hd, g1d, hs1s, "tt"), ("h2T", g1d, g2d, hs2s, "tb")]):
        wg = mem.sbuf("f2_wg", [128, 48 * 1024], BF16)
        ht = mem.sbuf("f2_ht", [128, 2 * 32 * 512], BF16)
        gstg = mem.sbuf("f2_g", [128, 16 * 512], BF16)
        ps2 = [mem.psum("f2_p0", [128, 512], F32), mem.psum("f2_p1", [128, 512], F32)]
        p = Plan(nc, DMAS)
        half_sem = {}
        rel_half = {}
        grp_end = {}

        def load_half(half):
            slot = half % 3
            sem = [dG, dG2, dG3][slot]
            g, off = half // 2, (half % 2) * 16
            w = rel_half.get(slot, [])
            for j in range(16):
                kc = off + j
                p.op("gpsimd", lambda e, k=kc, g=g, j=j, slot=slot, nm=wnm: e.dma_start(
                    out=wg[:, (slot * 16 + j) * 1024:(slot * 16 + j) * 1024 + 1024],
                    in_=din[nm][k * 128:(k + 1) * 128, g * 1024:(g + 1) * 1024]),
                    waits=(w if j == 0 else []), inc=(sem.h, 16))
                sem.bump(16)
            half_sem[half] = (sem.h, sem.n)

        def wslice(g, kc):
            half = 2 * g + (1 if kc >= 16 else 0)
            slot = half % 3
            j = kc % 16
            c0 = (slot * 16 + j) * 1024
            return c0

        load_half(0)
        load_half(1)
        rel_ht = [[], []]
        rel_ps = [[], []]
        rel_g = [[], []]
        ht_tile = [-1, -1]
        ht_dsn = [0, 0]
        ti = 0
        for ocg in range(4):
            if ocg >= 1:
                rel_half[(2 * ocg + 1) % 3] = [(smm.h, grp_end[ocg - 1])]
                load_half(2 * ocg + 1)
            if ocg + 1 < 4:
                rel_half[(2 * ocg + 2) % 3] = ([(smm.h, grp_end[ocg - 1])]
                                               if ocg >= 1 else [])
                load_half(2 * ocg + 2)
            tts = list(range(8)) if ocg % 2 == 0 else list(range(7, -1, -1))
            for tt in tts:
                sl = tt % 2
                hv = ht[:, sl * 16384:sl * 16384 + 16384]
                if ht_tile[sl] != tt:
                    for q in range(4):
                        p.op("sync", lambda e, t=tt, q=q, d=hv, sd=src_d: e.dma_start(
                            out=d.rearrange("p (k f) -> p k f", k=32)[:, q * 8:(q + 1) * 8, :],
                            in_=sd[t, q * 8:(q + 1) * 8].transpose([1, 0, 2])),
                            waits=(rel_ht[sl] if q == 0 else []), inc=(dL[sl].h, 16))
                        dL[sl].bump(16)
                    ht_tile[sl] = tt
                    ht_dsn[sl] = dL[sl].n
                dsn = ht_dsn[sl]
                an = 0
                for oc in range(8):
                    psl = ti % 2
                    w0 = [half_sem[2 * ocg], (dL[sl].h, dsn)] + rel_ps[psl]
                    for kc in range(32):
                        wts = w0 if kc == 0 else ([half_sem[2 * ocg + 1]] if kc == 16 else [])
                        p.op("tensor", lambda e, k=kc, o=oc, d=hv, ps=ps2[psl],
                             c0=wslice(ocg, kc): e.matmul(
                                 ps[:], wg[:, c0 + o * 128:c0 + o * 128 + 128],
                                 d[:, k * 512:k * 512 + 512],
                                 start=(k == 0), stop=(k == 31)),
                             waits=wts, inc=((smm.h, 1) if kc == 31 else None))
                    mn = smm.bump()
                    gsl = gstg[:, (sl * 8 + oc) * 512:(sl * 8 + oc) * 512 + 512]
                    p.op("scalar", lambda e, o=oc, g=ocg, ps=ps2[psl], d=gsl, hss=hss:
                         e.activation(d, ps[:], AF.Gelu, scale=hss[:, g * 8 + o:g * 8 + o + 1]),
                         waits=[(smm.h, mn)] + (rel_g[sl] if oc == 0 else []),
                         inc=(sact.h, 1))
                    an = sact.bump()
                    rel_ps[psl] = [(sact.h, an)]
                    ti += 1
                for oc in range(8):
                    gsl = gstg[:, (sl * 8 + oc) * 512:(sl * 8 + oc) * 512 + 512]
                    if dst_kind == "tt":
                        p.op("gpsimd", lambda e, t=tt, o=ocg * 8 + oc, d=gsl, dd=dst_d:
                             e.dma_start(out=dd[t, o], in_=d),
                             waits=([(sact.h, an)] if oc == 0 else []), inc=(dSt[sl].h, 16))
                    else:
                        p.op("gpsimd", lambda e, t=tt, o=ocg * 8 + oc, d=gsl, dd=dst_d:
                             e.dma_start(out=dd[t * 4:(t + 1) * 4, o].transpose([1, 0, 2]),
                                         in_=d.rearrange("p (b f) -> p b f", b=4)),
                             waits=([(sact.h, an)] if oc == 0 else []), inc=(dSt[sl].h, 16))
                    dSt[sl].bump(16)
                rel_g[sl] = [(dSt[sl].h, dSt[sl].n)]
                rel_ht[sl] = [(smm.h, smm.n)]
            grp_end[ocg] = smm.n
        p.emit()
        mem.free("f2_wg", "f2_ht", "f2_g", "f2_p0", "f2_p1")

    w3s = mem.sbuf("w3s", [128, 32 * 1024], BF16)
    w3raw = mem.sbuf("f4_wr", [128, 2 * 1024], BF16)
    s3f = mem.sbuf("f4_s3f", [1, 1024], F32)
    s3b = mem.sbuf("f4_s3b", [1, 1024], BF16)
    ones1 = mem.sbuf("f4_ones", [1, 128], BF16)
    s3bc = mem.sbuf("f4_s3bc", [128, 1024], BF16)
    g2t = mem.sbuf("f4_g2", [128, 2 * 4096], BF16)
    x1t2 = mem.sbuf("f4_x1", [128, 2 * D], F32)
    xadj2 = mem.sbuf("f4_xa", [128, 2 * D], F32)
    outt2 = mem.sbuf("f4_o", [128, 2 * D], F32)
    pO = [mem.psum("f4_p0", [128, 512], F32), mem.psum("f4_p1", [128, 512], F32)]
    p = Plan(nc, DMAS)
    p.op("sync", lambda e: e.dma_start(out=s3f[:], in_=din["s3row"]), inc=(dL[0].h, 16))
    dsn = dL[0].bump(16)
    p.op("vector", lambda e: e.tensor_copy(s3b[:], s3f[:]), waits=[(dL[0].h, dsn)])
    p.op("vector", lambda e: e.memset(ones1[:], 1.0), inc=(svec.h, 1))
    vn0 = svec.bump()
    for o in range(2):
        p.op("tensor", lambda e, o=o: e.matmul(pO[o][:], ones1[:],
                                               s3b[0:1, o * 512:o * 512 + 512],
                                               start=True, stop=True),
             waits=[(svec.h, vn0)], inc=(smm.h, 1))
        mn = smm.bump()
        p.op("vector", lambda e, o=o: e.tensor_copy(s3bc[:, o * 512:o * 512 + 512], pO[o][:]),
             waits=[(smm.h, mn)], inc=(svec.h, 1))
        svec.bump()
    vbc = svec.n
    rel_wr = [[], []]
    wsem = [dG, dG2, dG3, dG]
    vw3b = []
    for b in range(4):
        sem = wsem[b]
        for j in range(8):
            c = b * 8 + j
            sl = c % 2
            p.op("gpsimd", lambda e, c=c, d=w3raw[:, (c % 2) * 1024:(c % 2) * 1024 + 1024]:
                 e.dma_start(out=d, in_=din["w3T"][c * 128:(c + 1) * 128, :]),
                 waits=rel_wr[sl], inc=(sem.h, 16))
            dgn = sem.bump(16)
            p.op("vector", lambda e, c=c, s=w3raw[:, (c % 2) * 1024:(c % 2) * 1024 + 1024]:
                 e.tensor_tensor(w3s[:, c * 1024:(c + 1) * 1024], s, s3bc[:], ALU.mult),
                 waits=[(sem.h, dgn), (svec.h, vbc)], inc=(svec.h, 1))
            rel_wr[sl] = [(svec.h, svec.bump())]
        vw3b.append(svec.n)
    vw3 = svec.n
    ff = mem.sbuf("f4_ff", [128, 16 * 1024], F32)
    rel_g2 = [[], []]
    rel_x1 = [[], []]
    rel_pO = [[], [], [], []]
    rel_out = [[], []]
    pOs = [pO[0], pO[1], mem.psum("f4_p2", [128, 512], F32),
           mem.psum("f4_p3", [128, 512], F32)]
    pi = 0
    rel_ff = [[], []]
    for half in range(4):
        hb = half * 8
        fo = (half % 2) * 8
        for tbl in range(8):
            tb = hb + tbl
            sl = tb % 2
            gv = g2t[:, sl * 4096:sl * 4096 + 4096]
            ffs = ff[:, (fo + tbl) * 1024:(fo + tbl) * 1024 + 1024]
            for q in range(4):
                p.op("sync", lambda e, t=tb, q=q, d=gv: e.dma_start(
                    out=d.rearrange("p (k f) -> p k f", k=32)[:, q * 8:(q + 1) * 8, :],
                    in_=g2d[t, q * 8:(q + 1) * 8].transpose([1, 0, 2])),
                    waits=(rel_g2[sl] if q == 0 else []), inc=(dL[sl].h, 16))
                dL[sl].bump(16)
            dsn = dL[sl].n
            for ot in range(2):
                ps = pOs[pi % 4]
                w0 = [(dL[sl].h, dsn), (svec.h, vw3b[0])] + rel_pO[pi % 4]
                for dc in range(32):
                    wts = (w0 if dc == 0 else
                           ([(svec.h, vw3b[dc // 8])] if dc % 8 == 0 else []))
                    p.op("tensor", lambda e, d=dc, o=ot, g=gv, ps=ps: e.matmul(
                        ps[:], g[:, d * 128:d * 128 + 128],
                        w3s[:, d * 1024 + o * 512:d * 1024 + o * 512 + 512],
                        start=(d == 0), stop=(d == 31)),
                        waits=wts, inc=((smm.h, 1) if dc == 31 else None))
                mn = smm.bump()
                p.op("scalar", lambda e, o=ot, d=ffs, ps=ps: e.activation(
                    d[:, o * 512:o * 512 + 512], ps[:], AF.Identity),
                    waits=[(smm.h, mn)] + (rel_ff[half % 2] if tbl == 0 and ot == 0 else []),
                    inc=(sact.h, 1))
                rel_pO[pi % 4] = [(sact.h, sact.bump())]
                pi += 1
            rel_g2[sl] = [(smm.h, smm.n)]
            evn = sact.n
            p.op("vector", lambda e, d=ffs, t=fo + tbl: e.reduce_sum(
                bc(B_SUM, t), d, axis=AX.X),
                waits=[(sact.h, evn)], inc=(svec.h, 1))
            svec.bump()
            p.op("scalar", lambda e, d=ffs, t=fo + tbl: e.activation(
                scrap[:, (t % 2) * D:(t % 2) * D + D], d, AF.Square,
                accum_out=bc(B_SQ, t)),
                waits=[(sact.h, evn)], inc=(sact.h, 1))
            sact.bump()
        ln_batch(p, [(svec.h, svec.n), (sact.h, sact.n)], with_amr=True)
        rn = svec2.n
        for tbl in range(8):
            tb = hb + tbl
            sl = tb % 2
            x1t = x1t2[:, sl * D:sl * D + D]
            outt = outt2[:, sl * D:sl * D + D]
            xadj = xadj2[:, sl * D:sl * D + D]
            ffs = ff[:, (fo + tbl) * 1024:(fo + tbl) * 1024 + 1024]
            p.op("gpsimd", lambda e, t=tb, d=x1t: e.dma_start(out=d, in_=x1r[t]),
                 waits=rel_x1[sl], inc=(dG2.h, 16))
            dsx = dG2.bump(16)
            p.op("vector", lambda e, d=x1t, a=xadj, t=fo + tbl: e.tensor_scalar(
                a, d, bc(B_AMR, t), None, ALU.add),
                waits=[(dG2.h, dsx), (svec2.h, rn)], inc=(svec.h, 1))
            xan = svec.bump()
            rel_x1[sl] = [(svec.h, xan)]
            p.op("vector", lambda e, d=outt, a=xadj, s=ffs, t=fo + tbl:
                 e.scalar_tensor_tensor(d, s, bc(B_R, t), a, ALU.mult, ALU.add),
                 waits=[(svec.h, xan)] + rel_out[sl], inc=(svec.h, 1))
            vstt = svec.bump()
            p.op("gpsimd", lambda e, t=tb, d=outt: e.dma_start(out=outr[t], in_=d),
                 waits=[(svec.h, vstt)], inc=(dSt[sl].h, 16))
            dSt[sl].bump(16)
            rel_out[sl] = [(dSt[sl].h, dSt[sl].n), (svec.h, vstt)]
        rel_ff[half % 2] = [(svec.h, svec.n)]
    p.op("gpsimd", lambda e: e.nop(), waits=[(dSt[0].h, dSt[0].n), (dSt[1].h, dSt[1].n)])
    p.emit()
    mem.free("w3s", "f4_wr", "f4_s3f", "f4_s3b", "f4_ones", "f4_s3bc", "f4_g2",
             "f4_ff", "f4_x1", "f4_xa", "f4_o", "f4_p0", "f4_p1", "f4_p2", "f4_p3")

    for g in sem_guards:
        g.__exit__(None, None, None)
    mem.free(*list(mem.guards))
    return nc


_NC_CACHE = None


def kernel(**inputs):
    global _NC_CACHE
    f32c = lambda a: np.ascontiguousarray(np.asarray(a, dtype=np.float32))
    x = f32c(inputs["L0_attn_x"])
    wqT = f32c(np.asarray(inputs["L0_attn_wq"]).T)
    wkT = f32c(np.asarray(inputs["L0_attn_wk"]).T)
    wvT = f32c(np.asarray(inputs["L0_attn_wv"]).T)
    woT = f32c(np.asarray(inputs["L0_attn_wo"]).T)
    w1T = f32c(np.asarray(inputs["L0_ffn_w1_w"]).T)
    w2T = f32c(np.asarray(inputs["L0_ffn_w2_w"]).T)
    h1T = f32c(np.asarray(inputs["L0_ffn_h1_w"]).T)
    h2T = f32c(np.asarray(inputs["L0_ffn_h2_w"]).T)
    w3T = f32c(np.asarray(inputs["L0_ffn_w3_w"]).T)
    s1r = f32c(np.asarray(inputs["L0_ffn_w1_s"]).reshape(32, 128).T)
    s2r = f32c(np.asarray(inputs["L0_ffn_w2_s"]).reshape(32, 128).T)
    hs1r = f32c(np.asarray(inputs["L0_ffn_h1_s"]).reshape(32, 128).T)
    hs2r = f32c(np.asarray(inputs["L0_ffn_h2_s"]).reshape(32, 128).T)
    s3row = f32c(np.asarray(inputs["L0_ffn_w3_s"]).reshape(1, 1024))

    j = (np.arange(128) % 64) // 2
    inv = np.power(10000.0, -2.0 * j.astype(np.float64) / 64.0)
    ang = inv[:, None] * np.arange(S, dtype=np.float64)[None, :]
    cosT = np.cos(ang).astype(np.float32)
    sinT = np.sin(ang).astype(np.float32)
    pswap = np.zeros((128, 128), np.float32)
    idx = np.arange(0, 128, 2)
    pswap[idx + 1, idx] = -1.0
    pswap[idx, idx + 1] = 1.0
    identc = np.eye(128, dtype=np.float32)
    maskc = (np.arange(128)[:, None] <= np.arange(128)[None, :]).astype(np.float32)

    if _NC_CACHE is None:
        _NC_CACHE = build_program()
    nc = _NC_CACHE

    shared = dict(wqT=wqT, wkT=wkT, wvT=wvT, woT=woT, w1T=w1T, w2T=w2T,
                  h1T=h1T, h2T=h2T, w3T=w3T, s1r=s1r, s2r=s2r, hs1r=hs1r,
                  hs2r=hs2r, s3row=s3row, cosT=cosT, sinT=sinT, pswap=pswap,
                  identc=identc, maskc=maskc)
    in_maps = []
    for c in range(8):
        m = dict(shared)
        m["x"] = np.ascontiguousarray(x[2 * c:2 * c + 2].reshape(T, D))
        in_maps.append(m)
    res = run_bass_kernel_spmd(nc, in_maps, core_ids=list(range(8)))
    out = np.concatenate(
        [np.asarray(res.results[c]["out"], np.float32).reshape(2, S, D) for c in range(8)],
        axis=0)
    return out


# version 22
# speedup vs baseline: 1.0684x; 1.0275x over previous
import numpy as np
import concourse.bass as bass
import concourse.mybir as mybir
from concourse.bass_utils import run_bass_kernel_spmd

F32 = mybir.dt.float32
BF16 = mybir.dt.bfloat16
AF = mybir.ActivationFunctionType
ALU = mybir.AluOpType
AX = mybir.AxisListType

D = 1024
S = 2048
NSEQ = 2
T = NSEQ * S
H = 16
DH = 4096
NTB = S // 128
C_SHIFT = 10.0

ENGS = ("sync", "gpsimd", "tensor", "scalar", "vector")

C_NEGC, C_S8, C_EPS, C_INV1024, C_NEG1, C_NEGHALF = range(6)
CONST_VALS = [-C_SHIFT, 0.125, 1e-5, 1.0 / 1024.0, -1.0, -0.5]

(ST_SUM, ST_SQ0, ST_SQ1, ST_M, ST_M2, ST_V, ST_R, ST_NEGM, ST_MR, ST_AMR,
 ST_T0, ST_T1, ST_RZ, ST_T2, ST_T3) = range(15)


class Ctr:
    def __init__(self, h):
        self.h = h
        self.n = 0

    def bump(self, k=1):
        self.n += k
        return self.n


PLAN_LIMIT = None
DEBUG_DRAM = False
DEBUG_DUMP = None


class Plan:

    IDX = 0

    def __init__(self, nc, dmas=None):
        self.nc = nc
        self.dmas = dmas or {}
        self.ops = {e: [] for e in ENGS}

    def op(self, eng, fn, waits=(), inc=None):
        self.ops[eng].append((list(waits), fn, inc))

    def emit(self):
        Plan.IDX += 1
        if PLAN_LIMIT is not None and Plan.IDX > PLAN_LIMIT:
            return
        for eng, ctrs in self.dmas.items():
            if self.ops[eng]:
                self.op(eng, lambda e: e.nop(),
                        waits=[(c.h, c.n) for c in ctrs if c.n > 0])

        def mk(ops):
            def run(e):
                for waits, fn, inc in ops:
                    for sem, val in waits:
                        if val > 0:
                            e.wait_ge(sem, val)
                    inst = fn(e)
                    if inc is not None:
                        inst.then_inc(inc[0], inc[1])
            return run

        with self.nc.Block() as block:
            for eng in ENGS:
                if self.ops[eng]:
                    getattr(block, eng)(mk(self.ops[eng]))


class Mem:

    def __init__(self, nc):
        self.nc = nc
        self.guards = {}
        self.order = {}
        self.seq = 0

    def _reg(self, name, g):
        h = g.__enter__()
        self.guards[name] = g
        self.order[name] = self.seq
        self.seq += 1
        return h

    def sbuf(self, name, shape, dt, side=None):
        assert name not in self.guards
        return self._reg(name, self.nc.sbuf_tensor(f"{name}_u{self.seq}", shape, dt, side=side))

    def init_psum_pool(self):
        g = []
        for nm, shape, dt in [("psA", [128, 1024], F32), ("psB", [128, 1024], F32),
                              ("psC", [128, 512], F32), ("psD", [128, 512], F32),
                              ("psT0", [128, 512], BF16), ("psT1", [128, 512], BF16)]:
            cm = self.nc.psum_tensor(nm, shape, dt)
            g.append(cm.__enter__())
        pA, pB, pC, pD, pT0, pT1 = g
        self.pool = {
            "1024F": [pA[:], pB[:]],
            "512F": [pC[:], pD[:], pA[:, 0:512], pA[:, 512:1024],
                     pB[:, 0:512], pB[:, 512:1024]],
            "128B": [pT0[:], pT1[:]],
            "512B": [pT0[:], pT1[:]],
        }
        self.pool_free = {k: list(range(len(v))) for k, v in self.pool.items()}
        self.pool_named = {}

    def psum(self, name, shape, dt=F32):
        key = {(1024, 0): "1024F", (512, 0): "512F",
               (128, 1): "128B", (512, 1): "512B"}[(shape[1], 1 if dt == BF16 else 0)]
        idx = self.pool_free[key].pop(0)
        self.pool_named[name] = (key, idx)
        return self.pool[key][idx]

    def free(self, *names):
        ps = [n for n in names if n in self.pool_named]
        sb = [n for n in names if n not in self.pool_named]
        for n in ps:
            key, idx = self.pool_named.pop(n)
            self.pool_free[key].append(idx)
            self.pool_free[key].sort()
        for n in sorted(sb, key=lambda n: -self.order[n]):
            self.guards.pop(n).__exit__(None, None, None)
            del self.order[n]


BASS_KW = {}


def build_program():
    Plan.IDX = 0
    nc = bass.Bass("TRN2", target_bir_lowering=False, **BASS_KW)

    din = {}
    for name, shape in [
        ("x", [T, D]), ("wqT", [D, D]), ("wkT", [D, D]), ("wvT", [D, D]),
        ("woT", [D, D]), ("w1T", [D, DH]), ("w2T", [D, DH]), ("h1T", [DH, DH]),
        ("h2T", [DH, DH]), ("w3T", [DH, D]), ("s1r", [128, 32]), ("s2r", [128, 32]),
        ("hs1r", [128, 32]), ("hs2r", [128, 32]), ("s3row", [1, D]),
        ("cosT", [128, S]), ("sinT", [128, S]), ("pswap", [128, 128]),
        ("identc", [128, 128]), ("maskc", [128, 128]),
    ]:
        din[name] = nc.dram_tensor(name, shape, F32, kind="ExternalInput").ap()

    out_d = nc.dram_tensor("out", [T, D], F32, kind="ExternalOutput").ap()
    scratch_kind = "ExternalOutput" if DEBUG_DRAM else "Internal"
    x1d = nc.dram_tensor("x1d", [T, D], F32, kind=scratch_kind).ap()
    xln1d = nc.dram_tensor("xln1d", [8, 128, T], BF16, kind=scratch_kind).ap()
    hd = nc.dram_tensor("hd", [8, 32, 128, 512], BF16, kind=scratch_kind).ap()
    g1d = nc.dram_tensor("g1d", [8, 32, 128, 512], BF16, kind=scratch_kind).ap()
    g2d = nc.dram_tensor("g2d", [32, 32, 128, 128], BF16, kind=scratch_kind).ap()

    dbg = (nc.dram_tensor("dbg", [128, 16640], BF16, kind="ExternalOutput").ap()
           if DEBUG_DUMP else None)

    xr = din["x"].rearrange("(tb p) d -> tb p d", p=128)
    x1r = x1d.rearrange("(tb p) d -> tb p d", p=128)
    outr = out_d.rearrange("(tb p) d -> tb p d", p=128)

    mem = Mem(nc)
    mem.init_psum_pool()
    sem_guards = []
    sems = {}
    for nm in ("dG", "dG2", "dG3", "dL0", "dL1", "dS0", "dS1", "mm", "mm2", "act", "vec", "vec2"):
        g = nc.semaphore(nm)
        sems[nm] = Ctr(g.__enter__())
        sem_guards.append(g)
    dG = sems["dG"]
    dG2 = sems["dG2"]
    dG3 = sems["dG3"]
    dL = [sems["dL0"], sems["dL1"]]
    dSt = [sems["dS0"], sems["dS1"]]
    smm, smm2, sact, svec, svec2 = (sems[k] for k in ("mm", "mm2", "act", "vec", "vec2"))
    DMAS = {"sync": [dL[0], dL[1]], "gpsimd": [dG, dG2, dG3, dSt[0], dSt[1]]}

    consts = mem.sbuf("consts", [128, 8], F32)
    identb = mem.sbuf("identb", [128, 128], BF16)
    pswapb = mem.sbuf("pswapb", [128, 128], BF16)
    maskb = mem.sbuf("maskb", [128, 128], BF16)
    cosf = mem.sbuf("cosf", [128, S], F32, side="right")
    sinf = mem.sbuf("sinf", [128, S], F32, side="right")
    cosb = mem.sbuf("cosb", [128, S], BF16, side="right")
    s1s = mem.sbuf("s1s", [128, 32], F32)
    s2s = mem.sbuf("s2s", [128, 32], F32)
    hs1s = mem.sbuf("hs1s", [128, 32], F32)
    hs2s = mem.sbuf("hs2s", [128, 32], F32)
    scrap = mem.sbuf("scrap", [128, 2048], BF16)
    stt = mem.sbuf("stt", [128, 48], F32)
    bst = mem.sbuf("bst", [128, 8 * 16], F32)

    def st(i, par=0):
        return stt[:, par * 16 + i:par * 16 + i + 1]

    (B_SUM, B_SQ, B_M, B_M2, B_V, B_R, B_MR, B_AMR) = range(8)

    def bs(i):
        return bst[:, i * 16:(i + 1) * 16]

    def bc(i, tb):
        return bst[:, i * 16 + tb:i * 16 + tb + 1]

    def ln_batch(p, pre_wait, with_amr=False):
        def step(eng, fn, extra=()):
            p.op(eng, fn, waits=[(svec2.h, svec2.n)] + list(extra), inc=(svec2.h, 1))
            svec2.bump()

        step("vector", lambda e: e.tensor_scalar(bs(B_M), bs(B_SUM), cc(C_INV1024),
                                                 None, ALU.mult), extra=list(pre_wait))
        step("vector", lambda e: e.tensor_tensor(bs(B_M2), bs(B_M), bs(B_M), ALU.mult))
        step("vector", lambda e: e.tensor_scalar(bs(B_V), bs(B_SQ), cc(C_INV1024),
                                                 None, ALU.mult))
        step("vector", lambda e: e.tensor_tensor(bs(B_V), bs(B_V), bs(B_M2), ALU.subtract))
        step("scalar", lambda e: e.activation(bs(B_V), bs(B_V), AF.Ln, bias=cc(C_EPS)))
        step("scalar", lambda e: e.activation(bs(B_R), bs(B_V), AF.Exp, scale=cc(C_NEGHALF)))
        if with_amr:
            step("vector", lambda e: e.tensor_tensor(bs(B_MR), bs(B_M), bs(B_R), ALU.mult))
            step("vector", lambda e: e.tensor_scalar(bs(B_AMR), bs(B_MR), cc(C_NEG1),
                                                     None, ALU.mult))
        return svec2.n

    def cc(i):
        return consts[:, i:i + 1]

    p = Plan(nc, DMAS)
    for i, v in enumerate(CONST_VALS):
        p.op("vector", lambda e, i=i, v=v: e.memset(consts[:, i:i + 1], v))
    for dst, src in [(identb, din["identc"]), (pswapb, din["pswap"]),
                     (maskb, din["maskc"])]:
        p.op("gpsimd", lambda e, d=dst, s=src: e.dma_start(out=d[:], in_=s),
             inc=(dG.h, 16))
        dG.bump(16)
    for dst, src in [(cosf, din["cosT"]), (sinf, din["sinT"]), (s1s, din["s1r"]),
                     (s2s, din["s2r"]), (hs1s, din["hs1r"]), (hs2s, din["hs2r"])]:
        p.op("sync", lambda e, d=dst, s=src: e.dma_start(out=d[:], in_=s),
             inc=(dL[0].h, 16))
        dL[0].bump(16)
    p.op("vector", lambda e: e.tensor_copy(cosb[:], cosf[:]),
         waits=[(dL[0].h, dL[0].n)])
    p.emit()

    def ln_tail(p, sq_wait, two_sq=False, par=0):
        def step(eng, fn, extra=()):
            p.op(eng, fn, waits=[(svec2.h, svec2.n)] + list(extra), inc=(svec2.h, 1))
            svec2.bump()

        if two_sq:
            step("vector", lambda e, par=par: e.tensor_tensor(
                st(ST_T0, par), st(ST_SQ0, par), st(ST_SQ1, par), ALU.add), extra=[sq_wait])
            step("vector", lambda e, par=par: e.tensor_scalar(
                st(ST_T0, par), st(ST_T0, par), cc(C_INV1024), None, ALU.mult))
        else:
            step("vector", lambda e, par=par: e.tensor_scalar(
                st(ST_T0, par), st(ST_SQ0, par), cc(C_INV1024), None, ALU.mult),
                extra=[sq_wait])
        step("vector", lambda e, par=par: e.tensor_scalar(
            st(ST_M, par), st(ST_SUM, par), cc(C_INV1024), None, ALU.mult))
        step("vector", lambda e, par=par: e.tensor_tensor(
            st(ST_M2, par), st(ST_M, par), st(ST_M, par), ALU.mult))
        step("vector", lambda e, par=par: e.tensor_tensor(
            st(ST_V, par), st(ST_T0, par), st(ST_M2, par), ALU.subtract))
        step("scalar", lambda e, par=par: e.activation(
            st(ST_T1, par), st(ST_V, par), AF.Ln, bias=cc(C_EPS)))
        step("scalar", lambda e, par=par: e.activation(
            st(ST_R, par), st(ST_T1, par), AF.Exp, scale=cc(C_NEGHALF)))
        return svec2.n

    def emit_dump(buf, cols):
        p = Plan(nc, DMAS)
        for c0 in range(0, cols, 4096):
            cw = min(4096, cols - c0)
            p.op("gpsimd", lambda e, c0=c0, cw=cw: e.dma_start(
                out=dbg[:, c0:c0 + cw], in_=buf[:, c0:c0 + cw]), inc=(dSt[0].h, 16))
            dSt[0].bump(16)
        p.emit()

    rel_ws = []
    for seq in range(NSEQ):
        tb0 = seq * NTB

        xln0 = mem.sbuf(f"xln0_{seq}", [128, 8 * S], BF16)
        xt2 = mem.sbuf("a1_xt", [128, 2 * D], F32)
        xtm2 = mem.sbuf("a1_tm", [128, 2 * D], BF16)
        psT = [mem.psum("a1_ps0", [128, 512], BF16), mem.psum("a1_ps1", [128, 512], BF16)]
        p = Plan(nc, DMAS)
        rel_xt = [[], []]
        for tb in range(NTB):
            sl = tb % 2
            xt = xt2[:, sl * D:sl * D + D]
            p.op("sync", lambda e, t=tb, d=xt: e.dma_start(out=d, in_=xr[tb0 + t]),
                 waits=rel_xt[sl], inc=(dL[sl].h, 16))
            dsn = dL[sl].bump(16)
            p.op("vector", lambda e, d=xt, t=tb: e.reduce_sum(bc(B_SUM, t), d, axis=AX.X),
                 waits=[(dL[sl].h, dsn)], inc=(svec.h, 1))
            vr = svec.bump()
            p.op("scalar", lambda e, d=xt, t=tb: e.activation(
                scrap[:, (t % 2) * D:(t % 2) * D + D], d, AF.Square,
                accum_out=bc(B_SQ, t)),
                waits=[(dL[sl].h, dsn)], inc=(sact.h, 1))
            an = sact.bump()
            rel_xt[sl] = [(svec.h, vr), (sact.h, an)]
        rn = ln_batch(p, [(svec.h, svec.n), (sact.h, sact.n)])
        rel_xt_a = [list(rel_xt[0]), list(rel_xt[1])]
        rel_xt = [[], []]
        rel_psT = [[], []]
        rel_xtm = [[], []]
        for tb in range(NTB):
            sl = tb % 2
            xt = xt2[:, sl * D:sl * D + D]
            xtm = xtm2[:, sl * D:sl * D + D]
            p.op("sync", lambda e, t=tb, d=xt: e.dma_start(out=d, in_=xr[tb0 + t]),
                 waits=rel_xt[sl] + (rel_xt_a[sl] if tb < 2 else []), inc=(dL[sl].h, 16))
            dsn = dL[sl].bump(16)
            p.op("vector", lambda e, d=xt, m=xtm, t=tb: e.tensor_scalar(
                m, d, bc(B_M, t), bc(B_R, t), ALU.subtract, ALU.mult),
                waits=[(dL[sl].h, dsn), (svec2.h, rn)] + rel_xtm[sl], inc=(svec.h, 1))
            xn = svec.bump()
            rel_xt[sl] = [(svec.h, xn)]
            last_mn = 0
            for g in range(2):
                for c4 in range(4):
                    c = g * 4 + c4
                    w = ([(svec.h, xn)] if c4 == 0 else []) + (rel_psT[g] if c4 == 0 else [])
                    p.op("tensor", lambda e, c=c, c4=c4, m=xtm, g=g: e.transpose(
                        psT[g][:, c4 * 128:c4 * 128 + 128],
                        m[:, c * 128:c * 128 + 128], identb[:]),
                        waits=w, inc=((smm.h, 1) if c4 == 3 else None))
                last_mn = smm.bump()
                dv = xln0[:].rearrange("p (c t) -> p c t", c=8)
                p.op("vector", lambda e, g=g, t=tb, dv=dv: e.tensor_copy(
                    dv[:, g * 4:(g + 1) * 4, t * 128:t * 128 + 128], psT[g][:]),
                    waits=[(smm.h, last_mn)], inc=(svec2.h, 1))
                rel_psT[g] = [(svec2.h, svec2.bump())]
            rel_xtm[sl] = [(smm.h, last_mn)]
        p.emit()
        if seq == 0 and DEBUG_DUMP == "xln0":
            emit_dump(xln0, 8 * S)
        mem.free("a1_xt", "a1_tm", "a1_ps0", "a1_ps1")

        Qb = mem.sbuf(f"Qb_{seq}", [128, 8 * S], BF16, side="right")
        Kb = mem.sbuf(f"Kb_{seq}", [128, 8 * S], BF16, side="right")
        Va = mem.sbuf(f"Va_{seq}", [128, 16 * 1040], BF16, side="right")
        ws2 = mem.sbuf("a2_ws", [128, 2 * 8 * 1024], BF16)
        tm1b = mem.sbuf("a2_t1", [128, 3 * 512], BF16)
        tm2b = mem.sbuf("a2_t2", [128, 3 * 512], BF16)
        pa = [mem.psum("a2_pa0", [128, 512], F32), mem.psum("a2_pa1", [128, 512], F32)]
        pb = [mem.psum("a2_pb0", [128, 512], F32), mem.psum("a2_pb1", [128, 512], F32)]
        p = Plan(nc, DMAS)
        wsem = [dG, dG2, dG3]
        wn = {}
        rel_wstage = [[], []]

        def load_w(wi, nm):
            sem = wsem[wi]
            wst = ws2[:, (wi % 2) * 8192:(wi % 2) * 8192 + 8192]
            for kc in range(8):
                p.op("gpsimd", lambda e, k=kc, nm=nm, w=wst: e.dma_start(
                    out=w[:, k * 1024:(k + 1) * 1024],
                    in_=din[nm][k * 128:(k + 1) * 128, :]),
                    waits=((rel_ws + rel_wstage[wi % 2]) if kc == 0 else []),
                    inc=(sem.h, 16))
                sem.bump(16)
            wn[wi] = (sem.h, sem.n)

        load_w(0, "wqT")
        load_w(1, "wkT")
        for wi, buf in ((0, Qb), (1, Kb)):
            ws = ws2[:, (wi % 2) * 8192:(wi % 2) * 8192 + 8192]
            rel_pa = [[], []]
            rel_pb = [[], []]
            ti = 0
            pend = []

            def emit_rot(item, ws=ws):
                tj, dstj, ttj, anj = item
                tm1 = tm1b[:, (tj % 3) * 512:(tj % 3) * 512 + 512]
                tm2 = tm2b[:, (tj % 3) * 512:(tj % 3) * 512 + 512]
                pbs = pb[tj % 2]
                p.op("tensor", lambda e, d=dstj, pbs=pbs: e.matmul(pbs[:], pswapb[:], d,
                                                                   start=True, stop=True),
                     waits=[(sact.h, anj)] + rel_pb[tj % 2], inc=(smm2.h, 1))
                m2 = smm2.bump()
                p.op("vector", lambda e, t=ttj, pbs=pbs, tm2=tm2: e.tensor_tensor(
                    tm2, pbs[:], sinf[:, t * 512:t * 512 + 512], ALU.mult),
                    waits=[(smm2.h, m2)], inc=(svec.h, 1))
                vt2 = svec.bump()
                rel_pb[tj % 2] = [(svec.h, vt2)]
                p.op("vector", lambda e, d=dstj, tm1=tm1, tm2=tm2: e.tensor_tensor(
                    d, tm1, tm2, ALU.add),
                    waits=[(svec.h, vt2)], inc=(svec.h, 1))
                svec.bump()

            for oc in range(8):
                for tt in range(4):
                    dst = buf[:, oc * S + tt * 512:oc * S + tt * 512 + 512]
                    ps = pa[ti % 2]
                    w0 = [wn[wi]] + rel_pa[ti % 2]
                    for kc in range(8):
                        p.op("tensor", lambda e, k=kc, o=oc, t=tt, ps=ps, ws=ws: e.matmul(
                            ps[:], ws[:, k * 1024 + o * 128:k * 1024 + o * 128 + 128],
                            xln0[:, k * S + t * 512:k * S + t * 512 + 512],
                            start=(k == 0), stop=(k == 7)),
                            waits=(w0 if kc == 0 else []),
                            inc=((smm.h, 1) if kc == 7 else None))
                    mn = smm.bump()
                    p.op("scalar", lambda e, d=dst, ps=ps: e.activation(d, ps[:], AF.Identity),
                         waits=[(smm.h, mn)], inc=(sact.h, 1))
                    an = sact.bump()
                    tm1 = tm1b[:, (ti % 3) * 512:(ti % 3) * 512 + 512]
                    p.op("vector", lambda e, t=tt, ps=ps, tm1=tm1: e.tensor_tensor(
                        tm1, ps[:], cosf[:, t * 512:t * 512 + 512], ALU.mult),
                        waits=[(sact.h, an)], inc=(svec.h, 1))
                    v1 = svec.bump()
                    rel_pa[ti % 2] = [(sact.h, an), (svec.h, v1)]
                    pend.append((ti, dst, tt, an))
                    if len(pend) > 1:
                        emit_rot(pend.pop(0))
                    ti += 1
            while pend:
                emit_rot(pend.pop(0))
            rel_wstage[wi % 2] = [(smm.h, smm.n)]
            if wi == 0:
                load_w(2, "wvT")

        ws = ws2[:, 0:8192]
        p.op("vector", lambda e: e.memset(Va[:, 64::65], 1.0))
        vav = Va.ap().rearrange("p (tb h c) -> p tb h c", tb=16, h=16, c=65)
        rel_pa = [[], []]
        ti = 0
        for tb in range(NTB):
            for ot in range(2):
                ps = pa[ti % 2]
                w0 = [wn[2]] + rel_pa[ti % 2]
                for kc in range(8):
                    p.op("tensor", lambda e, k=kc, t=tb, o=ot, ps=ps, ws=ws: e.matmul(
                        ps[:], xln0[:, k * S + t * 128:t * 128 + k * S + 128],
                        ws[:, k * 1024 + o * 512:k * 1024 + o * 512 + 512],
                        start=(k == 0), stop=(k == 7)),
                        waits=(w0 if kc == 0 else []),
                        inc=((smm.h, 1) if kc == 7 else None))
                mn = smm.bump()
                p.op("vector", lambda e, t=tb, o=ot, ps=ps: e.tensor_copy(
                    vav[:, t, o * 8:(o + 1) * 8, 0:64],
                    ps.rearrange("p (h c) -> p h c", h=8)),
                    waits=[(smm.h, mn)], inc=(svec.h, 1))
                rel_pa[ti % 2] = [(svec.h, svec.bump())]
                ti += 1
        rel_ws = [(smm.h, smm.n)]
        p.emit()
        if seq == 0 and DEBUG_DUMP in ("Qb", "Kb", "Va"):
            emit_dump({"Qb": Qb, "Kb": Kb, "Va": Va}[DEBUG_DUMP],
                      8 * S if DEBUG_DUMP != "Va" else 16 * 1040)
        mem.free(f"xln0_{seq}", "a2_ws", "a2_t1", "a2_t2", "a2_pa0", "a2_pa1",
                 "a2_pb0", "a2_pb1")

        hctm = mem.sbuf(f"hctm_{seq}", [128, 16 * 1024], BF16)
        eb = mem.sbuf("a3_e", [128, 3 * S], BF16)
        scr2 = mem.sbuf("a3_scr", [65, 2 * 128], BF16)
        scs = [mem.psum("a3_sc0", [128, 1024], F32), mem.psum("a3_sc1", [128, 1024], F32)]
        pvb = [mem.psum("a3_pv0", [128, 512], F32), mem.psum("a3_pv1", [128, 512], F32)]
        pvs = [pvb[0][0:65, 0:128], pvb[1][0:65, 0:128]]
        ptrs = [mem.psum("a3_tr0", [128, 512], BF16), mem.psum("a3_tr1", [128, 512], BF16)]
        p = Plan(nc, DMAS)
        iters = [(h, qb) for h in range(H) for qb in range(NTB)]
        NIT = len(iters)
        rel_sc = [[], []]
        rel_pv = [[], []]
        rel_e = [[], [], []]
        mg = {}
        ag = {}
        vm = {}
        g0_of = {}
        gctr = [0]

        def grp_of(qb):
            return [list(range(g, min(g + 8, qb + 1))) for g in range(0, qb + 1, 8)]

        def plan_scores(it):
            h, qb = iters[it]
            hp, hr = h // 2, (h % 2) * 64
            q0 = hp * S + qb * 128
            g0_of[it] = gctr[0]
            mg[it] = []
            for kbg in grp_of(qb):
                ms = scs[gctr[0] % 2]
                w0 = list(rel_sc[gctr[0] % 2])
                for j, kb in enumerate(kbg):
                    p.op("tensor", lambda e, kb=kb, j=j, ms=ms, hr=hr, hp=hp, q0=q0: e.matmul(
                        ms[:, j * 128:j * 128 + 128],
                        Kb[hr:hr + 64, hp * S + kb * 128:hp * S + kb * 128 + 128],
                        Qb[hr:hr + 64, q0:q0 + 128],
                        start=True, stop=True),
                        waits=(w0 if j == 0 else []),
                        inc=((smm.h, 1) if j == len(kbg) - 1 else None))
                mg[it].append(smm.bump())
                gctr[0] += 1

        def plan_exps(it):
            h, qb = iters[it]
            e0 = (it % 3) * S
            groups = grp_of(qb)
            for gi, kbg in enumerate(groups):
                ln = len(kbg)
                ms = scs[(g0_of[it] + gi) % 2]
                base = e0 + kbg[0] * 128
                wts = [(smm.h, mg[it][gi])] + (list(rel_e[it % 3]) if gi == 0 else [])
                p.op("scalar", lambda e, b=base, ln=ln, ms=ms: e.activation(
                    eb[:, b:b + ln * 128], ms[:, 0:ln * 128],
                    AF.Exp, bias=cc(C_NEGC), scale=cc(C_S8)),
                    waits=wts, inc=(sact.h, 1))
                rel_sc[(g0_of[it] + gi) % 2] = [(sact.h, sact.bump())]
            ag[it] = sact.n

        def plan_mask(it):
            h, qb = iters[it]
            e0 = (it % 3) * S
            p.op("vector", lambda e, qb=qb, e0=e0: e.tensor_tensor(
                eb[:, e0 + qb * 128:e0 + qb * 128 + 128],
                eb[:, e0 + qb * 128:e0 + qb * 128 + 128], maskb[:], ALU.mult),
                waits=[(sact.h, ag[it])], inc=(svec.h, 1))
            vm[it] = svec.bump()

        ev_n = {}

        def plan_pv(it):
            h, qb = iters[it]
            e0 = (it % 3) * S
            pv = pvs[it % 2]
            w0 = [(sact.h, ag[it])] + rel_pv[it % 2]
            for kb in range(qb + 1):
                wts = (w0 if kb == 0 else []) + ([(svec.h, vm[it])] if kb == qb else [])
                p.op("tensor", lambda e, kb=kb, pv=pv, h=h, qb=qb, e0=e0: e.matmul(
                    pv, vav[:, kb, h, :], eb[:, e0 + kb * 128:e0 + kb * 128 + 128],
                    start=(kb == 0), stop=(kb == qb)),
                    waits=wts, inc=((smm2.h, 1) if kb == qb else None))
            rel_e[it % 3] = [(smm2.h, smm2.bump())]
            scr = scr2[:, (it % 2) * 128:(it % 2) * 128 + 128]
            p.op("vector", lambda e, pv=pv, scr=scr: e.tensor_copy(scr, pv),
                 waits=[(smm2.h, smm2.n)] + rel_scr[it % 2], inc=(svec2.h, 1))
            ev_n[it] = svec2.bump()
            rel_pv[it % 2] = [(svec2.h, ev_n[it])]

        def plan_fin(it):
            h, qb = iters[it]
            ptr = ptrs[it % 2]
            scr = scr2[:, (it % 2) * 128:(it % 2) * 128 + 128]
            p.op("tensor", lambda e, scr=scr, ptr=ptr: e.transpose(
                ptr[:, 0:65], scr, identb[0:65, 0:65]),
                waits=[(svec2.h, ev_n[it])] + rel_ptr[it % 2], inc=(smm.h, 1))
            trn = smm.bump()
            rel_scr[it % 2] = [(smm.h, trn)]
            p.op("vector", lambda e, ptr=ptr: e.reciprocal(st(ST_RZ), ptr[:, 64:65]),
                 waits=[(smm.h, trn)], inc=(svec2.h, 1))
            rzn = svec2.bump()
            p.op("vector", lambda e, ptr=ptr, qb=qb, h=h: e.tensor_scalar(
                hctm[:, qb * 1024 + h * 64:qb * 1024 + h * 64 + 64],
                ptr[:, 0:64], st(ST_RZ), None, ALU.mult),
                waits=[(svec2.h, rzn)], inc=(svec2.h, 1))
            rel_ptr[it % 2] = [(svec2.h, svec2.bump())]

        rel_scr = [[], []]
        rel_ptr = [[], []]
        plan_scores(0)
        for it in range(NIT):
            plan_exps(it)
            plan_mask(it)
            if it + 1 < NIT:
                plan_scores(it + 1)
            plan_pv(it)
            if it >= 1:
                plan_fin(it - 1)
        plan_fin(NIT - 1)
        p.emit()
        if seq == 0 and DEBUG_DUMP == "hctm":
            emit_dump(hctm, 16 * 1024)
        mem.free(f"Qb_{seq}", f"Kb_{seq}", f"Va_{seq}", "a3_e", "a3_scr",
                 "a3_sc0", "a3_sc1", "a3_pv0", "a3_pv1", "a3_tr0", "a3_tr1")

        hcf = mem.sbuf(f"hcf_{seq}", [128, 8 * S], BF16, side="right")
        psT = [mem.psum("a4_ps0", [128, 512], BF16), mem.psum("a4_ps1", [128, 512], BF16)]
        p = Plan(nc, DMAS)
        rel_psT = [[], []]
        for tb in range(NTB):
            for g in range(2):
                for c4 in range(4):
                    c = g * 4 + c4
                    p.op("tensor", lambda e, tb=tb, c=c, c4=c4, g=g: e.transpose(
                        psT[g][:, c4 * 128:c4 * 128 + 128],
                        hctm[:, tb * 1024 + c * 128:tb * 1024 + c * 128 + 128],
                        identb[:]),
                        waits=(rel_psT[g] if c4 == 0 else []),
                        inc=((smm.h, 1) if c4 == 3 else None))
                mn = smm.bump()
                dv = hcf[:].rearrange("p (c t) -> p c t", c=8)
                p.op("vector", lambda e, g=g, tb=tb, dv=dv: e.tensor_copy(
                    dv[:, g * 4:(g + 1) * 4, tb * 128:tb * 128 + 128], psT[g][:]),
                    waits=[(smm.h, mn)], inc=(svec.h, 1))
                rel_psT[g] = [(svec.h, svec.bump())]
        p.emit()
        if seq == 0 and DEBUG_DUMP == "hcf":
            emit_dump(hcf, 8 * S)
        mem.free(f"hctm_{seq}", "a4_ps0", "a4_ps1")

        ws = mem.sbuf("a5_ws", [128, 8 * 1024], BF16)
        aof = mem.sbuf("a5_ao", [128, 16 * 1024], F32)
        xt2 = mem.sbuf("a5_xt", [128, 2 * D], F32)
        x1t2 = mem.sbuf("a5_x1", [128, 2 * D], F32)
        xadj2 = mem.sbuf("a5_xa", [128, 2 * D], F32)
        xtm2 = mem.sbuf("a5_tm", [128, 2 * D], BF16)
        xstg2 = mem.sbuf("a5_st", [128, 2 * D], BF16)
        pOs = [mem.psum("a5_p0", [128, 512], F32), mem.psum("a5_p1", [128, 512], F32),
               mem.psum("a5_p2", [128, 512], F32), mem.psum("a5_p3", [128, 512], F32)]
        psT = [mem.psum("a5_t0", [128, 512], BF16), mem.psum("a5_t1", [128, 512], BF16)]
        p = Plan(nc, DMAS)
        for kc in range(8):
            p.op("gpsimd", lambda e, k=kc: e.dma_start(
                out=ws[:, k * 1024:(k + 1) * 1024],
                in_=din["woT"][k * 128:(k + 1) * 128, :]),
                waits=(rel_ws if kc == 0 else []), inc=(dG.h, 16))
            dG.bump(16)
        dGn = dG.n
        rel_pO = [[], [], [], []]
        pi = 0
        for tb in range(NTB):
            aos = aof[:, tb * 1024:(tb + 1) * 1024]
            for ot in range(2):
                ps = pOs[pi % 4]
                w0 = [(dG.h, dGn)] + rel_pO[pi % 4]
                for dc in range(8):
                    p.op("tensor", lambda e, d=dc, t=tb, o=ot, ps=ps: e.matmul(
                        ps[:], hcf[:, d * S + t * 128:d * S + t * 128 + 128],
                        ws[:, d * 1024 + o * 512:d * 1024 + o * 512 + 512],
                        start=(d == 0), stop=(d == 7)),
                        waits=(w0 if dc == 0 else []),
                        inc=((smm.h, 1) if dc == 7 else None))
                mn = smm.bump()
                p.op("scalar", lambda e, o=ot, d=aos, ps=ps: e.activation(
                    d[:, o * 512:o * 512 + 512], ps[:], AF.Identity),
                    waits=[(smm.h, mn)], inc=(sact.h, 1))
                rel_pO[pi % 4] = [(sact.h, sact.bump())]
                pi += 1
            evn = sact.n
            p.op("vector", lambda e, d=aos, t=tb: e.reduce_sum(bc(B_SUM, t), d, axis=AX.X),
                 waits=[(sact.h, evn)], inc=(svec.h, 1))
            svec.bump()
            p.op("scalar", lambda e, d=aos, t=tb: e.activation(
                scrap[:, (t % 2) * D:(t % 2) * D + D], d, AF.Square,
                accum_out=bc(B_SQ, t)),
                waits=[(sact.h, evn)], inc=(sact.h, 1))
            sact.bump()
        ln_batch(p, [(svec.h, svec.n), (sact.h, sact.n)], with_amr=True)
        rn = svec2.n
        rel_xt = [[], []]
        rel_x1t = [[], []]
        for tb in range(NTB):
            sl = tb % 2
            xt = xt2[:, sl * D:sl * D + D]
            x1t = x1t2[:, sl * D:sl * D + D]
            xadj = xadj2[:, sl * D:sl * D + D]
            aos = aof[:, tb * 1024:(tb + 1) * 1024]
            p.op("sync", lambda e, t=tb, d=xt: e.dma_start(out=d, in_=xr[tb0 + t]),
                 waits=rel_xt[sl], inc=(dL[sl].h, 16))
            dsx = dL[sl].bump(16)
            p.op("vector", lambda e, d=xt, a=xadj, t=tb: e.tensor_scalar(
                a, d, bc(B_AMR, t), None, ALU.add),
                waits=[(dL[sl].h, dsx), (svec2.h, rn)], inc=(svec.h, 1))
            xan = svec.bump()
            rel_xt[sl] = [(svec.h, xan)]
            p.op("vector", lambda e, d=x1t, a=xadj, s=aos, t=tb: e.scalar_tensor_tensor(
                d, s, bc(B_R, t), a, ALU.mult, ALU.add),
                waits=[(svec.h, xan)] + rel_x1t[sl], inc=(svec.h, 1))
            vstt = svec.bump()
            p.op("vector", lambda e, d=x1t, t=tb: e.reduce_sum(bc(B_SUM, t), d, axis=AX.X),
                 waits=[(svec.h, vstt)], inc=(svec.h, 1))
            vr1 = svec.bump()
            p.op("scalar", lambda e, d=x1t, t=tb: e.activation(
                scrap[:, (t % 2) * D:(t % 2) * D + D], d, AF.Square,
                accum_out=bc(B_SQ, t)),
                waits=[(svec.h, vstt)], inc=(sact.h, 1))
            an1 = sact.bump()
            p.op("gpsimd", lambda e, t=tb, d=x1t: e.dma_start(out=x1r[tb0 + t], in_=d),
                 waits=[(svec.h, vstt)], inc=(dSt[sl].h, 16))
            dSt[sl].bump(16)
            rel_x1t[sl] = [(dSt[sl].h, dSt[sl].n), (vr1 and svec.h, vr1), (sact.h, an1)]
        ln_batch(p, [(svec.h, svec.n), (sact.h, sact.n)])
        rn1 = svec2.n
        stc = [(dSt[0].h, dSt[0].n), (dSt[1].h, dSt[1].n)]
        rel_x1t2 = [[], []]
        rel_psT = [[], []]
        rel_stg = [[], []]
        rel_xtm = [[], []]
        for tb in range(NTB):
            sl = tb % 2
            x1t = x1t2[:, sl * D:sl * D + D]
            xtm = xtm2[:, sl * D:sl * D + D]
            xstg = xstg2[:, sl * D:sl * D + D]
            p.op("sync", lambda e, t=tb, d=x1t: e.dma_start(out=d, in_=x1r[tb0 + t]),
                 waits=rel_x1t2[sl] + (rel_x1t[sl] + stc if tb < 2 else []),
                 inc=(dL[sl].h, 16))
            dsx = dL[sl].bump(16)
            p.op("vector", lambda e, d=x1t, m=xtm, t=tb: e.tensor_scalar(
                m, d, bc(B_M, t), bc(B_R, t), ALU.subtract, ALU.mult),
                waits=[(dL[sl].h, dsx), (svec2.h, rn1)] + rel_xtm[sl], inc=(svec.h, 1))
            xn = svec.bump()
            rel_x1t2[sl] = [(svec.h, xn)]
            last_mn = 0
            for g in range(2):
                for c4 in range(4):
                    c = g * 4 + c4
                    w = ([(svec.h, xn)] if c4 == 0 else []) + (rel_psT[g] if c4 == 0 else [])
                    p.op("tensor", lambda e, c=c, c4=c4, m=xtm, g=g: e.transpose(
                        psT[g][:, c4 * 128:c4 * 128 + 128],
                        m[:, c * 128:c * 128 + 128], identb[:]),
                        waits=w, inc=((smm.h, 1) if c4 == 3 else None))
                last_mn = smm.bump()
                p.op("vector", lambda e, g=g, d=xstg: e.tensor_copy(
                    d[:, g * 512:g * 512 + 512], psT[g][:]),
                    waits=[(smm.h, last_mn)] + (rel_stg[sl] if g == 0 else []),
                    inc=(svec2.h, 1))
                rel_psT[g] = [(svec2.h, svec2.bump())]
            rel_xtm[sl] = [(smm.h, last_mn)]
            col0 = seq * S + tb * 128
            p.op("gpsimd", lambda e, d=xstg, c0=col0: e.dma_start(
                out=xln1d[:, :, c0:c0 + 128].transpose([1, 0, 2]),
                in_=d.rearrange("p (c f) -> p c f", c=8)),
                waits=[(svec2.h, svec2.n)], inc=(dSt[sl].h, 16))
            dSt[sl].bump(16)
            rel_stg[sl] = [(dSt[sl].h, dSt[sl].n)]
        p.emit()
        rel_ws = [(smm.h, smm.n)]
        mem.free(f"hcf_{seq}", "a5_ws", "a5_ao", "a5_xt", "a5_x1", "a5_xa", "a5_tm",
                 "a5_st", "a5_p0", "a5_p1", "a5_p2", "a5_p3", "a5_t0", "a5_t1")

    mem.free("cosf", "sinf", "cosb")

    w1s = mem.sbuf("w1s", [128, 8 * DH], BF16)
    w2s = mem.sbuf("w2s", [128, 8 * DH], BF16)
    xt1 = mem.sbuf("f1_x", [128, 2 * 8 * 512], BF16)
    t1f = mem.sbuf("f1_t1", [128, 2 * 512], F32)
    hstg = mem.sbuf("f1_h", [128, 2 * 512], BF16)
    pa = [mem.psum("f1_pa0", [128, 512], F32), mem.psum("f1_pa1", [128, 512], F32)]
    pb = [mem.psum("f1_pb0", [128, 512], F32), mem.psum("f1_pb1", [128, 512], F32)]
    p = Plan(nc, DMAS)
    grp_sem = {0: dG, 1: dG2, 2: dG3, 3: dSt[0]}
    wgate = {}
    for g in range(4):
        nm = "w1T" if g < 2 else "w2T"
        dstb = w1s if g < 2 else w2s
        sem = grp_sem[g]
        for j in range(4):
            kc = (g % 2) * 4 + j
            p.op("gpsimd", lambda e, k=kc, dstb=dstb, nm=nm: e.dma_start(
                out=dstb[:, k * DH:(k + 1) * DH], in_=din[nm][k * 128:(k + 1) * 128, :]),
                waits=(rel_ws if g == 0 and j == 0 else []), inc=(sem.h, 16))
            sem.bump(16)
        wgate[("w1" if g < 2 else "w2", (g % 2) * 4)] = (sem.h, sem.n)
    rel_x = [[], []]
    rel_pa = [[], []]
    rel_pb = [[], []]
    rel_t1 = [[], []]
    rel_h = [[], []]
    ti = 0
    for tt in range(8):
        sl = tt % 2
        xv = xt1[:, sl * 4096:sl * 4096 + 4096]
        p.op("sync", lambda e, t=tt, d=xv: e.dma_start(
            out=d.rearrange("p (k f) -> p k f", k=8),
            in_=xln1d[:, :, t * 512:t * 512 + 512].transpose([1, 0, 2])),
            waits=rel_x[sl], inc=(dL[sl].h, 16))
        dsn = dL[sl].bump(16)
        for oc in range(32):
            psl = ti % 2
            w0 = [(dL[sl].h, dsn)] + rel_pa[psl]
            for kc in range(8):
                wts = (w0 if kc == 0 else []) + (
                    [wgate[("w1", kc)]] if ti == 0 and kc in (0, 4) else [])
                p.op("tensor", lambda e, k=kc, o=oc, d=xv, ps=pa[psl]: e.matmul(
                    ps[:], w1s[:, k * DH + o * 128:k * DH + o * 128 + 128],
                    d[:, k * 512:k * 512 + 512], start=(k == 0), stop=(k == 7)),
                    waits=wts, inc=((smm.h, 1) if kc == 7 else None))
            mn1 = smm.bump()
            w0 = rel_pb[psl]
            for kc in range(8):
                wts = (w0 if kc == 0 else []) + (
                    [wgate[("w2", kc)]] if ti == 0 and kc in (0, 4) else [])
                p.op("tensor", lambda e, k=kc, o=oc, d=xv, ps=pb[psl]: e.matmul(
                    ps[:], w2s[:, k * DH + o * 128:k * DH + o * 128 + 128],
                    d[:, k * 512:k * 512 + 512], start=(k == 0), stop=(k == 7)),
                    waits=wts, inc=((smm.h, 1) if kc == 7 else None))
            mn2 = smm.bump()
            p.op("scalar", lambda e, o=oc, ps=pa[psl], d=t1f[:, psl * 512:psl * 512 + 512]:
                 e.activation(d, ps[:], AF.Silu, scale=s1s[:, o:o + 1]),
                 waits=[(smm.h, mn1)] + rel_t1[psl], inc=(sact.h, 1))
            an = sact.bump()
            rel_pa[psl] = [(sact.h, an)]
            p.op("vector", lambda e, o=oc, ps=pb[psl], t1=t1f[:, psl * 512:psl * 512 + 512],
                 d=hstg[:, psl * 512:psl * 512 + 512]:
                 e.scalar_tensor_tensor(d, ps[:], s2s[:, o:o + 1], t1, ALU.mult, ALU.mult),
                 waits=[(smm.h, mn2), (sact.h, an)] + rel_h[psl], inc=(svec.h, 1))
            vn = svec.bump()
            rel_pb[psl] = [(svec.h, vn)]
            rel_t1[psl] = [(svec.h, vn)]
            p.op("gpsimd", lambda e, t=tt, o=oc, d=hstg[:, psl * 512:psl * 512 + 512]:
                 e.dma_start(out=hd[t, o], in_=d), waits=[(svec.h, vn)], inc=(dSt[psl].h, 16))
            dSt[psl].bump(16)
            rel_h[psl] = [(dSt[psl].h, dSt[psl].n)]
            ti += 1
        rel_x[sl] = [(smm.h, smm.n)]
    p.emit()
    mem.free("w1s", "w2s", "f1_x", "f1_t1", "f1_h", "f1_pa0", "f1_pa1", "f1_pb0", "f1_pb1")

    for phase, (wnm, src_d, dst_d, hss, dst_kind) in enumerate(
            [("h1T", hd, g1d, hs1s, "tt"), ("h2T", g1d, g2d, hs2s, "tb")]):
        wg = mem.sbuf("f2_wg", [128, 48 * 1024], BF16)
        ht = mem.sbuf("f2_ht", [128, 2 * 32 * 512], BF16)
        gstg = mem.sbuf("f2_g", [128, 16 * 512], BF16)
        ps2 = [mem.psum("f2_p0", [128, 512], F32), mem.psum("f2_p1", [128, 512], F32)]
        p = Plan(nc, DMAS)
        half_sem = {}
        rel_half = {}
        grp_end = {}

        def load_half(half):
            slot = half % 3
            sem = [dG, dG2, dG3][slot]
            g, off = half // 2, (half % 2) * 16
            w = rel_half.get(slot, [])
            for j in range(16):
                kc = off + j
                p.op("gpsimd", lambda e, k=kc, g=g, j=j, slot=slot, nm=wnm: e.dma_start(
                    out=wg[:, (slot * 16 + j) * 1024:(slot * 16 + j) * 1024 + 1024],
                    in_=din[nm][k * 128:(k + 1) * 128, g * 1024:(g + 1) * 1024]),
                    waits=(w if j == 0 else []), inc=(sem.h, 16))
                sem.bump(16)
            half_sem[half] = (sem.h, sem.n)

        def wslice(g, kc):
            half = 2 * g + (1 if kc >= 16 else 0)
            slot = half % 3
            j = kc % 16
            c0 = (slot * 16 + j) * 1024
            return c0

        load_half(0)
        load_half(1)
        rel_ht = [[], []]
        rel_ps = [[], []]
        rel_g = [[], []]
        ht_tile = [-1, -1]
        ht_dsn = [0, 0]
        ti = 0
        for ocg in range(4):
            if ocg >= 1:
                rel_half[(2 * ocg + 1) % 3] = [(smm.h, grp_end[ocg - 1])]
                load_half(2 * ocg + 1)
            if ocg + 1 < 4:
                rel_half[(2 * ocg + 2) % 3] = ([(smm.h, grp_end[ocg - 1])]
                                               if ocg >= 1 else [])
                load_half(2 * ocg + 2)
            tts = list(range(8)) if ocg % 2 == 0 else list(range(7, -1, -1))
            for tt in tts:
                sl = tt % 2
                hv = ht[:, sl * 16384:sl * 16384 + 16384]
                if ht_tile[sl] != tt:
                    for q in range(4):
                        p.op("sync", lambda e, t=tt, q=q, d=hv, sd=src_d: e.dma_start(
                            out=d.rearrange("p (k f) -> p k f", k=32)[:, q * 8:(q + 1) * 8, :],
                            in_=sd[t, q * 8:(q + 1) * 8].transpose([1, 0, 2])),
                            waits=(rel_ht[sl] if q == 0 else []), inc=(dL[sl].h, 16))
                        dL[sl].bump(16)
                    ht_tile[sl] = tt
                    ht_dsn[sl] = dL[sl].n
                dsn = ht_dsn[sl]
                an = 0
                for oc in range(8):
                    psl = ti % 2
                    w0 = [half_sem[2 * ocg], (dL[sl].h, dsn)] + rel_ps[psl]
                    for kc in range(32):
                        wts = w0 if kc == 0 else ([half_sem[2 * ocg + 1]] if kc == 16 else [])
                        p.op("tensor", lambda e, k=kc, o=oc, d=hv, ps=ps2[psl],
                             c0=wslice(ocg, kc): e.matmul(
                                 ps[:], wg[:, c0 + o * 128:c0 + o * 128 + 128],
                                 d[:, k * 512:k * 512 + 512],
                                 start=(k == 0), stop=(k == 31)),
                             waits=wts, inc=((smm.h, 1) if kc == 31 else None))
                    mn = smm.bump()
                    gsl = gstg[:, (sl * 8 + oc) * 512:(sl * 8 + oc) * 512 + 512]
                    p.op("scalar", lambda e, o=oc, g=ocg, ps=ps2[psl], d=gsl, hss=hss:
                         e.activation(d, ps[:], AF.Gelu, scale=hss[:, g * 8 + o:g * 8 + o + 1]),
                         waits=[(smm.h, mn)] + (rel_g[sl] if oc == 0 else []),
                         inc=(sact.h, 1))
                    an = sact.bump()
                    rel_ps[psl] = [(sact.h, an)]
                    ti += 1
                for oc in range(8):
                    gsl = gstg[:, (sl * 8 + oc) * 512:(sl * 8 + oc) * 512 + 512]
                    if dst_kind == "tt":
                        p.op("gpsimd", lambda e, t=tt, o=ocg * 8 + oc, d=gsl, dd=dst_d:
                             e.dma_start(out=dd[t, o], in_=d),
                             waits=([(sact.h, an)] if oc == 0 else []), inc=(dSt[sl].h, 16))
                    else:
                        p.op("gpsimd", lambda e, t=tt, o=ocg * 8 + oc, d=gsl, dd=dst_d:
                             e.dma_start(out=dd[t * 4:(t + 1) * 4, o].transpose([1, 0, 2]),
                                         in_=d.rearrange("p (b f) -> p b f", b=4)),
                             waits=([(sact.h, an)] if oc == 0 else []), inc=(dSt[sl].h, 16))
                    dSt[sl].bump(16)
                rel_g[sl] = [(dSt[sl].h, dSt[sl].n)]
                rel_ht[sl] = [(smm.h, smm.n)]
            grp_end[ocg] = smm.n
        p.emit()
        mem.free("f2_wg", "f2_ht", "f2_g", "f2_p0", "f2_p1")

    w3s = mem.sbuf("w3s", [128, 32 * 1024], BF16)
    w3raw = mem.sbuf("f4_wr", [128, 2 * 1024], BF16)
    s3f = mem.sbuf("f4_s3f", [1, 1024], F32)
    s3b = mem.sbuf("f4_s3b", [1, 1024], BF16)
    ones1 = mem.sbuf("f4_ones", [1, 128], BF16)
    s3bc = mem.sbuf("f4_s3bc", [128, 1024], BF16)
    g2t = mem.sbuf("f4_g2", [128, 2 * 4096], BF16)
    x1t2 = mem.sbuf("f4_x1", [128, 2 * D], F32)
    xadj2 = mem.sbuf("f4_xa", [128, 2 * D], F32)
    outt2 = mem.sbuf("f4_o", [128, 2 * D], F32)
    pO = [mem.psum("f4_p0", [128, 512], F32), mem.psum("f4_p1", [128, 512], F32)]
    p = Plan(nc, DMAS)
    p.op("sync", lambda e: e.dma_start(out=s3f[:], in_=din["s3row"]), inc=(dL[0].h, 16))
    dsn = dL[0].bump(16)
    p.op("vector", lambda e: e.tensor_copy(s3b[:], s3f[:]), waits=[(dL[0].h, dsn)])
    p.op("vector", lambda e: e.memset(ones1[:], 1.0), inc=(svec.h, 1))
    vn0 = svec.bump()
    for o in range(2):
        p.op("tensor", lambda e, o=o: e.matmul(pO[o][:], ones1[:],
                                               s3b[0:1, o * 512:o * 512 + 512],
                                               start=True, stop=True),
             waits=[(svec.h, vn0)], inc=(smm.h, 1))
        mn = smm.bump()
        p.op("vector", lambda e, o=o: e.tensor_copy(s3bc[:, o * 512:o * 512 + 512], pO[o][:]),
             waits=[(smm.h, mn)], inc=(svec.h, 1))
        svec.bump()
    vbc = svec.n
    rel_wr = [[], []]
    wsem = [dG, dG2, dG3, dG]
    vw3b = []
    for b in range(4):
        sem = wsem[b]
        for j in range(8):
            c = b * 8 + j
            sl = c % 2
            p.op("gpsimd", lambda e, c=c, d=w3raw[:, (c % 2) * 1024:(c % 2) * 1024 + 1024]:
                 e.dma_start(out=d, in_=din["w3T"][c * 128:(c + 1) * 128, :]),
                 waits=rel_wr[sl], inc=(sem.h, 16))
            dgn = sem.bump(16)
            p.op("vector", lambda e, c=c, s=w3raw[:, (c % 2) * 1024:(c % 2) * 1024 + 1024]:
                 e.tensor_tensor(w3s[:, c * 1024:(c + 1) * 1024], s, s3bc[:], ALU.mult),
                 waits=[(sem.h, dgn), (svec.h, vbc)], inc=(svec.h, 1))
            rel_wr[sl] = [(svec.h, svec.bump())]
        vw3b.append(svec.n)
    vw3 = svec.n
    ff = mem.sbuf("f4_ff", [128, 16 * 1024], F32)
    rel_g2 = [[], []]
    rel_x1 = [[], []]
    rel_pO = [[], [], [], []]
    rel_out = [[], []]
    pOs = [pO[0], pO[1], mem.psum("f4_p2", [128, 512], F32),
           mem.psum("f4_p3", [128, 512], F32)]
    pi = 0
    rel_ff = [[], []]
    for half in range(4):
        hb = half * 8
        fo = (half % 2) * 8
        for tbl in range(8):
            tb = hb + tbl
            sl = tb % 2
            gv = g2t[:, sl * 4096:sl * 4096 + 4096]
            ffs = ff[:, (fo + tbl) * 1024:(fo + tbl) * 1024 + 1024]
            for q in range(4):
                p.op("sync", lambda e, t=tb, q=q, d=gv: e.dma_start(
                    out=d.rearrange("p (k f) -> p k f", k=32)[:, q * 8:(q + 1) * 8, :],
                    in_=g2d[t, q * 8:(q + 1) * 8].transpose([1, 0, 2])),
                    waits=(rel_g2[sl] if q == 0 else []), inc=(dL[sl].h, 16))
                dL[sl].bump(16)
            dsn = dL[sl].n
            for ot in range(2):
                ps = pOs[pi % 4]
                w0 = [(dL[sl].h, dsn), (svec.h, vw3b[0])] + rel_pO[pi % 4]
                for dc in range(32):
                    wts = (w0 if dc == 0 else
                           ([(svec.h, vw3b[dc // 8])] if dc % 8 == 0 else []))
                    p.op("tensor", lambda e, d=dc, o=ot, g=gv, ps=ps: e.matmul(
                        ps[:], g[:, d * 128:d * 128 + 128],
                        w3s[:, d * 1024 + o * 512:d * 1024 + o * 512 + 512],
                        start=(d == 0), stop=(d == 31)),
                        waits=wts, inc=((smm.h, 1) if dc == 31 else None))
                mn = smm.bump()
                p.op("scalar", lambda e, o=ot, d=ffs, ps=ps: e.activation(
                    d[:, o * 512:o * 512 + 512], ps[:], AF.Identity),
                    waits=[(smm.h, mn)] + (rel_ff[half % 2] if tbl == 0 and ot == 0 else []),
                    inc=(sact.h, 1))
                rel_pO[pi % 4] = [(sact.h, sact.bump())]
                pi += 1
            rel_g2[sl] = [(smm.h, smm.n)]
            evn = sact.n
            p.op("vector", lambda e, d=ffs, t=fo + tbl: e.reduce_sum(
                bc(B_SUM, t), d, axis=AX.X),
                waits=[(sact.h, evn)], inc=(svec.h, 1))
            svec.bump()
            p.op("scalar", lambda e, d=ffs, t=fo + tbl: e.activation(
                scrap[:, (t % 2) * D:(t % 2) * D + D], d, AF.Square,
                accum_out=bc(B_SQ, t)),
                waits=[(sact.h, evn)], inc=(sact.h, 1))
            sact.bump()
        ln_batch(p, [(svec.h, svec.n), (sact.h, sact.n)], with_amr=True)
        rn = svec2.n
        for tbl in range(8):
            tb = hb + tbl
            sl = tb % 2
            x1t = x1t2[:, sl * D:sl * D + D]
            outt = outt2[:, sl * D:sl * D + D]
            xadj = xadj2[:, sl * D:sl * D + D]
            ffs = ff[:, (fo + tbl) * 1024:(fo + tbl) * 1024 + 1024]
            p.op("gpsimd", lambda e, t=tb, d=x1t: e.dma_start(out=d, in_=x1r[t]),
                 waits=rel_x1[sl], inc=(dG2.h, 16))
            dsx = dG2.bump(16)
            p.op("vector", lambda e, d=x1t, a=xadj, t=fo + tbl: e.tensor_scalar(
                a, d, bc(B_AMR, t), None, ALU.add),
                waits=[(dG2.h, dsx), (svec2.h, rn)], inc=(svec.h, 1))
            xan = svec.bump()
            rel_x1[sl] = [(svec.h, xan)]
            p.op("vector", lambda e, d=outt, a=xadj, s=ffs, t=fo + tbl:
                 e.scalar_tensor_tensor(d, s, bc(B_R, t), a, ALU.mult, ALU.add),
                 waits=[(svec.h, xan)] + rel_out[sl], inc=(svec.h, 1))
            vstt = svec.bump()
            p.op("gpsimd", lambda e, t=tb, d=outt: e.dma_start(out=outr[t], in_=d),
                 waits=[(svec.h, vstt)], inc=(dSt[sl].h, 16))
            dSt[sl].bump(16)
            rel_out[sl] = [(dSt[sl].h, dSt[sl].n), (svec.h, vstt)]
        rel_ff[half % 2] = [(svec.h, svec.n)]
    p.op("gpsimd", lambda e: e.nop(), waits=[(dSt[0].h, dSt[0].n), (dSt[1].h, dSt[1].n)])
    p.emit()
    mem.free("w3s", "f4_wr", "f4_s3f", "f4_s3b", "f4_ones", "f4_s3bc", "f4_g2",
             "f4_ff", "f4_x1", "f4_xa", "f4_o", "f4_p0", "f4_p1", "f4_p2", "f4_p3")

    for g in sem_guards:
        g.__exit__(None, None, None)
    mem.free(*list(mem.guards))
    return nc


_NC_CACHE = None


def kernel(**inputs):
    global _NC_CACHE
    f32c = lambda a: np.ascontiguousarray(np.asarray(a, dtype=np.float32))
    x = f32c(inputs["L0_attn_x"])
    wqT = f32c(np.asarray(inputs["L0_attn_wq"]).T)
    wkT = f32c(np.asarray(inputs["L0_attn_wk"]).T)
    wvT = f32c(np.asarray(inputs["L0_attn_wv"]).T)
    woT = f32c(np.asarray(inputs["L0_attn_wo"]).T)
    w1T = f32c(np.asarray(inputs["L0_ffn_w1_w"]).T)
    w2T = f32c(np.asarray(inputs["L0_ffn_w2_w"]).T)
    h1T = f32c(np.asarray(inputs["L0_ffn_h1_w"]).T)
    h2T = f32c(np.asarray(inputs["L0_ffn_h2_w"]).T)
    w3T = f32c(np.asarray(inputs["L0_ffn_w3_w"]).T)
    s1r = f32c(np.asarray(inputs["L0_ffn_w1_s"]).reshape(32, 128).T)
    s2r = f32c(np.asarray(inputs["L0_ffn_w2_s"]).reshape(32, 128).T)
    hs1r = f32c(np.asarray(inputs["L0_ffn_h1_s"]).reshape(32, 128).T)
    hs2r = f32c(np.asarray(inputs["L0_ffn_h2_s"]).reshape(32, 128).T)
    s3row = f32c(np.asarray(inputs["L0_ffn_w3_s"]).reshape(1, 1024))

    j = (np.arange(128) % 64) // 2
    inv = np.power(10000.0, -2.0 * j.astype(np.float64) / 64.0)
    ang = inv[:, None] * np.arange(S, dtype=np.float64)[None, :]
    cosT = np.cos(ang).astype(np.float32)
    sinT = np.sin(ang).astype(np.float32)
    pswap = np.zeros((128, 128), np.float32)
    idx = np.arange(0, 128, 2)
    pswap[idx + 1, idx] = -1.0
    pswap[idx, idx + 1] = 1.0
    identc = np.eye(128, dtype=np.float32)
    maskc = (np.arange(128)[:, None] <= np.arange(128)[None, :]).astype(np.float32)

    if _NC_CACHE is None:
        _NC_CACHE = build_program()
    nc = _NC_CACHE

    shared = dict(wqT=wqT, wkT=wkT, wvT=wvT, woT=woT, w1T=w1T, w2T=w2T,
                  h1T=h1T, h2T=h2T, w3T=w3T, s1r=s1r, s2r=s2r, hs1r=hs1r,
                  hs2r=hs2r, s3row=s3row, cosT=cosT, sinT=sinT, pswap=pswap,
                  identc=identc, maskc=maskc)
    in_maps = []
    for c in range(8):
        m = dict(shared)
        m["x"] = np.ascontiguousarray(x[2 * c:2 * c + 2].reshape(T, D))
        in_maps.append(m)
    res = run_bass_kernel_spmd(nc, in_maps, core_ids=list(range(8)))
    out = np.concatenate(
        [np.asarray(res.results[c]["out"], np.float32).reshape(2, S, D) for c in range(8)],
        axis=0)
    return out


# version 23
# speedup vs baseline: 1.0929x; 1.0229x over previous
import numpy as np
import concourse.bass as bass
import concourse.mybir as mybir
from concourse.bass_utils import run_bass_kernel_spmd

F32 = mybir.dt.float32
BF16 = mybir.dt.bfloat16
AF = mybir.ActivationFunctionType
ALU = mybir.AluOpType
AX = mybir.AxisListType

D = 1024
S = 2048
NSEQ = 2
T = NSEQ * S
H = 16
DH = 4096
NTB = S // 128
C_SHIFT = 10.0

ENGS = ("sync", "gpsimd", "tensor", "scalar", "vector")

C_NEGC, C_S8, C_EPS, C_INV1024, C_NEG1, C_NEGHALF = range(6)
CONST_VALS = [-C_SHIFT, 0.125, 1e-5, 1.0 / 1024.0, -1.0, -0.5]

(ST_SUM, ST_SQ0, ST_SQ1, ST_M, ST_M2, ST_V, ST_R, ST_NEGM, ST_MR, ST_AMR,
 ST_T0, ST_T1, ST_RZ, ST_T2, ST_T3) = range(15)


class Ctr:
    def __init__(self, h):
        self.h = h
        self.n = 0

    def bump(self, k=1):
        self.n += k
        return self.n


PLAN_LIMIT = None
DEBUG_DRAM = False
DEBUG_DUMP = None


class Plan:

    IDX = 0

    def __init__(self, nc, dmas=None):
        self.nc = nc
        self.dmas = dmas or {}
        self.ops = {e: [] for e in ENGS}

    def op(self, eng, fn, waits=(), inc=None):
        self.ops[eng].append((list(waits), fn, inc))

    def emit(self):
        Plan.IDX += 1
        if PLAN_LIMIT is not None and Plan.IDX > PLAN_LIMIT:
            return
        for eng, ctrs in self.dmas.items():
            if self.ops[eng]:
                self.op(eng, lambda e: e.nop(),
                        waits=[(c.h, c.n) for c in ctrs if c.n > 0])

        def mk(ops):
            def run(e):
                for waits, fn, inc in ops:
                    for sem, val in waits:
                        if val > 0:
                            e.wait_ge(sem, val)
                    inst = fn(e)
                    if inc is not None:
                        inst.then_inc(inc[0], inc[1])
            return run

        with self.nc.Block() as block:
            for eng in ENGS:
                if self.ops[eng]:
                    getattr(block, eng)(mk(self.ops[eng]))


class Mem:

    def __init__(self, nc):
        self.nc = nc
        self.guards = {}
        self.order = {}
        self.seq = 0

    def _reg(self, name, g):
        h = g.__enter__()
        self.guards[name] = g
        self.order[name] = self.seq
        self.seq += 1
        return h

    def sbuf(self, name, shape, dt, side=None):
        assert name not in self.guards
        return self._reg(name, self.nc.sbuf_tensor(f"{name}_u{self.seq}", shape, dt, side=side))

    def init_psum_pool(self):
        g = []
        for nm, shape, dt in [("psA", [128, 1024], F32), ("psB", [128, 1024], F32),
                              ("psC", [128, 512], F32), ("psD", [128, 512], F32),
                              ("psT0", [128, 512], BF16), ("psT1", [128, 512], BF16)]:
            cm = self.nc.psum_tensor(nm, shape, dt)
            g.append(cm.__enter__())
        pA, pB, pC, pD, pT0, pT1 = g
        self.pool = {
            "1024F": [pA[:], pB[:]],
            "512F": [pC[:], pD[:], pA[:, 0:512], pA[:, 512:1024],
                     pB[:, 0:512], pB[:, 512:1024]],
            "128B": [pT0[:], pT1[:]],
            "512B": [pT0[:], pT1[:]],
        }
        self.pool_free = {k: list(range(len(v))) for k, v in self.pool.items()}
        self.pool_named = {}

    def psum(self, name, shape, dt=F32):
        key = {(1024, 0): "1024F", (512, 0): "512F",
               (128, 1): "128B", (512, 1): "512B"}[(shape[1], 1 if dt == BF16 else 0)]
        idx = self.pool_free[key].pop(0)
        self.pool_named[name] = (key, idx)
        return self.pool[key][idx]

    def free(self, *names):
        ps = [n for n in names if n in self.pool_named]
        sb = [n for n in names if n not in self.pool_named]
        for n in ps:
            key, idx = self.pool_named.pop(n)
            self.pool_free[key].append(idx)
            self.pool_free[key].sort()
        for n in sorted(sb, key=lambda n: -self.order[n]):
            self.guards.pop(n).__exit__(None, None, None)
            del self.order[n]


BASS_KW = {}


def build_program():
    Plan.IDX = 0
    nc = bass.Bass("TRN2", target_bir_lowering=False, **BASS_KW)

    din = {}
    for name, shape in [
        ("x", [T, D]), ("wqT", [D, D]), ("wkT", [D, D]), ("wvT", [D, D]),
        ("woT", [D, D]), ("w1T", [D, DH]), ("w2T", [D, DH]), ("h1T", [DH, DH]),
        ("h2T", [DH, DH]), ("w3T", [DH, D]), ("s1r", [128, 32]), ("s2r", [128, 32]),
        ("hs1r", [128, 32]), ("hs2r", [128, 32]), ("s3row", [1, D]),
        ("cosT", [128, S]), ("sinT", [128, S]), ("pswap", [128, 128]),
        ("identc", [128, 128]), ("maskc", [128, 128]),
    ]:
        din[name] = nc.dram_tensor(name, shape, F32, kind="ExternalInput").ap()

    out_d = nc.dram_tensor("out", [T, D], F32, kind="ExternalOutput").ap()
    scratch_kind = "ExternalOutput" if DEBUG_DRAM else "Internal"
    x1d = nc.dram_tensor("x1d", [T, D], F32, kind=scratch_kind).ap()
    xln1d = nc.dram_tensor("xln1d", [8, 128, T], BF16, kind=scratch_kind).ap()
    hd = nc.dram_tensor("hd", [8, 32, 128, 512], BF16, kind=scratch_kind).ap()
    g1d = nc.dram_tensor("g1d", [8, 32, 128, 512], BF16, kind=scratch_kind).ap()
    g2d = nc.dram_tensor("g2d", [32, 32, 128, 128], BF16, kind=scratch_kind).ap()

    dbg = (nc.dram_tensor("dbg", [128, 16640], BF16, kind="ExternalOutput").ap()
           if DEBUG_DUMP else None)

    xr = din["x"].rearrange("(tb p) d -> tb p d", p=128)
    x1r = x1d.rearrange("(tb p) d -> tb p d", p=128)
    outr = out_d.rearrange("(tb p) d -> tb p d", p=128)

    mem = Mem(nc)
    mem.init_psum_pool()
    sem_guards = []
    sems = {}
    for nm in ("dG", "dG2", "dG3", "dL0", "dL1", "dS0", "dS1", "mm", "mm2", "act", "vec", "vec2"):
        g = nc.semaphore(nm)
        sems[nm] = Ctr(g.__enter__())
        sem_guards.append(g)
    dG = sems["dG"]
    dG2 = sems["dG2"]
    dG3 = sems["dG3"]
    dL = [sems["dL0"], sems["dL1"]]
    dSt = [sems["dS0"], sems["dS1"]]
    smm, smm2, sact, svec, svec2 = (sems[k] for k in ("mm", "mm2", "act", "vec", "vec2"))
    DMAS = {"sync": [dL[0], dL[1]], "gpsimd": [dG, dG2, dG3, dSt[0], dSt[1]]}

    consts = mem.sbuf("consts", [128, 8], F32)
    identb = mem.sbuf("identb", [128, 128], BF16)
    pswapb = mem.sbuf("pswapb", [128, 128], BF16)
    maskb = mem.sbuf("maskb", [128, 128], BF16)
    cosf = mem.sbuf("cosf", [128, S], F32, side="right")
    sinf = mem.sbuf("sinf", [128, S], F32, side="right")
    cosb = mem.sbuf("cosb", [128, S], BF16, side="right")
    s1s = mem.sbuf("s1s", [128, 32], F32)
    s2s = mem.sbuf("s2s", [128, 32], F32)
    hs1s = mem.sbuf("hs1s", [128, 32], F32)
    hs2s = mem.sbuf("hs2s", [128, 32], F32)
    scrap = mem.sbuf("scrap", [128, 2048], BF16)
    stt = mem.sbuf("stt", [128, 48], F32)
    bst = mem.sbuf("bst", [128, 8 * 16], F32)

    def st(i, par=0):
        return stt[:, par * 16 + i:par * 16 + i + 1]

    (B_SUM, B_SQ, B_M, B_M2, B_V, B_R, B_MR, B_AMR) = range(8)

    def bs(i):
        return bst[:, i * 16:(i + 1) * 16]

    def bc(i, tb):
        return bst[:, i * 16 + tb:i * 16 + tb + 1]

    def ln_batch(p, pre_wait, with_amr=False):
        def step(eng, fn, extra=()):
            p.op(eng, fn, waits=[(svec2.h, svec2.n)] + list(extra), inc=(svec2.h, 1))
            svec2.bump()

        step("vector", lambda e: e.tensor_scalar(bs(B_M), bs(B_SUM), cc(C_INV1024),
                                                 None, ALU.mult), extra=list(pre_wait))
        step("vector", lambda e: e.tensor_tensor(bs(B_M2), bs(B_M), bs(B_M), ALU.mult))
        step("vector", lambda e: e.tensor_scalar(bs(B_V), bs(B_SQ), cc(C_INV1024),
                                                 None, ALU.mult))
        step("vector", lambda e: e.tensor_tensor(bs(B_V), bs(B_V), bs(B_M2), ALU.subtract))
        step("scalar", lambda e: e.activation(bs(B_V), bs(B_V), AF.Ln, bias=cc(C_EPS)))
        step("scalar", lambda e: e.activation(bs(B_R), bs(B_V), AF.Exp, scale=cc(C_NEGHALF)))
        if with_amr:
            step("vector", lambda e: e.tensor_tensor(bs(B_MR), bs(B_M), bs(B_R), ALU.mult))
            step("vector", lambda e: e.tensor_scalar(bs(B_AMR), bs(B_MR), cc(C_NEG1),
                                                     None, ALU.mult))
        return svec2.n

    def cc(i):
        return consts[:, i:i + 1]

    p = Plan(nc, DMAS)
    for i, v in enumerate(CONST_VALS):
        p.op("vector", lambda e, i=i, v=v: e.memset(consts[:, i:i + 1], v))
    for dst, src in [(identb, din["identc"]), (pswapb, din["pswap"]),
                     (maskb, din["maskc"])]:
        p.op("gpsimd", lambda e, d=dst, s=src: e.dma_start(out=d[:], in_=s),
             inc=(dG.h, 16))
        dG.bump(16)
    for dst, src in [(cosf, din["cosT"]), (sinf, din["sinT"]), (s1s, din["s1r"]),
                     (s2s, din["s2r"]), (hs1s, din["hs1r"]), (hs2s, din["hs2r"])]:
        p.op("sync", lambda e, d=dst, s=src: e.dma_start(out=d[:], in_=s),
             inc=(dL[0].h, 16))
        dL[0].bump(16)
    p.op("vector", lambda e: e.tensor_copy(cosb[:], cosf[:]),
         waits=[(dL[0].h, dL[0].n)])
    p.emit()

    def ln_tail(p, sq_wait, two_sq=False, par=0):
        def step(eng, fn, extra=()):
            p.op(eng, fn, waits=[(svec2.h, svec2.n)] + list(extra), inc=(svec2.h, 1))
            svec2.bump()

        if two_sq:
            step("vector", lambda e, par=par: e.tensor_tensor(
                st(ST_T0, par), st(ST_SQ0, par), st(ST_SQ1, par), ALU.add), extra=[sq_wait])
            step("vector", lambda e, par=par: e.tensor_scalar(
                st(ST_T0, par), st(ST_T0, par), cc(C_INV1024), None, ALU.mult))
        else:
            step("vector", lambda e, par=par: e.tensor_scalar(
                st(ST_T0, par), st(ST_SQ0, par), cc(C_INV1024), None, ALU.mult),
                extra=[sq_wait])
        step("vector", lambda e, par=par: e.tensor_scalar(
            st(ST_M, par), st(ST_SUM, par), cc(C_INV1024), None, ALU.mult))
        step("vector", lambda e, par=par: e.tensor_tensor(
            st(ST_M2, par), st(ST_M, par), st(ST_M, par), ALU.mult))
        step("vector", lambda e, par=par: e.tensor_tensor(
            st(ST_V, par), st(ST_T0, par), st(ST_M2, par), ALU.subtract))
        step("scalar", lambda e, par=par: e.activation(
            st(ST_T1, par), st(ST_V, par), AF.Ln, bias=cc(C_EPS)))
        step("scalar", lambda e, par=par: e.activation(
            st(ST_R, par), st(ST_T1, par), AF.Exp, scale=cc(C_NEGHALF)))
        return svec2.n

    def emit_dump(buf, cols):
        p = Plan(nc, DMAS)
        for c0 in range(0, cols, 4096):
            cw = min(4096, cols - c0)
            p.op("gpsimd", lambda e, c0=c0, cw=cw: e.dma_start(
                out=dbg[:, c0:c0 + cw], in_=buf[:, c0:c0 + cw]), inc=(dSt[0].h, 16))
            dSt[0].bump(16)
        p.emit()

    rel_ws = []
    for seq in range(NSEQ):
        tb0 = seq * NTB

        xln0 = mem.sbuf(f"xln0_{seq}", [128, 8 * S], BF16)
        xt2 = mem.sbuf("a1_xt", [128, 2 * D], F32)
        xtm2 = mem.sbuf("a1_tm", [128, 2 * D], BF16)
        psT = [mem.psum("a1_ps0", [128, 512], BF16), mem.psum("a1_ps1", [128, 512], BF16)]
        p = Plan(nc, DMAS)
        rel_xt = [[], []]
        for tb in range(NTB):
            sl = tb % 2
            xt = xt2[:, sl * D:sl * D + D]
            p.op("sync", lambda e, t=tb, d=xt: e.dma_start(out=d, in_=xr[tb0 + t]),
                 waits=rel_xt[sl], inc=(dL[sl].h, 16))
            dsn = dL[sl].bump(16)
            p.op("vector", lambda e, d=xt, t=tb: e.reduce_sum(bc(B_SUM, t), d, axis=AX.X),
                 waits=[(dL[sl].h, dsn)], inc=(svec.h, 1))
            vr = svec.bump()
            p.op("scalar", lambda e, d=xt, t=tb: e.activation(
                scrap[:, (t % 2) * D:(t % 2) * D + D], d, AF.Square,
                accum_out=bc(B_SQ, t)),
                waits=[(dL[sl].h, dsn)], inc=(sact.h, 1))
            an = sact.bump()
            rel_xt[sl] = [(svec.h, vr), (sact.h, an)]
        rn = ln_batch(p, [(svec.h, svec.n), (sact.h, sact.n)])
        rel_xt_a = [list(rel_xt[0]), list(rel_xt[1])]
        rel_xt = [[], []]
        rel_psT = [[], []]
        rel_xtm = [[], []]
        for tb in range(NTB):
            sl = tb % 2
            xt = xt2[:, sl * D:sl * D + D]
            xtm = xtm2[:, sl * D:sl * D + D]
            p.op("sync", lambda e, t=tb, d=xt: e.dma_start(out=d, in_=xr[tb0 + t]),
                 waits=rel_xt[sl] + (rel_xt_a[sl] if tb < 2 else []), inc=(dL[sl].h, 16))
            dsn = dL[sl].bump(16)
            p.op("vector", lambda e, d=xt, m=xtm, t=tb: e.tensor_scalar(
                m, d, bc(B_M, t), bc(B_R, t), ALU.subtract, ALU.mult),
                waits=[(dL[sl].h, dsn), (svec2.h, rn)] + rel_xtm[sl], inc=(svec.h, 1))
            xn = svec.bump()
            rel_xt[sl] = [(svec.h, xn)]
            last_mn = 0
            for g in range(2):
                for c4 in range(4):
                    c = g * 4 + c4
                    w = ([(svec.h, xn)] if c4 == 0 else []) + (rel_psT[g] if c4 == 0 else [])
                    p.op("tensor", lambda e, c=c, c4=c4, m=xtm, g=g: e.transpose(
                        psT[g][:, c4 * 128:c4 * 128 + 128],
                        m[:, c * 128:c * 128 + 128], identb[:]),
                        waits=w, inc=((smm.h, 1) if c4 == 3 else None))
                last_mn = smm.bump()
                dv = xln0[:].rearrange("p (c t) -> p c t", c=8)
                p.op("vector", lambda e, g=g, t=tb, dv=dv: e.tensor_copy(
                    dv[:, g * 4:(g + 1) * 4, t * 128:t * 128 + 128], psT[g][:]),
                    waits=[(smm.h, last_mn)], inc=(svec2.h, 1))
                rel_psT[g] = [(svec2.h, svec2.bump())]
            rel_xtm[sl] = [(smm.h, last_mn)]
        p.emit()
        if seq == 0 and DEBUG_DUMP == "xln0":
            emit_dump(xln0, 8 * S)
        mem.free("a1_xt", "a1_tm", "a1_ps0", "a1_ps1")

        Qb = mem.sbuf(f"Qb_{seq}", [128, 8 * S], BF16, side="right")
        Kb = mem.sbuf(f"Kb_{seq}", [128, 8 * S], BF16, side="right")
        Va = mem.sbuf(f"Va_{seq}", [128, 16 * 1040], BF16, side="right")
        ws2 = mem.sbuf("a2_ws", [128, 2 * 8 * 1024], BF16)
        tm1b = mem.sbuf("a2_t1", [128, 3 * 512], BF16)
        tm2b = mem.sbuf("a2_t2", [128, 3 * 512], BF16)
        pa = [mem.psum("a2_pa0", [128, 512], F32), mem.psum("a2_pa1", [128, 512], F32)]
        pb = [mem.psum("a2_pb0", [128, 512], F32), mem.psum("a2_pb1", [128, 512], F32)]
        p = Plan(nc, DMAS)
        wsem = [dG, dG2, dG3]
        wn = {}
        rel_wstage = [[], []]

        def load_w(wi, nm):
            sem = wsem[wi]
            wst = ws2[:, (wi % 2) * 8192:(wi % 2) * 8192 + 8192]
            for kc in range(8):
                p.op("gpsimd", lambda e, k=kc, nm=nm, w=wst: e.dma_start(
                    out=w[:, k * 1024:(k + 1) * 1024],
                    in_=din[nm][k * 128:(k + 1) * 128, :]),
                    waits=((rel_ws + rel_wstage[wi % 2]) if kc == 0 else []),
                    inc=(sem.h, 16))
                sem.bump(16)
            wn[wi] = (sem.h, sem.n)

        load_w(0, "wqT")
        load_w(1, "wkT")
        for wi, buf in ((0, Qb), (1, Kb)):
            ws = ws2[:, (wi % 2) * 8192:(wi % 2) * 8192 + 8192]
            rel_pa = [[], []]
            rel_pb = [[], []]
            ti = 0
            pend = []

            def emit_rot(item, ws=ws):
                tj, dstj, ttj, anj = item
                tm1 = tm1b[:, (tj % 3) * 512:(tj % 3) * 512 + 512]
                tm2 = tm2b[:, (tj % 3) * 512:(tj % 3) * 512 + 512]
                pbs = pb[tj % 2]
                p.op("tensor", lambda e, d=dstj, pbs=pbs: e.matmul(pbs[:], pswapb[:], d,
                                                                   start=True, stop=True),
                     waits=[(sact.h, anj)] + rel_pb[tj % 2], inc=(smm2.h, 1))
                m2 = smm2.bump()
                p.op("vector", lambda e, t=ttj, pbs=pbs, tm2=tm2: e.tensor_tensor(
                    tm2, pbs[:], sinf[:, t * 512:t * 512 + 512], ALU.mult),
                    waits=[(smm2.h, m2)], inc=(svec.h, 1))
                vt2 = svec.bump()
                rel_pb[tj % 2] = [(svec.h, vt2)]
                p.op("vector", lambda e, d=dstj, tm1=tm1, tm2=tm2: e.tensor_tensor(
                    d, tm1, tm2, ALU.add),
                    waits=[(svec.h, vt2)], inc=(svec.h, 1))
                svec.bump()

            for oc in range(8):
                for tt in range(4):
                    dst = buf[:, oc * S + tt * 512:oc * S + tt * 512 + 512]
                    ps = pa[ti % 2]
                    w0 = [wn[wi]] + rel_pa[ti % 2]
                    for kc in range(8):
                        p.op("tensor", lambda e, k=kc, o=oc, t=tt, ps=ps, ws=ws: e.matmul(
                            ps[:], ws[:, k * 1024 + o * 128:k * 1024 + o * 128 + 128],
                            xln0[:, k * S + t * 512:k * S + t * 512 + 512],
                            start=(k == 0), stop=(k == 7)),
                            waits=(w0 if kc == 0 else []),
                            inc=((smm.h, 1) if kc == 7 else None))
                    mn = smm.bump()
                    p.op("scalar", lambda e, d=dst, ps=ps: e.activation(d, ps[:], AF.Identity),
                         waits=[(smm.h, mn)], inc=(sact.h, 1))
                    an = sact.bump()
                    tm1 = tm1b[:, (ti % 3) * 512:(ti % 3) * 512 + 512]
                    p.op("vector", lambda e, t=tt, ps=ps, tm1=tm1: e.tensor_tensor(
                        tm1, ps[:], cosf[:, t * 512:t * 512 + 512], ALU.mult),
                        waits=[(sact.h, an)], inc=(svec.h, 1))
                    v1 = svec.bump()
                    rel_pa[ti % 2] = [(sact.h, an), (svec.h, v1)]
                    pend.append((ti, dst, tt, an))
                    if len(pend) > 1:
                        emit_rot(pend.pop(0))
                    ti += 1
            while pend:
                emit_rot(pend.pop(0))
            rel_wstage[wi % 2] = [(smm.h, smm.n)]
            if wi == 0:
                load_w(2, "wvT")

        ws = ws2[:, 0:8192]
        p.op("vector", lambda e: e.memset(Va[:, 64::65], 1.0))
        vav = Va.ap().rearrange("p (tb h c) -> p tb h c", tb=16, h=16, c=65)
        rel_pa = [[], []]
        ti = 0
        for tb in range(NTB):
            for ot in range(2):
                ps = pa[ti % 2]
                w0 = [wn[2]] + rel_pa[ti % 2]
                for kc in range(8):
                    p.op("tensor", lambda e, k=kc, t=tb, o=ot, ps=ps, ws=ws: e.matmul(
                        ps[:], xln0[:, k * S + t * 128:t * 128 + k * S + 128],
                        ws[:, k * 1024 + o * 512:k * 1024 + o * 512 + 512],
                        start=(k == 0), stop=(k == 7)),
                        waits=(w0 if kc == 0 else []),
                        inc=((smm.h, 1) if kc == 7 else None))
                mn = smm.bump()
                p.op("vector", lambda e, t=tb, o=ot, ps=ps: e.tensor_copy(
                    vav[:, t, o * 8:(o + 1) * 8, 0:64],
                    ps.rearrange("p (h c) -> p h c", h=8)),
                    waits=[(smm.h, mn)], inc=(svec.h, 1))
                rel_pa[ti % 2] = [(svec.h, svec.bump())]
                ti += 1
        rel_ws = [(smm.h, smm.n)]
        p.emit()
        if seq == 0 and DEBUG_DUMP in ("Qb", "Kb", "Va"):
            emit_dump({"Qb": Qb, "Kb": Kb, "Va": Va}[DEBUG_DUMP],
                      8 * S if DEBUG_DUMP != "Va" else 16 * 1040)
        mem.free(f"xln0_{seq}", "a2_ws", "a2_t1", "a2_t2", "a2_pa0", "a2_pa1",
                 "a2_pb0", "a2_pb1")

        hctm = mem.sbuf(f"hctm_{seq}", [128, 16 * 1024], BF16)
        eb = mem.sbuf("a3_e", [128, 3 * S], BF16)
        scs = [mem.psum("a3_sc0", [128, 1024], F32), mem.psum("a3_sc1", [128, 1024], F32)]
        pvb = [mem.psum("a3_pv0", [128, 512], F32), mem.psum("a3_pv1", [128, 512], F32)]
        pvs = [pvb[0][:, 0:65], pvb[1][:, 0:65]]
        p = Plan(nc, DMAS)
        iters = [(h, qb) for h in range(H) for qb in range(NTB)]
        NIT = len(iters)
        rel_sc = [[], []]
        rel_pv = [[], []]
        rel_e = [[], [], []]
        mg = {}
        ag = {}
        vm = {}
        g0_of = {}
        gctr = [0]

        def grp_of(qb):
            return [list(range(g, min(g + 8, qb + 1))) for g in range(0, qb + 1, 8)]

        def plan_scores(it):
            h, qb = iters[it]
            hp, hr = h // 2, (h % 2) * 64
            q0 = hp * S + qb * 128
            g0_of[it] = gctr[0]
            mg[it] = []
            for kbg in grp_of(qb):
                ms = scs[gctr[0] % 2]
                w0 = list(rel_sc[gctr[0] % 2])
                for j, kb in enumerate(kbg):
                    p.op("tensor", lambda e, kb=kb, j=j, ms=ms, hr=hr, hp=hp, q0=q0: e.matmul(
                        ms[:, j * 128:j * 128 + 128],
                        Kb[hr:hr + 64, hp * S + kb * 128:hp * S + kb * 128 + 128],
                        Qb[hr:hr + 64, q0:q0 + 128],
                        start=True, stop=True),
                        waits=(w0 if j == 0 else []),
                        inc=((smm.h, 1) if j == len(kbg) - 1 else None))
                mg[it].append(smm.bump())
                gctr[0] += 1

        def plan_exps(it):
            h, qb = iters[it]
            e0 = (it % 3) * S
            groups = grp_of(qb)
            for gi, kbg in enumerate(groups):
                ln = len(kbg)
                ms = scs[(g0_of[it] + gi) % 2]
                base = e0 + kbg[0] * 128
                wts = [(smm.h, mg[it][gi])] + (list(rel_e[it % 3]) if gi == 0 else [])
                p.op("scalar", lambda e, b=base, ln=ln, ms=ms: e.activation(
                    eb[:, b:b + ln * 128], ms[:, 0:ln * 128],
                    AF.Exp, bias=cc(C_NEGC), scale=cc(C_S8)),
                    waits=wts, inc=(sact.h, 1))
                rel_sc[(g0_of[it] + gi) % 2] = [(sact.h, sact.bump())]
            ag[it] = sact.n

        def plan_mask(it):
            h, qb = iters[it]
            e0 = (it % 3) * S
            p.op("vector", lambda e, qb=qb, e0=e0: e.tensor_tensor(
                eb[:, e0 + qb * 128:e0 + qb * 128 + 128],
                eb[:, e0 + qb * 128:e0 + qb * 128 + 128], maskb[:], ALU.mult),
                waits=[(sact.h, ag[it])], inc=(svec.h, 1))
            vm[it] = svec.bump()

        def plan_pv(it):
            h, qb = iters[it]
            e0 = (it % 3) * S
            pv = pvs[it % 2]
            w0 = [(sact.h, ag[it])] + rel_pv[it % 2]
            for kb in range(qb + 1):
                wts = (w0 if kb == 0 else []) + ([(svec.h, vm[it])] if kb == qb else [])
                p.op("tensor", lambda e, kb=kb, pv=pv, h=h, qb=qb, e0=e0: e.matmul(
                    pv, eb[:, e0 + kb * 128:e0 + kb * 128 + 128], vav[:, kb, h, :],
                    start=(kb == 0), stop=(kb == qb)),
                    waits=wts, inc=((smm2.h, 1) if kb == qb else None))
            rel_e[it % 3] = [(smm2.h, smm2.bump())]

        def plan_fin(it):
            h, qb = iters[it]
            pv = pvs[it % 2]
            p.op("vector", lambda e, pv=pv: e.reciprocal(st(ST_RZ), pv[:, 64:65]),
                 waits=[(smm2.h, smm2.n)], inc=(svec2.h, 1))
            rzn = svec2.bump()
            p.op("vector", lambda e, pv=pv, qb=qb, h=h: e.tensor_scalar(
                hctm[:, qb * 1024 + h * 64:qb * 1024 + h * 64 + 64],
                pv[:, 0:64], st(ST_RZ), None, ALU.mult),
                waits=[(svec2.h, rzn)], inc=(svec2.h, 1))
            rel_pv[it % 2] = [(svec2.h, svec2.bump())]

        plan_scores(0)
        for it in range(NIT):
            plan_exps(it)
            plan_mask(it)
            if it + 1 < NIT:
                plan_scores(it + 1)
            plan_pv(it)
            plan_fin(it)
        p.emit()
        if seq == 0 and DEBUG_DUMP == "hctm":
            emit_dump(hctm, 16 * 1024)
        mem.free(f"Qb_{seq}", f"Kb_{seq}", f"Va_{seq}", "a3_e",
                 "a3_sc0", "a3_sc1", "a3_pv0", "a3_pv1")

        hcf = mem.sbuf(f"hcf_{seq}", [128, 8 * S], BF16, side="right")
        psT = [mem.psum("a4_ps0", [128, 512], BF16), mem.psum("a4_ps1", [128, 512], BF16)]
        p = Plan(nc, DMAS)
        rel_psT = [[], []]
        for tb in range(NTB):
            for g in range(2):
                for c4 in range(4):
                    c = g * 4 + c4
                    p.op("tensor", lambda e, tb=tb, c=c, c4=c4, g=g: e.transpose(
                        psT[g][:, c4 * 128:c4 * 128 + 128],
                        hctm[:, tb * 1024 + c * 128:tb * 1024 + c * 128 + 128],
                        identb[:]),
                        waits=(rel_psT[g] if c4 == 0 else []),
                        inc=((smm.h, 1) if c4 == 3 else None))
                mn = smm.bump()
                dv = hcf[:].rearrange("p (c t) -> p c t", c=8)
                p.op("vector", lambda e, g=g, tb=tb, dv=dv: e.tensor_copy(
                    dv[:, g * 4:(g + 1) * 4, tb * 128:tb * 128 + 128], psT[g][:]),
                    waits=[(smm.h, mn)], inc=(svec.h, 1))
                rel_psT[g] = [(svec.h, svec.bump())]
        p.emit()
        if seq == 0 and DEBUG_DUMP == "hcf":
            emit_dump(hcf, 8 * S)
        mem.free(f"hctm_{seq}", "a4_ps0", "a4_ps1")

        ws = mem.sbuf("a5_ws", [128, 8 * 1024], BF16)
        aof = mem.sbuf("a5_ao", [128, 16 * 1024], F32)
        xt2 = mem.sbuf("a5_xt", [128, 2 * D], F32)
        x1t2 = mem.sbuf("a5_x1", [128, 2 * D], F32)
        xadj2 = mem.sbuf("a5_xa", [128, 2 * D], F32)
        xtm2 = mem.sbuf("a5_tm", [128, 2 * D], BF16)
        xstg2 = mem.sbuf("a5_st", [128, 2 * D], BF16)
        pOs = [mem.psum("a5_p0", [128, 512], F32), mem.psum("a5_p1", [128, 512], F32),
               mem.psum("a5_p2", [128, 512], F32), mem.psum("a5_p3", [128, 512], F32)]
        psT = [mem.psum("a5_t0", [128, 512], BF16), mem.psum("a5_t1", [128, 512], BF16)]
        p = Plan(nc, DMAS)
        for kc in range(8):
            p.op("gpsimd", lambda e, k=kc: e.dma_start(
                out=ws[:, k * 1024:(k + 1) * 1024],
                in_=din["woT"][k * 128:(k + 1) * 128, :]),
                waits=(rel_ws if kc == 0 else []), inc=(dG.h, 16))
            dG.bump(16)
        dGn = dG.n
        rel_pO = [[], [], [], []]
        pi = 0
        for tb in range(NTB):
            aos = aof[:, tb * 1024:(tb + 1) * 1024]
            for ot in range(2):
                ps = pOs[pi % 4]
                w0 = [(dG.h, dGn)] + rel_pO[pi % 4]
                for dc in range(8):
                    p.op("tensor", lambda e, d=dc, t=tb, o=ot, ps=ps: e.matmul(
                        ps[:], hcf[:, d * S + t * 128:d * S + t * 128 + 128],
                        ws[:, d * 1024 + o * 512:d * 1024 + o * 512 + 512],
                        start=(d == 0), stop=(d == 7)),
                        waits=(w0 if dc == 0 else []),
                        inc=((smm.h, 1) if dc == 7 else None))
                mn = smm.bump()
                p.op("scalar", lambda e, o=ot, d=aos, ps=ps: e.activation(
                    d[:, o * 512:o * 512 + 512], ps[:], AF.Identity),
                    waits=[(smm.h, mn)], inc=(sact.h, 1))
                rel_pO[pi % 4] = [(sact.h, sact.bump())]
                pi += 1
            evn = sact.n
            p.op("vector", lambda e, d=aos, t=tb: e.reduce_sum(bc(B_SUM, t), d, axis=AX.X),
                 waits=[(sact.h, evn)], inc=(svec.h, 1))
            svec.bump()
            p.op("scalar", lambda e, d=aos, t=tb: e.activation(
                scrap[:, (t % 2) * D:(t % 2) * D + D], d, AF.Square,
                accum_out=bc(B_SQ, t)),
                waits=[(sact.h, evn)], inc=(sact.h, 1))
            sact.bump()
        ln_batch(p, [(svec.h, svec.n), (sact.h, sact.n)], with_amr=True)
        rn = svec2.n
        rel_xt = [[], []]
        rel_x1t = [[], []]
        for tb in range(NTB):
            sl = tb % 2
            xt = xt2[:, sl * D:sl * D + D]
            x1t = x1t2[:, sl * D:sl * D + D]
            xadj = xadj2[:, sl * D:sl * D + D]
            aos = aof[:, tb * 1024:(tb + 1) * 1024]
            p.op("sync", lambda e, t=tb, d=xt: e.dma_start(out=d, in_=xr[tb0 + t]),
                 waits=rel_xt[sl], inc=(dL[sl].h, 16))
            dsx = dL[sl].bump(16)
            p.op("vector", lambda e, d=xt, a=xadj, t=tb: e.tensor_scalar(
                a, d, bc(B_AMR, t), None, ALU.add),
                waits=[(dL[sl].h, dsx), (svec2.h, rn)], inc=(svec.h, 1))
            xan = svec.bump()
            rel_xt[sl] = [(svec.h, xan)]
            p.op("vector", lambda e, d=x1t, a=xadj, s=aos, t=tb: e.scalar_tensor_tensor(
                d, s, bc(B_R, t), a, ALU.mult, ALU.add),
                waits=[(svec.h, xan)] + rel_x1t[sl], inc=(svec.h, 1))
            vstt = svec.bump()
            p.op("vector", lambda e, d=x1t, t=tb: e.reduce_sum(bc(B_SUM, t), d, axis=AX.X),
                 waits=[(svec.h, vstt)], inc=(svec.h, 1))
            vr1 = svec.bump()
            p.op("scalar", lambda e, d=x1t, t=tb: e.activation(
                scrap[:, (t % 2) * D:(t % 2) * D + D], d, AF.Square,
                accum_out=bc(B_SQ, t)),
                waits=[(svec.h, vstt)], inc=(sact.h, 1))
            an1 = sact.bump()
            p.op("gpsimd", lambda e, t=tb, d=x1t: e.dma_start(out=x1r[tb0 + t], in_=d),
                 waits=[(svec.h, vstt)], inc=(dSt[sl].h, 16))
            dSt[sl].bump(16)
            rel_x1t[sl] = [(dSt[sl].h, dSt[sl].n), (vr1 and svec.h, vr1), (sact.h, an1)]
        ln_batch(p, [(svec.h, svec.n), (sact.h, sact.n)])
        rn1 = svec2.n
        stc = [(dSt[0].h, dSt[0].n), (dSt[1].h, dSt[1].n)]
        rel_x1t2 = [[], []]
        rel_psT = [[], []]
        rel_stg = [[], []]
        rel_xtm = [[], []]
        for tb in range(NTB):
            sl = tb % 2
            x1t = x1t2[:, sl * D:sl * D + D]
            xtm = xtm2[:, sl * D:sl * D + D]
            xstg = xstg2[:, sl * D:sl * D + D]
            p.op("sync", lambda e, t=tb, d=x1t: e.dma_start(out=d, in_=x1r[tb0 + t]),
                 waits=rel_x1t2[sl] + (rel_x1t[sl] + stc if tb < 2 else []),
                 inc=(dL[sl].h, 16))
            dsx = dL[sl].bump(16)
            p.op("vector", lambda e, d=x1t, m=xtm, t=tb: e.tensor_scalar(
                m, d, bc(B_M, t), bc(B_R, t), ALU.subtract, ALU.mult),
                waits=[(dL[sl].h, dsx), (svec2.h, rn1)] + rel_xtm[sl], inc=(svec.h, 1))
            xn = svec.bump()
            rel_x1t2[sl] = [(svec.h, xn)]
            last_mn = 0
            for g in range(2):
                for c4 in range(4):
                    c = g * 4 + c4
                    w = ([(svec.h, xn)] if c4 == 0 else []) + (rel_psT[g] if c4 == 0 else [])
                    p.op("tensor", lambda e, c=c, c4=c4, m=xtm, g=g: e.transpose(
                        psT[g][:, c4 * 128:c4 * 128 + 128],
                        m[:, c * 128:c * 128 + 128], identb[:]),
                        waits=w, inc=((smm.h, 1) if c4 == 3 else None))
                last_mn = smm.bump()
                p.op("vector", lambda e, g=g, d=xstg: e.tensor_copy(
                    d[:, g * 512:g * 512 + 512], psT[g][:]),
                    waits=[(smm.h, last_mn)] + (rel_stg[sl] if g == 0 else []),
                    inc=(svec2.h, 1))
                rel_psT[g] = [(svec2.h, svec2.bump())]
            rel_xtm[sl] = [(smm.h, last_mn)]
            col0 = seq * S + tb * 128
            p.op("gpsimd", lambda e, d=xstg, c0=col0: e.dma_start(
                out=xln1d[:, :, c0:c0 + 128].transpose([1, 0, 2]),
                in_=d.rearrange("p (c f) -> p c f", c=8)),
                waits=[(svec2.h, svec2.n)], inc=(dSt[sl].h, 16))
            dSt[sl].bump(16)
            rel_stg[sl] = [(dSt[sl].h, dSt[sl].n)]
        p.emit()
        rel_ws = [(smm.h, smm.n)]
        mem.free(f"hcf_{seq}", "a5_ws", "a5_ao", "a5_xt", "a5_x1", "a5_xa", "a5_tm",
                 "a5_st", "a5_p0", "a5_p1", "a5_p2", "a5_p3", "a5_t0", "a5_t1")

    mem.free("cosf", "sinf", "cosb")

    w1s = mem.sbuf("w1s", [128, 8 * DH], BF16)
    w2s = mem.sbuf("w2s", [128, 8 * DH], BF16)
    xt1 = mem.sbuf("f1_x", [128, 2 * 8 * 512], BF16)
    t1f = mem.sbuf("f1_t1", [128, 2 * 512], F32)
    hstg = mem.sbuf("f1_h", [128, 2 * 512], BF16)
    pa = [mem.psum("f1_pa0", [128, 512], F32), mem.psum("f1_pa1", [128, 512], F32)]
    pb = [mem.psum("f1_pb0", [128, 512], F32), mem.psum("f1_pb1", [128, 512], F32)]
    p = Plan(nc, DMAS)
    grp_sem = {0: dG, 1: dG2, 2: dG3, 3: dSt[0]}
    wgate = {}
    for g in range(4):
        nm = "w1T" if g < 2 else "w2T"
        dstb = w1s if g < 2 else w2s
        sem = grp_sem[g]
        for j in range(4):
            kc = (g % 2) * 4 + j
            p.op("gpsimd", lambda e, k=kc, dstb=dstb, nm=nm: e.dma_start(
                out=dstb[:, k * DH:(k + 1) * DH], in_=din[nm][k * 128:(k + 1) * 128, :]),
                waits=(rel_ws if g == 0 and j == 0 else []), inc=(sem.h, 16))
            sem.bump(16)
        wgate[("w1" if g < 2 else "w2", (g % 2) * 4)] = (sem.h, sem.n)
    rel_x = [[], []]
    rel_pa = [[], []]
    rel_pb = [[], []]
    rel_t1 = [[], []]
    rel_h = [[], []]
    ti = 0
    for tt in range(8):
        sl = tt % 2
        xv = xt1[:, sl * 4096:sl * 4096 + 4096]
        p.op("sync", lambda e, t=tt, d=xv: e.dma_start(
            out=d.rearrange("p (k f) -> p k f", k=8),
            in_=xln1d[:, :, t * 512:t * 512 + 512].transpose([1, 0, 2])),
            waits=rel_x[sl], inc=(dL[sl].h, 16))
        dsn = dL[sl].bump(16)
        for oc in range(32):
            psl = ti % 2
            w0 = [(dL[sl].h, dsn)] + rel_pa[psl]
            for kc in range(8):
                wts = (w0 if kc == 0 else []) + (
                    [wgate[("w1", kc)]] if ti == 0 and kc in (0, 4) else [])
                p.op("tensor", lambda e, k=kc, o=oc, d=xv, ps=pa[psl]: e.matmul(
                    ps[:], w1s[:, k * DH + o * 128:k * DH + o * 128 + 128],
                    d[:, k * 512:k * 512 + 512], start=(k == 0), stop=(k == 7)),
                    waits=wts, inc=((smm.h, 1) if kc == 7 else None))
            mn1 = smm.bump()
            w0 = rel_pb[psl]
            for kc in range(8):
                wts = (w0 if kc == 0 else []) + (
                    [wgate[("w2", kc)]] if ti == 0 and kc in (0, 4) else [])
                p.op("tensor", lambda e, k=kc, o=oc, d=xv, ps=pb[psl]: e.matmul(
                    ps[:], w2s[:, k * DH + o * 128:k * DH + o * 128 + 128],
                    d[:, k * 512:k * 512 + 512], start=(k == 0), stop=(k == 7)),
                    waits=wts, inc=((smm.h, 1) if kc == 7 else None))
            mn2 = smm.bump()
            p.op("scalar", lambda e, o=oc, ps=pa[psl], d=t1f[:, psl * 512:psl * 512 + 512]:
                 e.activation(d, ps[:], AF.Silu, scale=s1s[:, o:o + 1]),
                 waits=[(smm.h, mn1)] + rel_t1[psl], inc=(sact.h, 1))
            an = sact.bump()
            rel_pa[psl] = [(sact.h, an)]
            p.op("vector", lambda e, o=oc, ps=pb[psl], t1=t1f[:, psl * 512:psl * 512 + 512],
                 d=hstg[:, psl * 512:psl * 512 + 512]:
                 e.scalar_tensor_tensor(d, ps[:], s2s[:, o:o + 1], t1, ALU.mult, ALU.mult),
                 waits=[(smm.h, mn2), (sact.h, an)] + rel_h[psl], inc=(svec.h, 1))
            vn = svec.bump()
            rel_pb[psl] = [(svec.h, vn)]
            rel_t1[psl] = [(svec.h, vn)]
            p.op("gpsimd", lambda e, t=tt, o=oc, d=hstg[:, psl * 512:psl * 512 + 512]:
                 e.dma_start(out=hd[t, o], in_=d), waits=[(svec.h, vn)], inc=(dSt[psl].h, 16))
            dSt[psl].bump(16)
            rel_h[psl] = [(dSt[psl].h, dSt[psl].n)]
            ti += 1
        rel_x[sl] = [(smm.h, smm.n)]
    p.emit()
    mem.free("w1s", "w2s", "f1_x", "f1_t1", "f1_h", "f1_pa0", "f1_pa1", "f1_pb0", "f1_pb1")

    for phase, (wnm, src_d, dst_d, hss, dst_kind) in enumerate(
            [("h1T", hd, g1d, hs1s, "tt"), ("h2T", g1d, g2d, hs2s, "tb")]):
        wg = mem.sbuf("f2_wg", [128, 48 * 1024], BF16)
        ht = mem.sbuf("f2_ht", [128, 2 * 32 * 512], BF16)
        gstg = mem.sbuf("f2_g", [128, 16 * 512], BF16)
        ps2 = [mem.psum("f2_p0", [128, 512], F32), mem.psum("f2_p1", [128, 512], F32)]
        p = Plan(nc, DMAS)
        half_sem = {}
        rel_half = {}
        grp_end = {}

        def load_half(half):
            slot = half % 3
            sem = [dG, dG2, dG3][slot]
            g, off = half // 2, (half % 2) * 16
            w = rel_half.get(slot, [])
            for j in range(16):
                kc = off + j
                p.op("gpsimd", lambda e, k=kc, g=g, j=j, slot=slot, nm=wnm: e.dma_start(
                    out=wg[:, (slot * 16 + j) * 1024:(slot * 16 + j) * 1024 + 1024],
                    in_=din[nm][k * 128:(k + 1) * 128, g * 1024:(g + 1) * 1024]),
                    waits=(w if j == 0 else []), inc=(sem.h, 16))
                sem.bump(16)
            half_sem[half] = (sem.h, sem.n)

        def wslice(g, kc):
            half = 2 * g + (1 if kc >= 16 else 0)
            slot = half % 3
            j = kc % 16
            c0 = (slot * 16 + j) * 1024
            return c0

        load_half(0)
        load_half(1)
        rel_ht = [[], []]
        rel_ps = [[], []]
        rel_g = [[], []]
        ht_tile = [-1, -1]
        ht_dsn = [0, 0]
        ti = 0
        for ocg in range(4):
            if ocg >= 1:
                rel_half[(2 * ocg + 1) % 3] = [(smm.h, grp_end[ocg - 1])]
                load_half(2 * ocg + 1)
            if ocg + 1 < 4:
                rel_half[(2 * ocg + 2) % 3] = ([(smm.h, grp_end[ocg - 1])]
                                               if ocg >= 1 else [])
                load_half(2 * ocg + 2)
            tts = list(range(8)) if ocg % 2 == 0 else list(range(7, -1, -1))
            for tt in tts:
                sl = tt % 2
                hv = ht[:, sl * 16384:sl * 16384 + 16384]
                if ht_tile[sl] != tt:
                    for q in range(4):
                        p.op("sync", lambda e, t=tt, q=q, d=hv, sd=src_d: e.dma_start(
                            out=d.rearrange("p (k f) -> p k f", k=32)[:, q * 8:(q + 1) * 8, :],
                            in_=sd[t, q * 8:(q + 1) * 8].transpose([1, 0, 2])),
                            waits=(rel_ht[sl] if q == 0 else []), inc=(dL[sl].h, 16))
                        dL[sl].bump(16)
                    ht_tile[sl] = tt
                    ht_dsn[sl] = dL[sl].n
                dsn = ht_dsn[sl]
                an = 0
                for oc in range(8):
                    psl = ti % 2
                    w0 = [half_sem[2 * ocg], (dL[sl].h, dsn)] + rel_ps[psl]
                    for kc in range(32):
                        wts = w0 if kc == 0 else ([half_sem[2 * ocg + 1]] if kc == 16 else [])
                        p.op("tensor", lambda e, k=kc, o=oc, d=hv, ps=ps2[psl],
                             c0=wslice(ocg, kc): e.matmul(
                                 ps[:], wg[:, c0 + o * 128:c0 + o * 128 + 128],
                                 d[:, k * 512:k * 512 + 512],
                                 start=(k == 0), stop=(k == 31)),
                             waits=wts, inc=((smm.h, 1) if kc == 31 else None))
                    mn = smm.bump()
                    gsl = gstg[:, (sl * 8 + oc) * 512:(sl * 8 + oc) * 512 + 512]
                    p.op("scalar", lambda e, o=oc, g=ocg, ps=ps2[psl], d=gsl, hss=hss:
                         e.activation(d, ps[:], AF.Gelu, scale=hss[:, g * 8 + o:g * 8 + o + 1]),
                         waits=[(smm.h, mn)] + (rel_g[sl] if oc == 0 else []),
                         inc=(sact.h, 1))
                    an = sact.bump()
                    rel_ps[psl] = [(sact.h, an)]
                    ti += 1
                for oc in range(8):
                    gsl = gstg[:, (sl * 8 + oc) * 512:(sl * 8 + oc) * 512 + 512]
                    if dst_kind == "tt":
                        p.op("gpsimd", lambda e, t=tt, o=ocg * 8 + oc, d=gsl, dd=dst_d:
                             e.dma_start(out=dd[t, o], in_=d),
                             waits=([(sact.h, an)] if oc == 0 else []), inc=(dSt[sl].h, 16))
                    else:
                        p.op("gpsimd", lambda e, t=tt, o=ocg * 8 + oc, d=gsl, dd=dst_d:
                             e.dma_start(out=dd[t * 4:(t + 1) * 4, o].transpose([1, 0, 2]),
                                         in_=d.rearrange("p (b f) -> p b f", b=4)),
                             waits=([(sact.h, an)] if oc == 0 else []), inc=(dSt[sl].h, 16))
                    dSt[sl].bump(16)
                rel_g[sl] = [(dSt[sl].h, dSt[sl].n)]
                rel_ht[sl] = [(smm.h, smm.n)]
            grp_end[ocg] = smm.n
        p.emit()
        mem.free("f2_wg", "f2_ht", "f2_g", "f2_p0", "f2_p1")

    w3s = mem.sbuf("w3s", [128, 32 * 1024], BF16)
    w3raw = mem.sbuf("f4_wr", [128, 2 * 1024], BF16)
    s3f = mem.sbuf("f4_s3f", [1, 1024], F32)
    s3b = mem.sbuf("f4_s3b", [1, 1024], BF16)
    ones1 = mem.sbuf("f4_ones", [1, 128], BF16)
    s3bc = mem.sbuf("f4_s3bc", [128, 1024], BF16)
    g2t = mem.sbuf("f4_g2", [128, 2 * 4096], BF16)
    x1t2 = mem.sbuf("f4_x1", [128, 2 * D], F32)
    xadj2 = mem.sbuf("f4_xa", [128, 2 * D], F32)
    outt2 = mem.sbuf("f4_o", [128, 2 * D], F32)
    pO = [mem.psum("f4_p0", [128, 512], F32), mem.psum("f4_p1", [128, 512], F32)]
    p = Plan(nc, DMAS)
    p.op("sync", lambda e: e.dma_start(out=s3f[:], in_=din["s3row"]), inc=(dL[0].h, 16))
    dsn = dL[0].bump(16)
    p.op("vector", lambda e: e.tensor_copy(s3b[:], s3f[:]), waits=[(dL[0].h, dsn)])
    p.op("vector", lambda e: e.memset(ones1[:], 1.0), inc=(svec.h, 1))
    vn0 = svec.bump()
    for o in range(2):
        p.op("tensor", lambda e, o=o: e.matmul(pO[o][:], ones1[:],
                                               s3b[0:1, o * 512:o * 512 + 512],
                                               start=True, stop=True),
             waits=[(svec.h, vn0)], inc=(smm.h, 1))
        mn = smm.bump()
        p.op("vector", lambda e, o=o: e.tensor_copy(s3bc[:, o * 512:o * 512 + 512], pO[o][:]),
             waits=[(smm.h, mn)], inc=(svec.h, 1))
        svec.bump()
    vbc = svec.n
    rel_wr = [[], []]
    wsem = [dG, dG2, dG3, dG]
    vw3b = []
    for b in range(4):
        sem = wsem[b]
        for j in range(8):
            c = b * 8 + j
            sl = c % 2
            p.op("gpsimd", lambda e, c=c, d=w3raw[:, (c % 2) * 1024:(c % 2) * 1024 + 1024]:
                 e.dma_start(out=d, in_=din["w3T"][c * 128:(c + 1) * 128, :]),
                 waits=rel_wr[sl], inc=(sem.h, 16))
            dgn = sem.bump(16)
            p.op("vector", lambda e, c=c, s=w3raw[:, (c % 2) * 1024:(c % 2) * 1024 + 1024]:
                 e.tensor_tensor(w3s[:, c * 1024:(c + 1) * 1024], s, s3bc[:], ALU.mult),
                 waits=[(sem.h, dgn), (svec.h, vbc)], inc=(svec.h, 1))
            rel_wr[sl] = [(svec.h, svec.bump())]
        vw3b.append(svec.n)
    vw3 = svec.n
    ff = mem.sbuf("f4_ff", [128, 16 * 1024], F32)
    rel_g2 = [[], []]
    rel_x1 = [[], []]
    rel_pO = [[], [], [], []]
    rel_out = [[], []]
    pOs = [pO[0], pO[1], mem.psum("f4_p2", [128, 512], F32),
           mem.psum("f4_p3", [128, 512], F32)]
    pi = 0
    rel_ff = [[], []]
    for half in range(4):
        hb = half * 8
        fo = (half % 2) * 8
        for tbl in range(8):
            tb = hb + tbl
            sl = tb % 2
            gv = g2t[:, sl * 4096:sl * 4096 + 4096]
            ffs = ff[:, (fo + tbl) * 1024:(fo + tbl) * 1024 + 1024]
            for q in range(4):
                p.op("sync", lambda e, t=tb, q=q, d=gv: e.dma_start(
                    out=d.rearrange("p (k f) -> p k f", k=32)[:, q * 8:(q + 1) * 8, :],
                    in_=g2d[t, q * 8:(q + 1) * 8].transpose([1, 0, 2])),
                    waits=(rel_g2[sl] if q == 0 else []), inc=(dL[sl].h, 16))
                dL[sl].bump(16)
            dsn = dL[sl].n
            for ot in range(2):
                ps = pOs[pi % 4]
                w0 = [(dL[sl].h, dsn), (svec.h, vw3b[0])] + rel_pO[pi % 4]
                for dc in range(32):
                    wts = (w0 if dc == 0 else
                           ([(svec.h, vw3b[dc // 8])] if dc % 8 == 0 else []))
                    p.op("tensor", lambda e, d=dc, o=ot, g=gv, ps=ps: e.matmul(
                        ps[:], g[:, d * 128:d * 128 + 128],
                        w3s[:, d * 1024 + o * 512:d * 1024 + o * 512 + 512],
                        start=(d == 0), stop=(d == 31)),
                        waits=wts, inc=((smm.h, 1) if dc == 31 else None))
                mn = smm.bump()
                p.op("scalar", lambda e, o=ot, d=ffs, ps=ps: e.activation(
                    d[:, o * 512:o * 512 + 512], ps[:], AF.Identity),
                    waits=[(smm.h, mn)] + (rel_ff[half % 2] if tbl == 0 and ot == 0 else []),
                    inc=(sact.h, 1))
                rel_pO[pi % 4] = [(sact.h, sact.bump())]
                pi += 1
            rel_g2[sl] = [(smm.h, smm.n)]
            evn = sact.n
            p.op("vector", lambda e, d=ffs, t=fo + tbl: e.reduce_sum(
                bc(B_SUM, t), d, axis=AX.X),
                waits=[(sact.h, evn)], inc=(svec.h, 1))
            svec.bump()
            p.op("scalar", lambda e, d=ffs, t=fo + tbl: e.activation(
                scrap[:, (t % 2) * D:(t % 2) * D + D], d, AF.Square,
                accum_out=bc(B_SQ, t)),
                waits=[(sact.h, evn)], inc=(sact.h, 1))
            sact.bump()
        ln_batch(p, [(svec.h, svec.n), (sact.h, sact.n)], with_amr=True)
        rn = svec2.n
        for tbl in range(8):
            tb = hb + tbl
            sl = tb % 2
            x1t = x1t2[:, sl * D:sl * D + D]
            outt = outt2[:, sl * D:sl * D + D]
            xadj = xadj2[:, sl * D:sl * D + D]
            ffs = ff[:, (fo + tbl) * 1024:(fo + tbl) * 1024 + 1024]
            p.op("gpsimd", lambda e, t=tb, d=x1t: e.dma_start(out=d, in_=x1r[t]),
                 waits=rel_x1[sl], inc=(dG2.h, 16))
            dsx = dG2.bump(16)
            p.op("vector", lambda e, d=x1t, a=xadj, t=fo + tbl: e.tensor_scalar(
                a, d, bc(B_AMR, t), None, ALU.add),
                waits=[(dG2.h, dsx), (svec2.h, rn)], inc=(svec.h, 1))
            xan = svec.bump()
            rel_x1[sl] = [(svec.h, xan)]
            p.op("vector", lambda e, d=outt, a=xadj, s=ffs, t=fo + tbl:
                 e.scalar_tensor_tensor(d, s, bc(B_R, t), a, ALU.mult, ALU.add),
                 waits=[(svec.h, xan)] + rel_out[sl], inc=(svec.h, 1))
            vstt = svec.bump()
            p.op("gpsimd", lambda e, t=tb, d=outt: e.dma_start(out=outr[t], in_=d),
                 waits=[(svec.h, vstt)], inc=(dSt[sl].h, 16))
            dSt[sl].bump(16)
            rel_out[sl] = [(dSt[sl].h, dSt[sl].n), (svec.h, vstt)]
        rel_ff[half % 2] = [(svec.h, svec.n)]
    p.op("gpsimd", lambda e: e.nop(), waits=[(dSt[0].h, dSt[0].n), (dSt[1].h, dSt[1].n)])
    p.emit()
    mem.free("w3s", "f4_wr", "f4_s3f", "f4_s3b", "f4_ones", "f4_s3bc", "f4_g2",
             "f4_ff", "f4_x1", "f4_xa", "f4_o", "f4_p0", "f4_p1", "f4_p2", "f4_p3")

    for g in sem_guards:
        g.__exit__(None, None, None)
    mem.free(*list(mem.guards))
    return nc


_NC_CACHE = None


def kernel(**inputs):
    global _NC_CACHE
    f32c = lambda a: np.ascontiguousarray(np.asarray(a, dtype=np.float32))
    x = f32c(inputs["L0_attn_x"])
    wqT = f32c(np.asarray(inputs["L0_attn_wq"]).T)
    wkT = f32c(np.asarray(inputs["L0_attn_wk"]).T)
    wvT = f32c(np.asarray(inputs["L0_attn_wv"]).T)
    woT = f32c(np.asarray(inputs["L0_attn_wo"]).T)
    w1T = f32c(np.asarray(inputs["L0_ffn_w1_w"]).T)
    w2T = f32c(np.asarray(inputs["L0_ffn_w2_w"]).T)
    h1T = f32c(np.asarray(inputs["L0_ffn_h1_w"]).T)
    h2T = f32c(np.asarray(inputs["L0_ffn_h2_w"]).T)
    w3T = f32c(np.asarray(inputs["L0_ffn_w3_w"]).T)
    s1r = f32c(np.asarray(inputs["L0_ffn_w1_s"]).reshape(32, 128).T)
    s2r = f32c(np.asarray(inputs["L0_ffn_w2_s"]).reshape(32, 128).T)
    hs1r = f32c(np.asarray(inputs["L0_ffn_h1_s"]).reshape(32, 128).T)
    hs2r = f32c(np.asarray(inputs["L0_ffn_h2_s"]).reshape(32, 128).T)
    s3row = f32c(np.asarray(inputs["L0_ffn_w3_s"]).reshape(1, 1024))

    j = (np.arange(128) % 64) // 2
    inv = np.power(10000.0, -2.0 * j.astype(np.float64) / 64.0)
    ang = inv[:, None] * np.arange(S, dtype=np.float64)[None, :]
    cosT = np.cos(ang).astype(np.float32)
    sinT = np.sin(ang).astype(np.float32)
    pswap = np.zeros((128, 128), np.float32)
    idx = np.arange(0, 128, 2)
    pswap[idx + 1, idx] = -1.0
    pswap[idx, idx + 1] = 1.0
    identc = np.eye(128, dtype=np.float32)
    maskc = (np.arange(128)[:, None] <= np.arange(128)[None, :]).astype(np.float32)

    if _NC_CACHE is None:
        _NC_CACHE = build_program()
    nc = _NC_CACHE

    shared = dict(wqT=wqT, wkT=wkT, wvT=wvT, woT=woT, w1T=w1T, w2T=w2T,
                  h1T=h1T, h2T=h2T, w3T=w3T, s1r=s1r, s2r=s2r, hs1r=hs1r,
                  hs2r=hs2r, s3row=s3row, cosT=cosT, sinT=sinT, pswap=pswap,
                  identc=identc, maskc=maskc)
    in_maps = []
    for c in range(8):
        m = dict(shared)
        m["x"] = np.ascontiguousarray(x[2 * c:2 * c + 2].reshape(T, D))
        in_maps.append(m)
    res = run_bass_kernel_spmd(nc, in_maps, core_ids=list(range(8)))
    out = np.concatenate(
        [np.asarray(res.results[c]["out"], np.float32).reshape(2, S, D) for c in range(8)],
        axis=0)
    return out
